# version 11
# baseline (speedup 1.0000x reference)
import numpy as np
import ml_dtypes
import concourse.bass as bass
import concourse.mybir as mybir
from concourse.bass_utils import run_bass_kernel_spmd

F32 = mybir.dt.float32
BF16 = mybir.dt.bfloat16
AF = mybir.ActivationFunctionType
ALU = mybir.AluOpType
AX = mybir.AxisListType.X

BFD = ml_dtypes.bfloat16
NEG = -1e30
MAGIC = 12582912.0
EPS_LN = 1e-5
EPS_BIT = 1e-8

B, T, D = 2, 4096, 2048
NCORE, GRP = 8, 4
CH = 512
TOK = 1024
HEADS, HD = 16, 128
LH = 4
DFF = 8192
SCL = float(HD) ** -0.5
RG_BATCH = [[0, 1, 2, 3], [4, 5, 6, 7]]
RG_ALL = [list(range(8))]

SHUF_PAIR = [1, 0, 3, 2, 5, 4, 7, 6, 9, 8, 11, 10, 13, 12, 15, 14,
             17, 16, 19, 18, 21, 20, 23, 22, 25, 24, 27, 26, 29, 28, 31, 30]


def owner(g):
    return g if g < 4 else 7 - g


def lidx(g):
    return 0 if g < 4 else 1


def _f32(a):
    return np.ascontiguousarray(np.asarray(a), dtype=np.float32)


def _bf(a):
    return np.ascontiguousarray(np.asarray(a, dtype=np.float32).astype(BFD))


def _consts():
    c = {}
    c["ident_bf"] = _bf(np.eye(128))
    c["ident_f32"] = _f32(np.eye(128))
    m = np.where(np.arange(CH)[:, None] <= np.arange(CH)[None, :], 0.0, NEG)
    c["maskT_bf"] = _bf(np.concatenate(
        [m[i * 128:(i + 1) * 128] for i in range(4)], axis=1))
    c["ones_f"] = _f32(np.ones((1, 128)))
    c["onesc_f"] = _f32(np.ones((128, 1)))
    c["onesc_b"] = _bf(np.ones((128, 1)))
    c["iota8"] = _f32(np.broadcast_to(np.arange(8, dtype=np.float32), (128, 8)))
    c["m8"] = _f32(np.broadcast_to(np.arange(8, dtype=np.float32) - 8.0, (128, 8)))
    c["eye64x8"] = _f32((np.arange(64)[:, None] // 8 == np.arange(8)[None, :]))
    c["eye8"] = _f32(np.eye(8))
    inv = 1.0 / (10000.0 ** (np.arange(0, HD, 2, dtype=np.float64) / HD))
    fr = np.arange(T, dtype=np.float64)[None, :] * inv[:, None]
    cfm = np.empty((HD, T))
    sfm = np.empty((HD, T))
    cfm[0::2] = np.cos(fr)
    cfm[1::2] = np.cos(fr)
    sfm[0::2] = -np.sin(fr)
    sfm[1::2] = np.sin(fr)
    c["cos_fm"] = _bf(cfm)
    c["sin_fm"] = _bf(sfm)
    return c


def _prep_core(inp, c):
    b, r = c // 4, c % 4
    d = {}
    x = _f32(inp["L0_attn_x"])
    d["x_in"] = _f32(np.concatenate(
        [x[b, r * CH:(r + 1) * CH], x[b, (7 - r) * CH:(8 - r) * CH]], axis=0))

    wq0 = np.transpose(_f32(inp["L0_attn_Wq"]), (1, 0, 2)).reshape(D, D)
    wk0 = np.transpose(_f32(inp["L0_attn_Wk"]), (1, 0, 2)).reshape(D, D)
    wv0 = np.transpose(_f32(inp["L0_attn_Wv"]), (1, 0, 2)).reshape(D, D)
    hsl = slice(4 * r * HD, (4 * r + 4) * HD)
    d["wq_0"] = _f32(wq0[:, hsl])
    d["wk_0"] = _f32(wk0[:, hsl])
    d["wv_0"] = _f32(wv0[:, hsl])
    d["bq_0"] = _f32(_f32(inp["L0_attn_bq"]).reshape(D)[hsl].reshape(4, 128).T)
    d["bk_0"] = _f32(_f32(inp["L0_attn_bk"]).reshape(D)[hsl].reshape(4, 128).T)
    d["bv_0"] = _f32(_f32(inp["L0_attn_bv"]).reshape(D)[hsl].reshape(1, 512))
    d["wo_0"] = _f32(_f32(inp["L0_attn_Wo"])[hsl, :])
    d["bo_0"] = _f32(inp["L0_attn_bo"]).reshape(1, D)

    wq1 = _f32(inp["L1_attn_Wq"]).T
    wk1 = _f32(inp["L1_attn_Wk"]).T
    wv1 = _f32(inp["L1_attn_Wv"]).T
    kvsl = slice(r * HD, (r + 1) * HD)
    d["wq_1"] = _f32(wq1[:, hsl])
    d["wk_1"] = _f32(wk1[:, kvsl])
    d["wv_1"] = _f32(wv1[:, kvsl])
    d["bq_1"] = _f32(_f32(inp["L1_attn_bq"])[hsl].reshape(4, 128).T)
    d["bk_1"] = _f32(_f32(inp["L1_attn_bk"])[kvsl].reshape(1, 128).T)
    d["bv_1"] = _f32(_f32(inp["L1_attn_bv"])[kvsl].reshape(1, 128))
    d["wo_1"] = _f32(_f32(inp["L1_attn_Wo"]).T[hsl, :])
    d["bo_1"] = _f32(inp["L1_attn_bo"]).reshape(1, D)

    for l in range(2):
        wm = _f32(inp[f"L{l}_wmod"])
        sl = np.concatenate([wm[:, m * D + r * 512:m * D + (r + 1) * 512]
                             for m in range(6)], axis=1)
        d[f"wmod_{l}"] = _f32(sl)
        d[f"lc_{l}"] = _f32(inp[f"L{l}_c"])[b:b + 1]

    sw = _f32(inp["L0_ffn_select_weight"])
    md = _f32(inp["L0_ffn_select_mean_diff"])
    lu = _f32(inp["L0_ffn_lora_update"])
    lg = _f32(inp["L0_ffn_lora_degen"])
    sw2 = np.transpose(sw, (1, 0, 2)).reshape(D, 64)
    lu2 = np.transpose(lu, (1, 0, 2)).reshape(D, 64)
    d["rsmall"] = _f32(np.concatenate([sw2, md.T, lu2], axis=1))
    d["dgnt"] = _bf(np.transpose(lg, (0, 2, 1)).reshape(64, D))
    d["biasw"] = _bf(inp["L0_ffn_bias_w"])
    d["debiaswT"] = _f32(_f32(inp["L0_ffn_debias_w"]).T)
    d["impc"] = _f32(inp["L0_ffn_imp_center"]).reshape(1, 8)
    d["imps"] = _f32(inp["L0_ffn_imp_slope"]).reshape(1, 8)

    osl = slice(c * 1024, (c + 1) * 1024)
    d["w1t_s"] = _f32(_f32(inp["L1_ffn_w1"])[osl, :].T)
    d["w2t_s"] = _f32(_f32(inp["L1_ffn_w2"])[osl, :].T)
    d["w3t_s"] = _f32(_f32(inp["L1_ffn_w3"])[:, osl].T)
    d["g1"] = _f32(inp["L1_ffn_g1"]).reshape(1, D)
    d["g2"] = _f32(inp["L1_ffn_g2"]).reshape(1, D)
    d["g3"] = _f32(inp["L1_ffn_g3"]).reshape(1, DFF)
    return d


class Sem:
    def __init__(self, nc, name):
        self.h = nc.alloc_semaphore(name)
        self.n = 0


_GUARD_OPS = ("tensor_reduce", "reduce_max", "reduce_sum", "reciprocal",
              "stream_shuffle")


class _Synced:
    def __init__(self, i):
        self.i = i


class VG:

    def __init__(self, eng, sem):
        self._e = eng
        self._s = sem

    def __getattr__(self, name):
        a = getattr(self._e, name)
        if not callable(a) or name.startswith("wait") or name.startswith("sem"):
            return a

        def w(*args, **kw):
            i = a(*args, **kw)
            if hasattr(i, "then_inc"):
                self._s.n += 1
                i.then_inc(self._s.h, 1)
                self._e.wait_ge(self._s.h, self._s.n)
                return _Synced(i)
            return i
        return w


class Ser:

    def __init__(self, nc, block):
        self.nc = nc
        self.block = block
        self.sems = {n: Sem(nc, n) for n in
                     ("sy", "gp", "cc", "pe", "dv", "sc", "pl")}
        self.seen = {n: {} for n in ("sy", "gp", "pe", "dv", "sc", "pl")}
        self.qsem = {"sy": "sy", "gp": "gp"}

    def _waits(self, ekey, eng):
        for n, s in self.sems.items():
            if n == ekey and ekey not in ("sy", "gp"):
                continue
            if s.n > self.seen[ekey].get(n, 0):
                eng.wait_ge(s.h, s.n)
                self.seen[ekey][n] = s.n

    def _body(self, ekey, blockfn, fn):
        def run(eng):
            self._waits(ekey, eng)
            s = self.sems[ekey]
            if ekey == "dv":
                last = fn(VG(eng, s))
            else:
                last = fn(eng)
            if isinstance(last, _Synced):
                eng.sem_inc(s.h, 1)
                s.n += 1
            elif last is not None:
                last.then_inc(s.h, 1)
                s.n += 1
        blockfn(run)

    def pe(self, fn):
        self._body("pe", self.block.tensor, fn)

    def dv(self, fn):
        self._body("dv", self.block.vector, fn)

    def sc(self, fn):
        self._body("sc", self.block.scalar, fn)

    def pl(self, fn):
        self._body("pl", self.block.gpsimd, fn)

    def sy(self, fn):
        def run(eng):
            self._waits("sy", eng)
            def dma(**kw):
                i = eng.dma_start(**kw)
                i.then_inc(self.sems["sy"].h, 16)
                self.sems["sy"].n += 16
                return i
            fn(eng, dma)
        self.block.sync(run)

    def gp(self, fn):
        def run(eng):
            self._waits("gp", eng)
            def dma(**kw):
                i = eng.dma_start(**kw)
                i.then_inc(self.sems["gp"].h, 16)
                self.sems["gp"].n += 16
                return i
            def coll(*a, **kw):
                i = eng.collective_compute(*a, **kw)
                i.then_inc(self.sems["cc"].h, 1)
                self.sems["cc"].n += 1
                return i
            fn(eng, dma, coll)
        self.block.gpsimd(run)

    def raw(self, ekey, fn):
        blockfn = {"pe": self.block.tensor, "dv": self.block.vector,
                   "sc": self.block.scalar, "pl": self.block.gpsimd,
                   "sy": self.block.sync}[ekey]

        def run(eng):
            self._waits(ekey, eng)
            if ekey in ("sy", "pl"):
                qs = "sy" if ekey == "sy" else "gp"

                def dma(**kw):
                    i = eng.dma_start(**kw)
                    i.then_inc(self.sems[qs].h, 16)
                    return i
                if ekey == "pl":
                    fn(eng, dma, None)
                else:
                    fn(eng, dma)
            else:
                fn(eng)
        blockfn(run)

    def final_wait(self):
        def run(eng):
            self._waits("sy", eng)
            eng.wait_ge(self.sems["sy"].h, self.sems["sy"].n)
        self.block.sync(run)



class DBuf:

    def __init__(self, tsr, n, width):
        self.t = tsr
        self.n = n
        self.w = width

    def __getitem__(self, i):
        return self.t[:, (i % self.n) * self.w:((i % self.n) + 1) * self.w]


def build_nc(taps=()):
    taps = set(taps)
    nc = bass.Bass("TRN2", target_bir_lowering=False,
                   detect_race_conditions=False)

    def din(name, shape, dt=F32):
        return nc.dram_tensor(name, list(shape), dt, kind="ExternalInput").ap()

    ins = {}
    ins["x_in"] = din("x_in", [TOK, D])
    y_out = nc.dram_tensor("y_out", [TOK, D], F32, kind="ExternalOutput").ap()
    for l in (0, 1):
        kw = 512 if l == 0 else 128
        ins[f"wq_{l}"] = din(f"wq_{l}", [D, 512])
        ins[f"wk_{l}"] = din(f"wk_{l}", [D, kw])
        ins[f"wv_{l}"] = din(f"wv_{l}", [D, kw])
        ins[f"bq_{l}"] = din(f"bq_{l}", [128, 4])
        ins[f"bk_{l}"] = din(f"bk_{l}", [128, kw // 128])
        ins[f"bv_{l}"] = din(f"bv_{l}", [1, kw])
        ins[f"wo_{l}"] = din(f"wo_{l}", [512, D])
        ins[f"bo_{l}"] = din(f"bo_{l}", [1, D])
        ins[f"wmod_{l}"] = din(f"wmod_{l}", [D, 3072])
        ins[f"lc_{l}"] = din(f"lc_{l}", [1, D])
    ins["rsmall"] = din("rsmall", [D, 136])
    ins["dgnt"] = din("dgnt", [64, D], BF16)
    ins["biasw"] = din("biasw", [8, D], BF16)
    ins["debiaswT"] = din("debiaswT", [D, 8])
    ins["impc"] = din("impc", [1, 8])
    ins["imps"] = din("imps", [1, 8])
    ins["w1t_s"] = din("w1t_s", [D, 1024])
    ins["w2t_s"] = din("w2t_s", [D, 1024])
    ins["w3t_s"] = din("w3t_s", [1024, D])
    ins["g1"] = din("g1", [1, D])
    ins["g2"] = din("g2", [1, D])
    ins["g3"] = din("g3", [1, DFF])
    for n, shape, dt in [("ident_bf", [128, 128], BF16),
                         ("ident_f32", [128, 128], F32),
                         ("maskT_bf", [128, 4 * CH], BF16),
                         ("ones_f", [1, 128], F32),
                         ("onesc_f", [128, 1], F32),
                         ("onesc_b", [128, 1], BF16),
                         ("iota8", [128, 8], F32), ("m8", [128, 8], F32),
                         ("eye64x8", [64, 8], F32), ("eye8", [8, 8], F32),
                         ("cos_fm", [HD, T], BF16), ("sin_fm", [HD, T], BF16)]:
        ins[n] = din(n, shape, dt)

    x_cur = nc.dram_tensor("x_cur", [TOK, D], F32).ap()
    hfm_in = nc.dram_tensor("hfm_in", [D, TOK], BF16).ap()
    hfm_ag = nc.dram_tensor("hfm_ag", [GRP * D, TOK], BF16).ap()
    rs_in = nc.dram_tensor("rs_in", [T, D], BF16).ap()
    rs_out = nc.dram_tensor("rs_out", [TOK, D], BF16).ap()
    mod_in = nc.dram_tensor("mod_in", [1, 6144], F32).ap()
    mod_ag = nc.dram_tensor("mod_ag", [4, 6144], F32).ap()
    st_in = nc.dram_tensor("st_in", [1, 4], F32).ap()
    st_ag = nc.dram_tensor("st_ag", [1, 4], F32).ap()
    tern_in = nc.dram_tensor("tern_in", [6144, 1024], BF16).ap()
    tern_ag = nc.dram_tensor("tern_ag", [NCORE * 6144, 1024], BF16).ap()
    h2_dram = nc.dram_tensor("h2_dram", [TOK, DFF], BF16).ap()
    h2qf_dram = nc.dram_tensor("h2qf_dram", [DFF, TOK], BF16).ap()

    ARENA = 211264
    _arena = nc.bump_sbuf(ARENA)
    BASE = _arena[0] if isinstance(_arena, tuple) else _arena

    def at(name, off, shape, dt):
        return nc.alloc_sbuf_tensor_at(name, list(shape), dt, offset=BASE + off)

    Ao, Bo, Co, Do, Eo, Fo = 0, 32768, 65536, 98304, 131072, 163840
    hfm_sb = at("hfm_sb", Ao, [128, 16 * TOK], BF16)
    qfm_sb = at("qfm_sb", Ao, [128, LH * T], BF16)
    attn_tm = at("attn_tm", Ao, [128, 8 * D], BF16)
    gb_t = at("gb_t", Ao, [128, D], F32)
    hfm32_t = at("hfm32_t", Ao + 8192, [128, 16 * 128], F32)
    lcs_sb = at("lcs_sb", Ao + 16384, [1, 2 * D], F32)
    xq1_sb = at("xq1_sb", Ao, [128, 16 * TOK], BF16)
    kfm_sb = at("kfm_sb", Bo, [128, LH * T], BF16)
    h_sb1 = at("h_sb1", Bo, [128, 4 * D], F32)
    wo_buf = DBuf(at("wo_buf", Bo, [128, 2 * 16 * 512], BF16), 2, 16 * 512)
    t1_buf = DBuf(at("t1_buf", Bo, [128, 2 * 16 * 512], BF16), 2, 16 * 512)
    g3b_sb = at("g3b_sb", Bo, [128, DFF], BF16)
    t3_buf = DBuf(at("t3_buf", Bo, [128, 2 * 16 * 512], BF16), 2, 16 * 512)
    modc_sb = at("modc_sb", Bo, [1, 6144], F32)
    q3st2_sb = at("q3st2", Bo + 16384, [128, 512], BF16)
    vtm_sb = at("vtm_sb", Co, [128, 32 * 512], BF16)
    h_sb2 = at("h_sb2", Co, [128, 4 * D], F32)
    ga_t = at("ga_t", Co, [128, D], F32)
    bo_t = at("bo_t", Co + 8192, [128, D], F32)
    t2_buf = DBuf(at("t2_buf", Co, [128, 2 * 16 * 512], BF16), 2, 16 * 512)
    h2qf_buf = DBuf(at("h2qf_buf", Co, [128, 2 * 16 * 128], BF16), 2, 16 * 128)
    wst_buf = DBuf(at("wst_buf", Do, [128, 2 * 16 * 512], BF16), 2, 16 * 512)
    attn_sb = at("attn_sb", Do, [128, 32 * 512], BF16)
    xq2_sb = at("xq2_sb", Do, [128, 16 * TOK], BF16)
    ca_t = at("ca_t", Do, [128, D], F32)
    sa_t = at("sa_t", Do + 8192, [128, D], F32)
    tw_buf = DBuf(at("tw_buf", Do, [128, 2 * 3072], F32), 2, 3072)
    xt_sb = at("xt_sb", Eo, [128, D], F32)
    zt_sb = at("zt_sb", Eo + 8192, [128, D], F32)
    hbt_sb = at("hbt_sb", Eo + 16384, [128, D], BF16)
    hag_buf = DBuf(at("hag_buf", Eo, [128, 2 * 16 * CH], BF16), 2, 16 * CH)
    g1b_sb = at("g1b_sb", Eo + 16384, [128, D], F32)
    xqst_sb = at("xqst_sb", Eo + 24576, [128, D], BF16)
    pbt_sb = DBuf(at("pbt_sb", Eo + 20480, [128, 4 * CH], BF16), 4, CH)
    av_sb = at("av_sb", Eo + 24576, [128, CH], BF16)
    dn_sb = at("dn_sb", Eo + 25600, [1, CH], F32)
    rl_sb = at("rl_sb", Eo + 27648, [128, 4], F32)
    rsm_sb = at("rsm_sb", Eo + 8192, [128, 16 * 136], F32)
    rsm2_sb = at("rsm2_sb", Ao + 16384, [128, 16 * 136], F32)
    afm_t = at("afm_t", Eo + 8192, [128, 16 * 128], BF16)
    h2_t = at("h2_t", Eo, [128, DFF], BF16)
    n3_t = at("n3_t", Eo + 16384, [128, DFF], BF16)
    s1_t = at("s1_t", Eo + 12288, [128, CH], F32)
    h2st = at("h2st", Eo + 14336, [128, CH], BF16)
    _fc = [BASE + Fo]

    def fal(name, shape, dt):
        t = nc.alloc_sbuf_tensor_at(name, list(shape), dt, offset=_fc[0])
        it = 2 if dt == BF16 else 4
        n = shape[-1] * (shape[1] if len(shape) == 3 else 1) * it
        _fc[0] += (n + 31) // 32 * 32
        return t

    ident_bf = fal("identb", [128, 128], BF16)
    ident_f32 = fal("identf", [128, 128], F32)
    maskT_sb = fal("maskT", [128, 4 * CH], BF16)
    ones_f = fal("ones_fs", [1, 128], F32)
    onesc_f = fal("onesc_fs", [128, 1], F32)
    onesc_b = fal("onesc_bs", [128, 1], BF16)
    iota8_sb = fal("iota8s", [128, 8], F32)
    m8_sb = fal("m8s", [128, 8], F32)
    eye64_sb = fal("eye64s", [64, 8], F32)
    eye8_sb = fal("eye8s", [8, 8], F32)
    cosg_sb = fal("cosg", [128, CH], BF16)
    sing_sb = fal("sing", [128, CH], BF16)
    rp0_sb = fal("rp0", [128, CH], F32)
    rp1_sb = fal("rp1", [128, CH], F32)
    dgnt_sb = fal("dgnts", [64, D], BF16)
    biasw_sb = fal("biasws", [8, D], BF16)
    dbwT_sb = fal("dbwTs", [128, 16 * 8], F32)
    stg_sb = fal("stgs", [1, D], F32)
    stg2_sb = fal("stg2s", [1, D], F32)
    bq_sb = fal("bqs", [128, 4], F32)
    bk_sb = fal("bks", [128, 4], F32)
    bvb_sb = fal("bvbs", [128, 512], F32)
    sml = fal("smls", [128, 144], F32)
    gsc_sb = fal("gscs", [128, 64], F32)
    g8a_sb = fal("g8as", [128, 8], F32)
    g8b_sb = fal("g8bs", [128, 8], F32)
    g8c_sb = fal("g8cs", [128, 8], F32)
    g8d_sb = fal("g8ds", [128, 8], F32)
    smlb = fal("smlbs", [128, 64], BF16)
    mkb_sb = fal("mkbs", [128, 8], BF16)
    fmm_sb = fal("fmms", [64, 128], BF16)
    fmk_sb = fal("fmks", [8, 128], BF16)
    qn_sb = fal("qns", [1, 64], F32)
    kn_sb = fal("kns", [1, 8], F32)
    nB_sb = fal("nBs", [128, 1], F32)
    epsln = fal("epslns", [128, 1], F32)
    epsbt = fal("epsbts", [128, 1], F32)
    sq_sb = fal("sqs", [128, 512], F32)
    stat_sb = fal("stats", [128, 16], F32)
    asum_sb = fal("asums", [128, 4], F32)
    arr_sb = fal("arrs", [1, 64], F32)
    a_bc = fal("a_bcs", [128, 8], F32)
    impb = fal("impbs", [128, 16], F32)
    c1b_sb = fal("c1bs", [128, 64], F32)
    c3b_sb = fal("c3bs", [128, 64], F32)
    mtmb_sb = fal("mtmbs", [128, 8], F32)
    scfm_sb = fal("scfms", [128, 64], F32)
    xst_sb = fal("xsts", [128, 512], F32)
    q3st_sb = fal("q3st", [128, 512], BF16)
    assert _fc[0] - BASE - Fo <= 47424, f"F overflow {_fc[0] - BASE - Fo}"

    psA = nc.alloc_psum_tensor("psA", [128, 512], F32)
    psB = nc.alloc_psum_tensor("psB", [128, 512], F32)
    psC = nc.alloc_psum_tensor("psC", [128, 512], F32)
    psD = nc.alloc_psum_tensor("psD", [128, 512], F32)
    psEF = nc.alloc_psum_tensor("psEF", [128, 1024], F32)
    psE = psEF[:, 0:512]
    psF = psEF[:, 512:1024]
    pbA = nc.alloc_psum_tensor("pbA", [128, 512], BF16)
    pdn = nc.alloc_psum_tensor("pdn", [2, 512], F32)
    pstile = [psA, psB, psC, psD, psE, psF, pbA, pdn]

    tap_list = []

    def want(name):
        return name in taps

    def mk_tap(name, shape, dt=F32):
        t = nc.dram_tensor("tap_" + name, list(shape), dt,
                           kind="ExternalOutput").ap()
        tap_list.append(name)
        return t

    def V3(tsr, a):
        return tsr[:].rearrange("p (x y) -> p x y", x=a)

    hfm_v = V3(hfm_sb, 16)
    qfm_v = V3(qfm_sb, LH)
    kfm_v = V3(kfm_sb, LH)
    vtm_v = V3(vtm_sb, 32)
    attn_v = V3(attn_sb, 32)
    attn_tm_v = V3(attn_tm, 8)
    xq1_v = V3(xq1_sb, 16)
    xq2_v = V3(xq2_sb, 16)
    hfm32_v = V3(hfm32_t, 16)
    afm_v = V3(afm_t, 16)
    rsm_v = V3(rsm_sb, 16)
    dbwT_v = V3(dbwT_sb, 16)

    with nc.Block() as block:
        S = Ser(nc, block)

        def p0_loads(s, dma):
            dma(out=x_cur, in_=ins["x_in"])
            dma(out=ident_bf[:], in_=ins["ident_bf"])
            dma(out=ident_f32[:], in_=ins["ident_f32"])
            dma(out=maskT_sb[:], in_=ins["maskT_bf"])
            dma(out=ones_f[:], in_=ins["ones_f"])
            dma(out=onesc_f[:], in_=ins["onesc_f"])
            dma(out=onesc_b[:], in_=ins["onesc_b"])
            dma(out=iota8_sb[:], in_=ins["iota8"])
            dma(out=m8_sb[:], in_=ins["m8"])
            dma(out=eye64_sb[:], in_=ins["eye64x8"])
            dma(out=eye8_sb[:], in_=ins["eye8"])
            dma(out=dgnt_sb[:], in_=ins["dgnt"])
            dma(out=biasw_sb[:], in_=ins["biasw"])
            dma(out=dbwT_v, in_=ins["debiaswT"].rearrange("(f p) c -> p f c", p=128))
            dma(out=lcs_sb[0:1, 0:D], in_=ins["lc_0"])
            dma(out=lcs_sb[0:1, D:2 * D], in_=ins["lc_1"])
        S.sy(p0_loads)

        def p0_eps(v):
            v.memset(epsln[:], EPS_LN)
            v.memset(asum_sb[:], 0.0)
            v.memset(arr_sb[0:1, :], 0.0)
            v.memset(qn_sb[0:1, :], 0.0)
            v.memset(kn_sb[0:1, :], 0.0)
            v.memset(stat_sb[:], 0.0)
            return v.memset(epsbt[:], EPS_BIT)
        S.dv(p0_eps)

        S.sc(lambda s: s.activation(lcs_sb[0:1, 0:2 * D], lcs_sb[0:1, 0:2 * D],
                                    AF.Silu))

        def p1_sctr(t):
            for l in range(2):
                for fc in range(16):
                    i = t.transpose(
                        psD[:, (l * 16 + fc):(l * 16 + fc) + 1],
                        lcs_sb[0:1, l * D + fc * 128:l * D + (fc + 1) * 128],
                        ident_f32[0:1, 0:1])
            return i
        S.pe(p1_sctr)
        S.dv(lambda v: v.tensor_copy(scfm_sb[:, 0:32], psD[:, 0:32]))

        for l in range(2):
            for fc in range(16):
                S.sy(lambda s, dma, l=l, fc=fc:
                     dma(out=tw_buf[fc % 2][:, 0:3072],
                         in_=ins[f"wmod_{l}"][fc * 128:(fc + 1) * 128, :]))

                def p1_modmm(t, l=l, fc=fc):
                    for j in range(6):
                        i = t.matmul(pstile[j][0:1, 0:512],
                                     scfm_sb[:, (l * 16 + fc):(l * 16 + fc) + 1],
                                     tw_buf[fc % 2][:, j * 512:(j + 1) * 512],
                                     start=(fc == 0), stop=(fc == 15))
                    return i
                S.pe(p1_modmm)

            def p1_modev(v, l=l):
                for j in range(6):
                    i = v.tensor_copy(
                        modc_sb[0:1, l * 3072 + j * 512:l * 3072 + (j + 1) * 512],
                        pstile[j][0:1, 0:512])
                return i
            S.dv(p1_modev)

        def p1_modag(g, dma, coll):
            dma(out=mod_in, in_=modc_sb[0:1, :])
            g.wait_ge(S.sems["gp"].h, S.sems["gp"].n)
            coll("AllGather", ALU.bypass, replica_groups=RG_BATCH,
                 ins=[mod_in], outs=[mod_ag])
            g.wait_ge(S.sems["cc"].h, S.sems["cc"].n)
        S.gp(p1_modag)


        wseq = []
        for wi, (name, nch, wdt) in enumerate(
                [("w1t_s", 16, 1024), ("w2t_s", 16, 1024), ("w3t_s", 8, 2048)]):
            for chk in range(nch):
                wseq.append((wi, name, chk, wdt))

        pe_s, sc_s, dv_s, sy_s, gp_s, cc_s = (S.sems[k] for k in
                                              ("pe", "sc", "dv", "sy", "gp", "cc"))
        P0, D0, Y0 = pe_s.n, dv_s.n, sy_s.n

        def abs_sy(s2, dma):
            for i, (wi, name, chk, wdt) in enumerate(wseq):
                if i >= 2:
                    s2.wait_ge(dv_s.h, D0 + 2 * (i - 1))
                dma(out=tw_buf[i][:, 0:wdt],
                    in_=ins[name][chk * 128:(chk + 1) * 128, :])
        S.raw("sy", abs_sy)

        def abs_dv(v):
            n = D0
            for i, (wi, name, chk, wdt) in enumerate(wseq):
                def g(ii):
                    nonlocal n
                    n += 1
                    ii.then_inc(dv_s.h, 1)
                    v.wait_ge(dv_s.h, n)
                v.wait_ge(sy_s.h, Y0 + 16 * (i + 1))
                if chk == 0:
                    g(v.tensor_reduce(asum_sb[:, wi:wi + 1],
                                      tw_buf[i][:, 0:wdt], AX, ALU.add,
                                      apply_absolute_value=True))
                    g(v.engine_nop())
                else:
                    g(v.tensor_reduce(stat_sb[:, 0:1], tw_buf[i][:, 0:wdt],
                                      AX, ALU.add, apply_absolute_value=True))
                    g(v.tensor_tensor(asum_sb[:, wi:wi + 1],
                                      asum_sb[:, wi:wi + 1], stat_sb[:, 0:1],
                                      ALU.add))
        S.raw("dv", abs_dv)
        dv_s.n = D0 + 2 * len(wseq)
        sy_s.n = Y0 + 16 * len(wseq)

        S.pe(lambda t: t.matmul(pdn[0:1, 0:4], onesc_f[:, 0:1], asum_sb[:, 0:4],
                                start=True, stop=True))
        S.dv(lambda v: v.tensor_copy(arr_sb[0:1, 0:4], pdn[0:1, 0:4]))
        if want("arr0"):
            tp = mk_tap("arr0", [1, 64])
            S.sy(lambda s, dma: dma(out=tp, in_=arr_sb[0:1, :]))

        def p1_ar(g, dma, coll):
            dma(out=st_in, in_=arr_sb[0:1, 0:4])
            g.wait_ge(S.sems["gp"].h, S.sems["gp"].n)
            coll("AllReduce", ALU.add, replica_groups=RG_ALL,
                 ins=[st_in], outs=[st_ag])
            g.wait_ge(S.sems["cc"].h, S.sems["cc"].n)
            dma(out=arr_sb[0:1, 4:8], in_=st_ag)
        S.gp(p1_ar)

        def p1_scal(v):
            v.tensor_scalar(arr_sb[0:1, 8:11], arr_sb[0:1, 4:7],
                            1.0 / (DFF * D), None, ALU.mult)
            v.tensor_scalar(arr_sb[0:1, 12:15], arr_sb[0:1, 8:11], EPS_BIT,
                            None, ALU.add)
            v.reciprocal(arr_sb[0:1, 12:15], arr_sb[0:1, 12:15])
            v.tensor_scalar(arr_sb[0:1, 16:19], arr_sb[0:1, 8:11], 1.0, None,
                            ALU.add)
            v.tensor_scalar(arr_sb[0:1, 20:23], arr_sb[0:1, 8:11], -1.0, 1.0,
                            ALU.mult, ALU.add)
            v.reciprocal(arr_sb[0:1, 20:23], arr_sb[0:1, 20:23])
            return v.tensor_tensor(arr_sb[0:1, 16:19], arr_sb[0:1, 16:19],
                                   arr_sb[0:1, 20:23], ALU.mult)
        S.dv(p1_scal)
        S.sc(lambda s: s.activation(arr_sb[0:1, 24:27], arr_sb[0:1, 16:19], AF.Ln))
        S.dv(lambda v: v.tensor_scalar(arr_sb[0:1, 24:27], arr_sb[0:1, 24:27],
                                       0.5, None, ALU.mult))
        if want("arr1"):
            tp = mk_tap("arr1", [1, 64])
            S.sy(lambda s, dma: dma(out=tp, in_=arr_sb[0:1, :]))

        def p1_bc(t):
            for j in range(3):
                t.matmul(psD[:, j:j + 1], ones_f[0:1, :],
                         arr_sb[0:1, 12 + j:13 + j], start=True, stop=True)
            for j in range(3):
                i = t.matmul(psD[:, 3 + j:4 + j], ones_f[0:1, :],
                             arr_sb[0:1, 24 + j:25 + j], start=True, stop=True)
            return i
        S.pe(p1_bc)
        S.dv(lambda v: v.tensor_copy(a_bc[:, 0:6], psD[:, 0:6]))

        P0, C0, D0, Y0, G0 = pe_s.n, sc_s.n, dv_s.n, sy_s.n, gp_s.n

        def tern_sy(s2, dma):
            for i, (wi, name, chk, wdt) in enumerate(wseq):
                if i >= 2:
                    s2.wait_ge(sc_s.h, C0 + i - 1)
                dma(out=tw_buf[i][:, 0:wdt],
                    in_=ins[name][chk * 128:(chk + 1) * 128, :])
        S.raw("sy", tern_sy)

        def tern_sc(s):
            for i, (wi, name, chk, wdt) in enumerate(wseq):
                s.wait_ge(sy_s.h, Y0 + 16 * (i + 1))
                if i >= 1:
                    s.wait_ge(dv_s.h, D0 + i)
                s.activation(zt_sb[:, 0:wdt], tw_buf[i][:, 0:wdt], AF.Tanh,
                             scale=a_bc[:, wi:wi + 1]).then_inc(sc_s.h, 1)
        S.raw("sc", tern_sc)

        def tern_dv(v):
            for i, (wi, name, chk, wdt) in enumerate(wseq):
                v.wait_ge(sc_s.h, C0 + i + 1)
                if i >= 2:
                    v.wait_ge(gp_s.h, G0 + 16 * (i - 1))
                dst = hbt_sb[:, 0:wdt] if i % 2 == 0 else xqst_sb[:, 0:wdt]
                v.tensor_scalar(dst, zt_sb[:, 0:wdt], MAGIC, -MAGIC,
                                ALU.add, ALU.add).then_inc(dv_s.h, 1)
        S.raw("dv", tern_dv)

        def tern_gp(g, dma, coll):
            for i, (wi, name, chk, wdt) in enumerate(wseq):
                g.wait_ge(dv_s.h, D0 + i + 1)
                row0 = [0, 2048, 4096][wi] + chk * (128 if wi < 2 else 256)
                srcb = hbt_sb[:, 0:wdt] if i % 2 == 0 else xqst_sb[:, 0:wdt]
                if wi < 2:
                    dst = tern_in[row0:row0 + 128, :]
                else:
                    dst = tern_in.rearrange("r t -> (r t)")[
                        row0 * 1024: row0 * 1024 + 128 * 2048].rearrange(
                        "(p w) -> p w", p=128)
                dma(out=dst, in_=srcb)
        S.raw("pl", tern_gp)
        sc_s.n = C0 + len(wseq)
        dv_s.n = D0 + len(wseq)
        sy_s.n = Y0 + 16 * len(wseq)
        gp_s.n = G0 + 16 * len(wseq)

        tern_sem = Sem(nc, "tc")

        def p1_ternag(g, dma, coll):
            g.collective_compute(
                "AllGather", ALU.bypass, replica_groups=RG_ALL,
                ins=[tern_in], outs=[tern_ag]).then_inc(tern_sem.h, 1)
        S.gp(p1_ternag)

        if want("mod"):
            tp = mk_tap("mod", [4, 6144])
            S.sy(lambda s, dma: dma(out=tp, in_=mod_ag))
        if want("abc"):
            tp = mk_tap("abc", [128, 8])
            S.sy(lambda s, dma: dma(out=tp, in_=a_bc[:]))


        S.sy(lambda s, dma: dma(
            out=rsm_v, in_=ins["rsmall"].rearrange("(f p) c -> p f c", p=128)))

        def p2_cacc(v):
            for fc in range(16):
                base = fc * 136
                sw3 = rsm_sb[:, base:base + 64].rearrange("p (a b) -> p a b", b=8)
                md2 = rsm_sb[:, base + 64:base + 72]
                mdb = md2.to_broadcast([128, 8, 8])
                uu3 = rsm_sb[:, base + 72:base + 136].rearrange("p (a b) -> p a b", b=8)
                dbb = dbwT_sb[:, fc * 8:(fc + 1) * 8].to_broadcast([128, 8, 8])
                g3 = gsc_sb[:, 0:64].rearrange("p (a b) -> p a b", b=8)
                v.tensor_tensor(g3, sw3, mdb, ALU.mult)
                if fc == 0:
                    v.tensor_copy(sq_sb[:, 0:64], gsc_sb[:, 0:64])
                else:
                    v.tensor_tensor(sq_sb[:, 0:64], sq_sb[:, 0:64],
                                    gsc_sb[:, 0:64], ALU.add)
                v.tensor_tensor(gsc_sb[:, 0:8], md2, md2, ALU.mult)
                if fc == 0:
                    v.tensor_copy(sq_sb[:, 64:72], gsc_sb[:, 0:8])
                else:
                    v.tensor_tensor(sq_sb[:, 64:72], sq_sb[:, 64:72],
                                    gsc_sb[:, 0:8], ALU.add)
                v.tensor_tensor(g3, uu3, dbb, ALU.mult)
                if fc == 0:
                    i = v.tensor_copy(sq_sb[:, 72:136], gsc_sb[:, 0:64])
                else:
                    i = v.tensor_tensor(sq_sb[:, 72:136], sq_sb[:, 72:136],
                                        gsc_sb[:, 0:64], ALU.add)
            return i
        S.dv(p2_cacc)
        S.pe(lambda t: t.matmul(pdn[0:1, 0:136], onesc_f[:, 0:1],
                                sq_sb[:, 0:136], start=True, stop=True))
        S.dv(lambda v: v.tensor_copy(stg_sb[0:1, 0:136], pdn[0:1, 0:136]))

        def bcast_to(dst, src_row, W, j0=0):
            nj = (W + 511) // 512
            pe_s, dv_s = S.sems["pe"], S.sems["dv"]
            P0, D0 = pe_s.n, dv_s.n

            def bc_pe(t):
                for j in range(nj):
                    w_ = min(512, W - j * 512)
                    if j >= 1:
                        t.wait_ge(dv_s.h, D0 + j)
                    t.matmul(psD[:, 0:w_], ones_f[0:1, :],
                             src_row[0:1, j0 + j * 512:j0 + j * 512 + w_],
                             start=True, stop=True).then_inc(pe_s.h, 1)
            S.raw("pe", bc_pe)

            def bc_dv(v):
                for j in range(nj):
                    w_ = min(512, W - j * 512)
                    v.wait_ge(pe_s.h, P0 + j + 1)
                    v.tensor_copy(dst[:, j * 512:j * 512 + w_],
                                  psD[:, 0:w_]).then_inc(dv_s.h, 1)
            S.raw("dv", bc_dv)
            pe_s.n = P0 + nj
            dv_s.n = D0 + nj

        bcast_to(c1b_sb, stg_sb, 64)
        bcast_to(mtmb_sb, stg_sb, 8, j0=64)
        bcast_to(c3b_sb, stg_sb, 64, j0=72)
        S.sy(lambda s, dma: (dma(out=stg_sb[0:1, 0:8], in_=ins["impc"]),
                             dma(out=stg_sb[0:1, 8:16], in_=ins["imps"])))
        bcast_to(impb[:, 0:8], stg_sb, 8)
        bcast_to(impb[:, 8:16], stg_sb, 8, j0=8)
        if want("c1b"):
            tp = mk_tap("c1b", [128, 64])
            S.sy(lambda s, dma: dma(out=tp, in_=c1b_sb[:]))

        def load_mod_row(l, m, plus1=False, transform_g=False):
            def f(s, dma):
                dma(out=stg_sb[0:1, :].rearrange("x (rk c) -> x rk c", rk=4),
                    in_=mod_ag[:, l * 3072 + m * 512:l * 3072 + (m + 1) * 512]
                        .rearrange("(x rk) c -> x rk c", x=1))
            S.sy(f)
            if plus1:
                S.dv(lambda v: v.tensor_scalar(stg_sb[0:1, :], stg_sb[0:1, :],
                                               1.0, None, ALU.add))
            if transform_g:
                def tg(v):
                    v.tensor_single_scalar(stg2_sb[0:1, :], stg_sb[0:1, :],
                                           0.0, ALU.is_ge)
                    v.tensor_scalar(stg2_sb[0:1, :], stg2_sb[0:1, :], 2.0,
                                    -1.0, ALU.mult, ALU.add)
                    v.tensor_tensor(stg_sb[0:1, :], stg_sb[0:1, :],
                                    stg2_sb[0:1, :], ALU.mult)
                    v.tensor_single_scalar(stg_sb[0:1, :], stg_sb[0:1, :],
                                           0.25, ALU.max)
                    return v.tensor_tensor(stg_sb[0:1, :], stg_sb[0:1, :],
                                           stg2_sb[0:1, :], ALU.mult)
                S.dv(tg)

        def ln_phase(dst, l):
            pe_s, sc_s, dv_s, sy_s = (S.sems[k] for k in ("pe", "sc", "dv", "sy"))
            P0, C0, D0, Y0 = pe_s.n, sc_s.n, dv_s.n, sy_s.n
            att = dst == "attn"
            dvn = 13 if att else 8
            scn = 4
            pen = 4 if att else 0

            def ln_sy(s2, dma):
                for t_i in range(8):
                    if t_i >= 1:
                        s2.wait_ge(sc_s.h, C0 + scn * t_i)
                        s2.wait_ge(dv_s.h, D0 + dvn * (t_i - 1) + 1)
                    dma(out=xt_sb[:], in_=x_cur[t_i * 128:(t_i + 1) * 128, :])
            S.raw("sy", ln_sy)

            def ln_sc(s):
                n = C0
                for t_i in range(8):
                    s.wait_ge(sy_s.h, Y0 + 16 * (t_i + 1))
                    if t_i >= 1:
                        s.wait_ge(dv_s.h, D0 + dvn * t_i)
                    i = s.activation(zt_sb[:], xt_sb[:], AF.Square,
                                     accum_out=stat_sb[:, 1:2])
                    i.then_inc(s.h_ if False else sc_s.h, 1)
                    n += 1
                    s.wait_ge(sc_s.h, n)
                    s.wait_ge(dv_s.h, D0 + dvn * t_i + 5)
                    i = s.activation(stat_sb[:, 6:7], stat_sb[:, 5:6], AF.Ln,
                                     bias=epsln[:, 0:1])
                    i.then_inc(sc_s.h, 1)
                    n += 1
                    s.wait_ge(sc_s.h, n)
                    i = s.activation(stat_sb[:, 7:8], stat_sb[:, 6:7], AF.Exp,
                                     scale=-0.5)
                    i.then_inc(sc_s.h, 1)
                    n += 1
                    s.wait_ge(sc_s.h, n)
                    s.wait_ge(dv_s.h, D0 + dvn * t_i + 6)
                    i = s.activation(zt_sb[:], xt_sb[:], AF.Identity,
                                     bias=stat_sb[:, 8:9],
                                     scale=stat_sb[:, 7:8])
                    i.then_inc(sc_s.h, 1)
                    n += 1
            S.raw("sc", ln_sc)

            def ln_dv(v):
                n = D0
                dvt = 13 if att else 8
                for t_i in range(8):
                    def g(i):
                        nonlocal n
                        n += 1
                        i.then_inc(dv_s.h, 1)
                        v.wait_ge(dv_s.h, n)
                    v.wait_ge(sy_s.h, Y0 + 16 * (t_i + 1))
                    g(v.tensor_reduce(stat_sb[:, 0:1], xt_sb[:], AX, ALU.add))
                    g(v.tensor_scalar(stat_sb[:, 2:3], stat_sb[:, 0:1],
                                      1.0 / D, None, ALU.mult))
                    v.wait_ge(sc_s.h, C0 + scn * t_i + 1)
                    g(v.tensor_scalar(stat_sb[:, 3:4], stat_sb[:, 1:2],
                                      1.0 / D, None, ALU.mult))
                    g(v.tensor_tensor(stat_sb[:, 4:5], stat_sb[:, 2:3],
                                      stat_sb[:, 2:3], ALU.mult))
                    g(v.tensor_tensor(stat_sb[:, 5:6], stat_sb[:, 3:4],
                                      stat_sb[:, 4:5], ALU.subtract))
                    v.wait_ge(sc_s.h, C0 + scn * t_i + 3)
                    g(v.tensor_scalar(stat_sb[:, 8:9], stat_sb[:, 2:3],
                                      stat_sb[:, 7:8], -1.0, ALU.mult,
                                      ALU.mult))
                    v.wait_ge(sc_s.h, C0 + scn * t_i + 4)
                    if att:
                        g(v.tensor_tensor(zt_sb[:], zt_sb[:], ca_t[:],
                                          ALU.mult))
                        g(v.tensor_tensor(zt_sb[:], zt_sb[:], sa_t[:],
                                          ALU.add))
                        if t_i >= 1:
                            v.wait_ge(pe_s.h, P0 + 4 * t_i)
                        g(v.tensor_copy(hbt_sb[:], zt_sb[:]))
                        for fb in range(4):
                            v.wait_ge(pe_s.h, P0 + 4 * t_i + fb + 1)
                            i = v.tensor_copy(
                                hfm_sb[:].rearrange("p (f k) -> p f k", f=16)
                                [:, fb * 4:(fb + 1) * 4,
                                 t_i * 128:(t_i + 1) * 128],
                                pbA[:].rearrange("p (a b) -> p a b", a=4))
                            i.then_inc(dv_s.h, 1)
                            n += 1
                    else:
                        hdst = h_sb1 if t_i < 4 else h_sb2
                        off = (t_i % 4) * D
                        g(v.tensor_tensor(zt_sb[:], zt_sb[:], ca_t[:],
                                          ALU.mult))
                        g(v.tensor_tensor(hdst[:, off:off + D], zt_sb[:],
                                          sa_t[:], ALU.add))
            S.raw("dv", ln_dv)

            if att:
                def ln_pe(t):
                    dvt = 13
                    for t_i in range(8):
                        t.wait_ge(dv_s.h, D0 + dvt * t_i + 9)
                        for fb in range(4):
                            if t_i > 0 and fb == 0:
                                t.wait_ge(dv_s.h, D0 + dvt * t_i)
                            elif fb >= 1:
                                t.wait_ge(dv_s.h, D0 + dvt * t_i + 9 + fb)
                            for j in range(4):
                                i = t.transpose(
                                    pbA[:, j * 128:(j + 1) * 128],
                                    hbt_sb[:, (fb * 4 + j) * 128:
                                           (fb * 4 + j + 1) * 128],
                                    ident_bf[:])
                            i.then_inc(pe_s.h, 1)
                S.raw("pe", ln_pe)
            pe_s.n = P0 + pen * 8
            pe_s.n = P0 + pen * 8
            sc_s.n = C0 + scn * 8
            dv_s.n = D0 + dvn * 8
            sy_s.n = Y0 + 16 * 8

        def layer(l):
            kw = 512 if l == 0 else 128
            nof_kv = kw // 128
            load_mod_row(l, 1, plus1=True)
            bcast_to(ca_t, stg_sb, D)
            load_mod_row(l, 0)
            bcast_to(sa_t, stg_sb, D)
            ln_phase("attn", l)
            if want(f"h_l{l}"):
                tp = mk_tap(f"h_l{l}", [128, 16 * TOK], BF16)
                S.sy(lambda s, dma: dma(out=tp, in_=hfm_sb[:]))

            def agh(g, dma, coll):
                dma(out=hfm_in.rearrange("(f p) c -> p f c", p=128),
                    in_=hfm_sb[:].rearrange("p (f k) -> p f k", f=16))
                g.wait_ge(S.sems["gp"].h, S.sems["gp"].n)
                coll("AllGather", ALU.bypass, replica_groups=RG_BATCH,
                     ins=[hfm_in], outs=[hfm_ag])
                g.wait_ge(S.sems["cc"].h, S.sems["cc"].n)
            S.gp(agh)

            def ldb(s, dma):
                dma(out=bq_sb[:, 0:4], in_=ins[f"bq_{l}"])
                dma(out=bk_sb[:, 0:nof_kv], in_=ins[f"bk_{l}"])
                dma(out=stg_sb[0:1, 0:kw], in_=ins[f"bv_{l}"])
            S.sy(ldb)
            bcast_to(bvb_sb[:, 0:kw], stg_sb, kw)

            pe_s, sc_s, dv_s, sy_s, gp_s = (S.sems[k] for k in
                                            ("pe", "sc", "dv", "sy", "gp"))
            for wg in ("q", "k", "v"):
                nof = 4 if wg == "q" else nof_kv
                wname = f"w{wg}_{l}"
                rope = (l == 1 and wg in ("q", "k"))
                S.gp(lambda g, dma, coll, wname=wname, nof=nof:
                     dma(out=wst_buf.t[:, 0:16 * nof * 128]
                         .rearrange("p (f c) -> p f c", f=16),
                         in_=ins[wname].rearrange("(f p) c -> p f c", p=128)))
                steps = [(g_, j) for g_ in range(8)
                         for j in range(nof if wg != "v" else 4)]
                ns = len(steps)
                P0, D0, Y0, G0 = pe_s.n, dv_s.n, sy_s.n, gp_s.n
                dvinc = 2 if rope else 1
                syinc = 16 * (3 if rope else 1)
                nperg = nof if wg != "v" else 4

                def qkv_sy(s2, dma, wg=wg, steps=steps, nperg=nperg,
                           dvinc=dvinc, rope=rope, P0=P0, D0=D0):
                    for g_ in range(8):
                        if rope and g_ >= 1:
                            s2.wait_ge(dv_s.h, D0 + dvinc * nperg * g_)
                        elif g_ >= 2:
                            s2.wait_ge(dv_s.h, D0 + dvinc * nperg * (g_ - 1))
                        dma(out=hag_buf[g_][:].rearrange("p (f c) -> p f c", f=16),
                            in_=hfm_ag[owner(g_) * D:(owner(g_) + 1) * D,
                                       lidx(g_) * CH:(lidx(g_) + 1) * CH]
                            .rearrange("(f p) c -> p f c", p=128))
                        if rope:
                            dma(out=cosg_sb[:],
                                in_=ins["cos_fm"][:, g_ * CH:(g_ + 1) * CH])
                            dma(out=sing_sb[:],
                                in_=ins["sin_fm"][:, g_ * CH:(g_ + 1) * CH])
                S.raw("sy", qkv_sy)

                def qkv_pe(t, wg=wg, steps=steps, nof=nof, syinc=syinc,
                           dvinc=dvinc, P0=P0, D0=D0, Y0=Y0, G0=G0):
                    t.wait_ge(gp_s.h, G0)
                    seen = [0]

                    def wdv(val):
                        if val > seen[0]:
                            t.wait_ge(dv_s.h, val)
                            seen[0] = val
                    for si, (g_, j) in enumerate(steps):
                        if j == 0:
                            t.wait_ge(sy_s.h, Y0 + syinc * (g_ + 1))
                        if si >= 2:
                            wdv(D0 + dvinc * (si - 1))
                        ps = psA if si % 2 == 0 else psB
                        kw2 = 512 if (wg != "v" or l == 0) else 128
                        for fc in range(16):
                            if wg == "v":
                                i = t.matmul(ps[:, 0:kw2],
                                             hag_buf[g_][:, fc * 512 + j * 128:
                                                         fc * 512 + (j + 1) * 128],
                                             wst_buf.t[:, fc * kw2:(fc + 1) * kw2],
                                             start=(fc == 0), stop=(fc == 15))
                            else:
                                i = t.matmul(ps[:, 0:512],
                                             wst_buf.t[:, (fc * nof + j) * 128:
                                                       (fc * nof + j + 1) * 128],
                                             hag_buf[g_][:, fc * 512:(fc + 1) * 512],
                                             start=(fc == 0), stop=(fc == 15))
                        i.then_inc(pe_s.h, 1)
                S.raw("pe", qkv_pe)

                def qkv_dv(v, wg=wg, steps=steps, nof=nof, rope=rope,
                           P0=P0, D0=D0):
                    for si, (g_, j) in enumerate(steps):
                        v.wait_ge(pe_s.h, P0 + si + 1)
                        ps = psA if si % 2 == 0 else psB
                        if wg == "v":
                            kw2 = 512 if l == 0 else 128
                            v.scalar_tensor_tensor(
                                vtm_sb[:, (g_ * 4 + j) * 512:
                                       (g_ * 4 + j) * 512 + kw2],
                                ps[:, 0:kw2], 1.0, bvb_sb[:, 0:kw2],
                                ALU.mult, ALU.add).then_inc(dv_s.h, 1)
                            continue
                        dst = qfm_sb if wg == "q" else kfm_sb
                        dstoff = (j * T + g_ * CH) if wg == "q" else                                  ((j * T + g_ * CH) if l == 0 else g_ * CH)
                        bias = bq_sb if wg == "q" else bk_sb
                        if rope:
                            v.tensor_scalar(rp0_sb[:], ps[:, 0:512],
                                            bias[:, j:j + 1],
                                            SCL if wg == "q" else None,
                                            ALU.add,
                                            ALU.mult if wg == "q" else ALU.bypass)
                            v.tensor_tensor(rp1_sb[:], rp0_sb[:], cosg_sb[:],
                                            ALU.mult)
                            i = v.stream_shuffle(xst_sb[:], rp0_sb[:], SHUF_PAIR)
                            i.then_inc(dv_s.h, 1)
                            v.wait_ge(dv_s.h, D0 + 2 * si + 1)
                            v.tensor_tensor(xst_sb[:], xst_sb[:], sing_sb[:],
                                            ALU.mult)
                            v.tensor_tensor(dst[:, dstoff:dstoff + 512],
                                            rp1_sb[:], xst_sb[:],
                                            ALU.add).then_inc(dv_s.h, 1)
                        else:
                            v.tensor_scalar(dst[:, dstoff:dstoff + 512],
                                            ps[:, 0:512], bias[:, j:j + 1],
                                            SCL if wg == "q" else None,
                                            ALU.add,
                                            ALU.mult if wg == "q" else
                                            ALU.bypass).then_inc(dv_s.h, 1)
                S.raw("dv", qkv_dv)
                pe_s.n = P0 + ns
                dv_s.n = D0 + dvinc * ns
                sy_s.n = Y0 + syinc * 8

            nus = [( "q", h, g_) for h in range(LH) for g_ in range(8)] +                   [("k", kvh, g_) for kvh in range(nof_kv) for g_ in range(8)]
            P0, C0, D0 = pe_s.n, sc_s.n, dv_s.n

            def nrm_sc(s, nus=nus, P0=P0):
                for ui, (wg, h, g_) in enumerate(nus):
                    if ui >= 2:
                        s.wait_ge(pe_s.h, P0 + ui - 1)
                    src_ = qfm_sb if wg == "q" else kfm_sb
                    off = (h * T + g_ * CH) if (wg == "q" or l == 0) else g_ * CH
                    dstsq = sq_sb if ui % 2 == 0 else xst_sb
                    s.activation(dstsq[:], src_[:, off:off + 512],
                                 AF.Square).then_inc(sc_s.h, 1)
            S.raw("sc", nrm_sc)

            def nrm_pe(t, nus=nus, C0=C0, D0=D0):
                for ui, (wg, h, g_) in enumerate(nus):
                    t.wait_ge(sc_s.h, C0 + ui + 1)
                    if ui >= 2:
                        t.wait_ge(dv_s.h, D0 + 2 * (ui - 1))
                    dstsq = sq_sb if ui % 2 == 0 else xst_sb
                    ps = psE if ui % 2 == 0 else psF
                    t.matmul(ps[0:1, 0:512], onesc_f[:, 0:1], dstsq[:],
                             start=True, stop=True).then_inc(pe_s.h, 1)
            S.raw("pe", nrm_pe)

            def nrm_dv(v, nus=nus, P0=P0, D0=D0):
                n = D0
                for ui, (wg, h, g_) in enumerate(nus):
                    v.wait_ge(pe_s.h, P0 + ui + 1)
                    ps = psE if ui % 2 == 0 else psF
                    if wg == "q":
                        i = v.tensor_reduce(qn_sb[0:1, h * 8 + g_:h * 8 + g_ + 1],
                                            ps[0:1, 0:512], AX, ALU.max)
                        i.then_inc(dv_s.h, 1)
                        n += 1
                        v.wait_ge(dv_s.h, n)
                        v.engine_nop().then_inc(dv_s.h, 1)
                        n += 1
                    else:
                        i = v.tensor_reduce(arr_sb[0:1, 40:41], ps[0:1, 0:512],
                                            AX, ALU.max)
                        i.then_inc(dv_s.h, 1)
                        n += 1
                        v.wait_ge(dv_s.h, n)
                        if g_ == 0:
                            i = v.tensor_copy(kn_sb[0:1, h:h + 1],
                                              arr_sb[0:1, 40:41])
                        else:
                            i = v.tensor_tensor(kn_sb[0:1, h:h + 1],
                                                kn_sb[0:1, h:h + 1],
                                                arr_sb[0:1, 40:41], ALU.max)
                        i.then_inc(dv_s.h, 1)
                        n += 1
            S.raw("dv", nrm_dv)
            pe_s.n = P0 + len(nus)
            sc_s.n = C0 + len(nus)
            dv_s.n = D0 + 2 * len(nus)

            if l == 1:
                S.dv(lambda v: v.tensor_copy(
                    kn_sb[0:1, 1:4].rearrange("x (a b) -> x a b", a=1),
                    kn_sb[0:1, 0:1].to_broadcast([1, 1, 3])))
            def negsq(s):
                i = s.activation(qn_sb[0:1, 32:64], qn_sb[0:1, 0:32], AF.Ln)
                i.then_inc(S.sems["sc"].h, 1)
                S.sems["sc"].n += 1
                s.wait_ge(S.sems["sc"].h, S.sems["sc"].n)
                i = s.activation(qn_sb[0:1, 32:64], qn_sb[0:1, 32:64], AF.Exp,
                                 scale=0.5)
                i.then_inc(S.sems["sc"].h, 1)
                S.sems["sc"].n += 1
                s.wait_ge(S.sems["sc"].h, S.sems["sc"].n)
                i = s.activation(kn_sb[0:1, 4:8], kn_sb[0:1, 0:4], AF.Ln)
                i.then_inc(S.sems["sc"].h, 1)
                S.sems["sc"].n += 1
                s.wait_ge(S.sems["sc"].h, S.sems["sc"].n)
                return s.activation(kn_sb[0:1, 4:8], kn_sb[0:1, 4:8], AF.Exp,
                                    scale=0.5)
            S.sc(negsq)

            def negb(v):
                v.tensor_tensor(arr_sb[0:1, 0:32].rearrange("x (a b) -> x a b", b=8),
                                qn_sb[0:1, 32:64].rearrange("x (a b) -> x a b", b=8),
                                kn_sb[0:1, 4:8].to_broadcast([1, 4, 8]),
                                ALU.mult)
                return v.tensor_scalar(arr_sb[0:1, 0:32], arr_sb[0:1, 0:32],
                                       -1.0, None, ALU.mult)
            S.dv(negb)
            if want(f"qkv_l{l}"):
                tq = mk_tap(f"q_l{l}", [128, LH * T], BF16)
                tk_ = mk_tap(f"k_l{l}", [128, LH * T], BF16)
                tv = mk_tap(f"v_l{l}", [128, 32 * 512], BF16)
                S.sy(lambda s, dma: (dma(out=tq, in_=qfm_sb[:]),
                                     dma(out=tk_, in_=kfm_sb[:]),
                                     dma(out=tv, in_=vtm_sb[:])))

            pe_s, sc_s, dv_s = S.sems["pe"], S.sems["sc"], S.sems["dv"]
            P0, C0, D0 = pe_s.n, sc_s.n, dv_s.n
            us = [(h, gq) for h in range(LH) for gq in range(8)]
            nkcs = [4 * gq + 4 for (h, gq) in us]
            Pu = [P0]
            Cu = [C0]
            Du = [D0]
            for n_ in nkcs:
                Pu.append(Pu[-1] + 2 * n_ + 2)
                Cu.append(Cu[-1] + n_ // 2)
                Du.append(Du[-1] + 5)

            spos = {}

            def attn_pe(t):
                seen = {}
                pn = [P0]

                def w(sem, val):
                    if val > seen.get(sem, -1):
                        t.wait_ge(S.sems[sem].h, val)
                        seen[sem] = val

                def inc(i):
                    i.then_inc(pe_s.h, 1)
                    pn[0] += 1
                    return pn[0]
                for ui, (h, gq) in enumerate(us):
                    n_ = nkcs[ui]
                    qoff = h * T + gq * CH
                    idx = h * 8 + gq
                    if ui > 0:
                        w("dv", Du[ui] - 1)
                    inc(t.matmul(psD[:, 0:1], ones_f[0:1, :],
                                 arr_sb[0:1, idx:idx + 1], start=True,
                                 stop=True))

                    def scores(kc):
                        ps = psE if kc % 2 == 0 else psF
                        koff = (h * T if l == 0 else 0) + kc * 128
                        diag = kc >= 4 * gq
                        i = t.matmul(ps[:, 0:512], kfm_sb[:, koff:koff + 128],
                                     qfm_sb[:, qoff:qoff + 512],
                                     start=True, stop=not diag)
                        if diag:
                            kcr = kc % 4
                            i = t.matmul(ps[:, 0:512], ident_bf[:],
                                         maskT_sb[:, kcr * 512:(kcr + 1) * 512],
                                         start=False, stop=True)
                        spos[(ui, kc)] = inc(i)

                    for kc in range(min(2, n_)):
                        scores(kc)
                    for j in range(n_ // 2):
                        w("sc", Cu[ui] + j + 1)
                        for kc in (2 * j, 2 * j + 1):
                            if kc == 0 and ui > 0:
                                w("dv", Du[ui] - 2)
                            voff = kc * 512 + (h * 128 if l == 0 else 0)
                            t.matmul(psC[:, 0:512], vtm_sb[:, voff:voff + 128],
                                     pbt_sb[kc], start=(kc == 0),
                                     stop=(kc == n_ - 1))
                            inc(t.matmul(pdn[0:1, 0:512], onesc_b[:, 0:1],
                                         pbt_sb[kc], start=(kc == 0),
                                         stop=(kc == n_ - 1)))
                        for kc in (2 * j + 2, 2 * j + 3):
                            if kc < n_:
                                scores(kc)
                    w("dv", Du[ui] + 3)
                    for qt in range(4):
                        t.transpose(pbA[:, qt * 128:(qt + 1) * 128],
                                    av_sb[:, qt * 128:(qt + 1) * 128],
                                    ident_bf[:])
                    for qt in range(4):
                        i = t.transpose(psD[:, qt:qt + 1],
                                        dn_sb[0:1, qt * 128:(qt + 1) * 128],
                                        ident_f32[0:1, 0:1])
                    inc(i)
                    assert pn[0] == Pu[ui + 1], (pn[0], ui)
            S.raw("pe", attn_pe)

            def attn_sc(s):
                seen = {}

                def w(sem, val):
                    if val > seen.get(sem, -1):
                        s.wait_ge(S.sems[sem].h, val)
                        seen[sem] = val
                for ui, (h, gq) in enumerate(us):
                    n_ = nkcs[ui]
                    for j in range(n_ // 2):
                        w("pe", spos[(ui, 2 * j + 1)])
                        if j == 0:
                            w("dv", Du[ui] + 1)
                        slot0 = (2 * j) % 4
                        s.activation(pbt_sb.t[:, slot0 * CH:(slot0 + 2) * CH],
                                     psEF[:, 0:1024], AF.Exp,
                                     bias=nB_sb[:, 0:1]).then_inc(sc_s.h, 1)
            S.raw("sc", attn_sc)

            def attn_dv(v):
                seen = {}

                def w(sem, val):
                    if val > seen.get(sem, -1):
                        v.wait_ge(S.sems[sem].h, val)
                        seen[sem] = val
                for ui, (h, gq) in enumerate(us):
                    n_ = nkcs[ui]
                    w("pe", Pu[ui] + 1)
                    if ui > 0:
                        w("sc", Cu[ui])
                    v.tensor_copy(nB_sb[:], psD[:, 0:1]).then_inc(dv_s.h, 1)
                    w("pe", Pu[ui + 1] - 1)
                    v.tensor_copy(av_sb[:], psC[:, 0:512]).then_inc(dv_s.h, 1)
                    v.reciprocal(dn_sb[0:1, :],
                                 pdn[0:1, 0:512]).then_inc(dv_s.h, 1)
                    v.wait_ge(dv_s.h, Du[ui] + 3)
                    w("pe", Pu[ui + 1])
                    v.tensor_copy(rl_sb[:, 0:4], psD[:, 0:4]).then_inc(dv_s.h, 1)
                    v.wait_ge(dv_s.h, Du[ui] + 4)
                    for qt in range(4):
                        doff = (gq * 4 + qt) * 512 + h * 128
                        i = v.tensor_scalar(attn_sb[:, doff:doff + 128],
                                            pbA[:, qt * 128:(qt + 1) * 128],
                                            rl_sb[:, qt:qt + 1], None,
                                            ALU.mult)
                    i.then_inc(dv_s.h, 1)
            S.raw("dv", attn_dv)
            pe_s.n, sc_s.n, dv_s.n = Pu[-1], Cu[-1], Du[-1]

            if want(f"attn_l{l}"):
                tp = mk_tap(f"attn_l{l}", [128, 32 * 512], BF16)
                S.sy(lambda s, dma: dma(out=tp, in_=attn_sb[:]))

            P0, D0 = pe_s.n, dv_s.n
            yts = [(h, fb) for h in range(LH) for fb in range(8)]

            def ytr_pe(t, P0=P0, D0=D0):
                for ui, (h, fb) in enumerate(yts):
                    if ui >= 1:
                        t.wait_ge(dv_s.h, D0 + ui)
                    for j in range(4):
                        tc = fb * 4 + j
                        i = t.transpose(pbA[:, j * 128:(j + 1) * 128],
                                        attn_sb[:, tc * 512 + h * 128:
                                                tc * 512 + (h + 1) * 128],
                                        ident_bf[:])
                    i.then_inc(pe_s.h, 1)
            S.raw("pe", ytr_pe)

            def ytr_dv(v, P0=P0, D0=D0):
                for ui, (h, fb) in enumerate(yts):
                    v.wait_ge(pe_s.h, P0 + ui + 1)
                    v.tensor_copy(kfm_sb[:, h * T + fb * 512:h * T + (fb + 1) * 512],
                                  pbA[:, 0:512]).then_inc(dv_s.h, 1)
            S.raw("dv", ytr_dv)
            pe_s.n = P0 + 32
            dv_s.n = D0 + 32

            S.gp(lambda g, dma, coll: dma(
                out=wst_buf.t[:, 0:4 * D].rearrange("p (f c) -> p f c", f=4),
                in_=ins[f"wo_{l}"].rearrange("(f p) c -> p f c", p=128)))
            P0, D0, Y0, G0 = pe_s.n, dv_s.n, sy_s.n, gp_s.n

            def op_pe(t, P0=P0, D0=D0, G0=G0):
                t.wait_ge(gp_s.h, G0)
                for tt in range(32):
                    for ob in range(4):
                        s = tt * 4 + ob
                        if s >= 2:
                            t.wait_ge(dv_s.h, D0 + s - 1)
                        ps = psA if s % 2 == 0 else psB
                        for hh in range(4):
                            i = t.matmul(ps[:, 0:512],
                                         kfm_sb[:, hh * T + tt * 128:
                                                hh * T + (tt + 1) * 128],
                                         wst_buf.t[:, hh * D + ob * 512:
                                                   hh * D + (ob + 1) * 512],
                                         start=(hh == 0), stop=(hh == 3))
                        i.then_inc(pe_s.h, 1)
            S.raw("pe", op_pe)

            def op_dv(v, P0=P0, Y0=Y0):
                for tt in range(32):
                    for ob in range(4):
                        s = tt * 4 + ob
                        v.wait_ge(pe_s.h, P0 + s + 1)
                        if tt >= 1 and ob == 0:
                            v.wait_ge(sy_s.h, Y0 + 16 * tt)
                        ps = psA if s % 2 == 0 else psB
                        v.tensor_copy(hbt_sb[:, ob * 512:(ob + 1) * 512],
                                      ps[:, 0:512]).then_inc(dv_s.h, 1)
            S.raw("dv", op_dv)

            def op_sy(s2, dma, D0=D0):
                for tt in range(32):
                    s2.wait_ge(dv_s.h, D0 + 4 * (tt + 1))
                    g_ = tt // 4
                    ro = owner(g_) * 1024 + lidx(g_) * 512 + (tt % 4) * 128
                    dma(out=rs_in[ro:ro + 128, :], in_=hbt_sb[:])
            S.raw("sy", op_sy)
            pe_s.n = P0 + 128
            dv_s.n = D0 + 128
            sy_s.n = Y0 + 16 * 32

            def rsc(g, dma, coll):
                coll("ReduceScatter", ALU.add, replica_groups=RG_BATCH,
                     ins=[rs_in], outs=[rs_out])
                g.wait_ge(S.sems["cc"].h, S.sems["cc"].n)
            S.gp(rsc)

            S.sy(lambda s, dma: dma(out=stg_sb[0:1, :], in_=ins[f"bo_{l}"]))
            bcast_to(bo_t, stg_sb, D)
            load_mod_row(l, 2, transform_g=True)
            bcast_to(ga_t, stg_sb, D)
            P0, D0, Y0 = pe_s.n, dv_s.n, sy_s.n

            def ae_sy(s2, dma):
                for t_i in range(8):
                    if t_i >= 1:
                        s2.wait_ge(dv_s.h, D0 + 3 * t_i)
                        s2.wait_ge(sy_s.h, Y0 + 48 * t_i)
                    dma(out=xt_sb[:], in_=x_cur[t_i * 128:(t_i + 1) * 128, :])
                    dma(out=hbt_sb[:], in_=rs_out[t_i * 128:(t_i + 1) * 128, :])
                    s2.wait_ge(dv_s.h, D0 + 3 * (t_i + 1))
                    dma(out=x_cur[t_i * 128:(t_i + 1) * 128, :], in_=xt_sb[:])
            S.raw("sy", ae_sy)

            def ae_dv(v):
                n = D0
                for t_i in range(8):
                    def g(i):
                        nonlocal n
                        n += 1
                        i.then_inc(dv_s.h, 1)
                        v.wait_ge(dv_s.h, n)
                    v.wait_ge(sy_s.h, Y0 + 48 * t_i + 32)
                    g(v.tensor_tensor(zt_sb[:], hbt_sb[:], bo_t[:], ALU.add))
                    g(v.tensor_tensor(zt_sb[:], zt_sb[:], ga_t[:], ALU.mult))
                    g(v.tensor_tensor(xt_sb[:], xt_sb[:], zt_sb[:], ALU.add))
            S.raw("dv", ae_dv)
            dv_s.n = D0 + 24
            sy_s.n = Y0 + 48 * 8
            if want(f"xattn_l{l}"):
                tp = mk_tap(f"xattn_l{l}", [TOK, D], F32)
                S.sy(lambda s, dma: dma(out=tp, in_=x_cur))

            load_mod_row(l, 4, plus1=True)
            bcast_to(ca_t, stg_sb, D)
            load_mod_row(l, 3)
            bcast_to(sa_t, stg_sb, D)
            ln_phase("ffn", l)
            if l == 0:
                load_mod_row(l, 5, transform_g=True)
                bcast_to(gb_t, stg_sb, D)
            if want(f"hffn_l{l}"):
                tp = mk_tap(f"hffn_l{l}", [128, 4 * D], F32)
                S.sy(lambda s, dma: dma(out=tp, in_=h_sb1[:]))

            if l == 0:
                ffn0()
            else:
                ffn1()
            if want(f"xout_l{l}"):
                tp = mk_tap(f"xout_l{l}", [TOK, D], F32)
                S.sy(lambda s, dma: dma(out=tp, in_=x_cur))

        def ffn0():
            pe_s, sc_s, dv_s, sy_s = (S.sems[k] for k in ("pe", "sc", "dv", "sy"))
            S.sy(lambda s, dma: dma(
                out=rsm2_sb[:].rearrange("p (f c) -> p f c", f=16),
                in_=ins["rsmall"].rearrange("(f p) c -> p f c", p=128)))
            P0, C0, D0, Y0 = pe_s.n, sc_s.n, dv_s.n, sy_s.n
            PET = 4 + 1 + 1 + 4
            SCT = 2
            DVT0 = 43
            SYT = 32

            def f0_sy(s2, dma):
                for t_i in range(8):
                    if t_i >= 1:
                        s2.wait_ge(dv_s.h, D0 + DVT0 * t_i)
                        s2.wait_ge(sy_s.h, Y0 + SYT * t_i)
                    dma(out=xt_sb[:], in_=x_cur[t_i * 128:(t_i + 1) * 128, :])
                    s2.wait_ge(dv_s.h, D0 + DVT0 * (t_i + 1))
                    dma(out=x_cur[t_i * 128:(t_i + 1) * 128, :], in_=xt_sb[:])
            S.raw("sy", f0_sy)

            def f0_sc(s):
                for t_i in range(8):
                    hsl_ = h_sb1 if t_i < 4 else h_sb2
                    h_ap = hsl_[:, (t_i % 4) * D:(t_i % 4) * D + D]
                    if t_i >= 1:
                        s.wait_ge(dv_s.h, D0 + DVT0 * (t_i - 1) + 10)
                    s.activation(zt_sb[:], h_ap, AF.Square,
                                 accum_out=stat_sb[:, 9:10]).then_inc(sc_s.h, 1)
                    if t_i >= 1:
                        s.wait_ge(dv_s.h, D0 + DVT0 * (t_i - 1) + 17)
                    s.wait_ge(dv_s.h, D0 + DVT0 * t_i + 15)
                    s.activation(g8d_sb[:], g8a_sb[:],
                                 AF.Sigmoid).then_inc(sc_s.h, 1)
            S.raw("sc", f0_sc)

            def f0_pe(t):
                for t_i in range(8):
                    hsl_ = h_sb1 if t_i < 4 else h_sb2
                    h_ap = hsl_[:, (t_i % 4) * D:(t_i % 4) * D + D]
                    for fb in range(4):
                        if t_i >= 1 and fb == 0:
                            t.wait_ge(dv_s.h, D0 + DVT0 * (t_i - 1) + 43)
                        if fb >= 1:
                            t.wait_ge(dv_s.h, D0 + DVT0 * t_i + fb + 1)
                        for j in range(4):
                            i = t.transpose(psD[:, j * 128:(j + 1) * 128],
                                            h_ap[:, (fb * 4 + j) * 128:
                                                 (fb * 4 + j + 1) * 128],
                                            ident_f32[:])
                        i.then_inc(pe_s.h, 1)
                    t.wait_ge(dv_s.h, D0 + DVT0 * t_i + 5)
                    if t_i >= 1:
                        t.wait_ge(dv_s.h, D0 + DVT0 * (t_i - 1) + 6)
                    for fc in range(16):
                        i = t.matmul(psE[:, 0:136],
                                     hfm32_t[:, fc * 128:(fc + 1) * 128],
                                     rsm2_sb[:, fc * 136:(fc + 1) * 136],
                                     start=(fc == 0), stop=(fc == 15))
                    i.then_inc(pe_s.h, 1)
                    t.wait_ge(dv_s.h, D0 + DVT0 * t_i + 25)
                    t.transpose(pbA[0:64, 0:128], smlb[:, 0:64], ident_bf[:])
                    t.transpose(pbA[0:8, 128:256], mkb_sb[:, 0:8],
                                ident_bf[:]).then_inc(pe_s.h, 1)
                    t.wait_ge(dv_s.h, D0 + DVT0 * t_i + 27)
                    for ob in range(4):
                        ps = psA if ob % 2 == 0 else psB
                        if t_i + ob > 0:
                            if ob >= 2:
                                t.wait_ge(dv_s.h, D0 + DVT0 * t_i + 27 + 4 * (ob - 1))
                            elif t_i >= 1:
                                t.wait_ge(dv_s.h, D0 + DVT0 * (t_i - 1) + 27 + 4 * (ob + 3))
                        t.matmul(ps[:, 0:512], fmm_sb[0:64, :],
                                 dgnt_sb[0:64, ob * 512:(ob + 1) * 512],
                                 start=True, stop=False)
                        t.matmul(ps[:, 0:512], fmk_sb[0:8, :],
                                 biasw_sb[0:8, ob * 512:(ob + 1) * 512],
                                 start=False, stop=True).then_inc(pe_s.h, 1)
            S.raw("pe", f0_pe)

            def f0_dv(v):
                n = D0
                for t_i in range(8):
                    hsl_ = h_sb1 if t_i < 4 else h_sb2
                    h_ap = hsl_[:, (t_i % 4) * D:(t_i % 4) * D + D]

                    def g(i):
                        nonlocal n
                        n += 1
                        i.then_inc(dv_s.h, 1)
                        v.wait_ge(dv_s.h, n)
                    v.wait_ge(sy_s.h, Y0 + SYT * t_i + 16)
                    g(v.engine_nop())
                    for fb in range(4):
                        v.wait_ge(pe_s.h, P0 + PET * t_i + fb + 1)
                        g(v.tensor_copy(hfm32_t[:, fb * 512:(fb + 1) * 512],
                                        psD[:, 0:512]))
                    v.wait_ge(pe_s.h, P0 + PET * t_i + 5)
                    g(v.tensor_copy(sml[:, 0:136], psE[:, 0:136]))
                    v.wait_ge(sc_s.h, C0 + SCT * t_i + 1)
                    g(v.tensor_tensor(gsc_sb[:], sml[:, 0:64], c1b_sb[:],
                                      ALU.subtract))
                    g(v.tensor_tensor(gsc_sb[:], gsc_sb[:], gsc_sb[:],
                                      ALU.mult))
                    g(v.tensor_reduce(g8a_sb[:],
                                      gsc_sb[:].rearrange("p (a b) -> p a b", b=8),
                                      AX, ALU.add))
                    g(v.tensor_scalar(g8b_sb[:], sml[:, 64:72], -2.0,
                                      stat_sb[:, 9:10], ALU.mult, ALU.add))
                    g(v.tensor_tensor(g8b_sb[:], g8b_sb[:], mtmb_sb[:],
                                      ALU.add))
                    g(v.reciprocal(g8c_sb[:], g8b_sb[:]))
                    g(v.tensor_tensor(g8a_sb[:], g8a_sb[:], g8c_sb[:],
                                      ALU.mult))
                    g(v.tensor_tensor(g8a_sb[:], g8a_sb[:], impb[:, 0:8],
                                      ALU.subtract))
                    g(v.tensor_tensor(g8a_sb[:], g8a_sb[:], impb[:, 8:16],
                                      ALU.mult))
                    v.wait_ge(sc_s.h, C0 + SCT * t_i + 2)
                    g(v.reduce_max(stat_sb[:, 10:11], g8d_sb[:], AX))
                    g(v.tensor_single_scalar(g8a_sb[:], g8d_sb[:],
                                             stat_sb[:, 10:11], ALU.is_equal))
                    g(v.tensor_tensor(g8a_sb[:], g8a_sb[:], m8_sb[:],
                                      ALU.mult))
                    g(v.tensor_scalar(g8a_sb[:], g8a_sb[:], 8.0, None,
                                      ALU.add))
                    g(v.tensor_reduce(stat_sb[:, 11:12], g8a_sb[:], AX,
                                      ALU.min))
                    g(v.tensor_single_scalar(g8a_sb[:], iota8_sb[:],
                                             stat_sb[:, 11:12], ALU.is_equal))
                    g(v.tensor_tensor(gsc_sb[:], sml[:, 72:136], c3b_sb[:],
                                      ALU.subtract))
                    g(v.tensor_tensor(gsc_sb[:].rearrange("p (a b) -> p a b", b=8),
                                      gsc_sb[:].rearrange("p (a b) -> p a b", b=8),
                                      g8a_sb[:].to_broadcast([128, 8, 8]),
                                      ALU.mult))
                    if t_i >= 1:
                        v.wait_ge(pe_s.h, P0 + PET * (t_i - 1) + 6)
                    g(v.tensor_copy(smlb[:, 0:64], gsc_sb[:]))
                    g(v.tensor_copy(mkb_sb[:], g8a_sb[:]))
                    v.wait_ge(pe_s.h, P0 + PET * t_i + 6)
                    g(v.tensor_copy(fmm_sb[:], pbA[0:64, 0:128]))
                    g(v.tensor_copy(fmk_sb[:], pbA[0:8, 128:256]))
                    for ob in range(4):
                        ps = psA if ob % 2 == 0 else psB
                        v.wait_ge(pe_s.h, P0 + PET * t_i + 6 + ob + 1)
                        hsl2 = h_ap[:, ob * 512:(ob + 1) * 512]
                        g(v.tensor_tensor(s1_t[:], ps[:, 0:512], hsl2,
                                          ALU.subtract))
                        g(v.scalar_tensor_tensor(s1_t[:], s1_t[:],
                                                 stat_sb[:, 10:11], hsl2,
                                                 ALU.mult, ALU.add))
                        g(v.tensor_tensor(s1_t[:], s1_t[:],
                                          gb_t[:, ob * 512:(ob + 1) * 512],
                                          ALU.mult))
                        g(v.tensor_tensor(xt_sb[:, ob * 512:(ob + 1) * 512],
                                          xt_sb[:, ob * 512:(ob + 1) * 512],
                                          s1_t[:], ALU.add))
                    assert n == D0 + DVT0 * (t_i + 1), (n, t_i)
            S.raw("dv", f0_dv)
            pe_s.n = P0 + PET * 8
            sc_s.n = C0 + SCT * 8
            dv_s.n = D0 + DVT0 * 8
            sy_s.n = Y0 + SYT * 8

        def ffn1():
            for t_i in range(8):
                hsl_ = h_sb1 if t_i < 4 else h_sb2
                off = (t_i % 4) * D
                h_ap = hsl_[:, off:off + D]
                S.sc(lambda s, h_ap=h_ap: s.activation(
                    zt_sb[:], h_ap, AF.Square, accum_out=stat_sb[:, 1:2]))
                S.dv(lambda v: v.tensor_scalar(stat_sb[:, 2:3],
                                               stat_sb[:, 1:2], 1.0 / D, None,
                                               ALU.mult))
                S.sc(lambda s: s.activation(stat_sb[:, 3:4], stat_sb[:, 2:3],
                                            AF.Sqrt, bias=epsbt[:, 0:1]))
                S.dv(lambda v, t_i=t_i: v.reciprocal(
                    stat_sb[:, 8 + t_i:9 + t_i], stat_sb[:, 3:4]))
            for br, (gname, dstv) in enumerate((("g1", xq1_sb), ("g2", xq2_sb))):
                S.sy(lambda s, dma, gname=gname: dma(out=stg_sb[0:1, :],
                                                     in_=ins[gname]))
                bcast_to(g1b_sb, stg_sb, D)
                for t_i in range(8):
                    hsl_ = h_sb1 if t_i < 4 else h_sb2
                    off = (t_i % 4) * D
                    h_ap = hsl_[:, off:off + D]
                    S.sc(lambda s, h_ap=h_ap, t_i=t_i: s.activation(
                        zt_sb[:], h_ap, AF.Identity,
                        scale=stat_sb[:, 8 + t_i:9 + t_i]))

                    def qnt(v):
                        v.tensor_tensor(xt_sb[:], zt_sb[:], g1b_sb[:], ALU.mult)
                        v.tensor_reduce(stat_sb[:, 5:6], xt_sb[:], AX, ALU.max,
                                        apply_absolute_value=True)
                        v.tensor_single_scalar(stat_sb[:, 5:6],
                                               stat_sb[:, 5:6], 1e-4, ALU.max)
                        v.reciprocal(stat_sb[:, 6:7], stat_sb[:, 5:6])
                        v.tensor_scalar(stat_sb[:, 6:7], stat_sb[:, 6:7],
                                        127.0, None, ALU.mult)
                        v.tensor_scalar(stat_sb[:, 7:8], stat_sb[:, 5:6],
                                        1.0 / 127.0, None, ALU.mult)
                        v.tensor_scalar(xt_sb[:], xt_sb[:], stat_sb[:, 6:7],
                                        None, ALU.mult)
                        v.tensor_scalar(xt_sb[:], xt_sb[:], MAGIC, -MAGIC,
                                        ALU.add, ALU.add)
                        return v.tensor_scalar(xqst_sb[:], xt_sb[:],
                                               stat_sb[:, 7:8], None, ALU.mult)
                    S.dv(qnt)
                    for fb in range(4):
                        def trq(t, fb=fb):
                            for j in range(4):
                                i = t.transpose(
                                    pbA[:, j * 128:(j + 1) * 128],
                                    xqst_sb[:, (fb * 4 + j) * 128:
                                            (fb * 4 + j + 1) * 128],
                                    ident_bf[:])
                            return i
                        S.pe(trq)
                        S.dv(lambda v, fb=fb, dstv=dstv, t_i=t_i: v.tensor_copy(
                            dstv[:].rearrange("p (f k) -> p f k", f=16)
                            [:, fb * 4:(fb + 1) * 4, t_i * 128:(t_i + 1) * 128],
                            pbA[:].rearrange("p (a b) -> p a b", a=4)))
            pe_s, sc_s, dv_s, sy_s, gp_s = (S.sems[k] for k in
                                            ("pe", "sc", "dv", "sy", "gp"))
            P0, C0, D0, Y0, G0 = (pe_s.n, sc_s.n, dv_s.n, sy_s.n, gp_s.n)

            def w12_sy(s2, dma, P0=P0):
                s2.wait_ge(tern_sem.h, 1)
                for ob in range(16):
                    rk, oc = ob // 2, ob % 2
                    if ob >= 2:
                        s2.wait_ge(pe_s.h, P0 + 8 * (ob - 1))
                    dma(out=t1_buf[ob][:].rearrange("p (f c) -> p f c", f=16),
                        in_=tern_ag[rk * 6144:rk * 6144 + 2048,
                                    oc * 512:(oc + 1) * 512]
                        .rearrange("(f p) c -> p f c", p=128))
                    dma(out=t2_buf[ob][:].rearrange("p (f c) -> p f c", f=16),
                        in_=tern_ag[rk * 6144 + 2048:rk * 6144 + 4096,
                                    oc * 512:(oc + 1) * 512]
                        .rearrange("(f p) c -> p f c", p=128))
            S.raw("sy", w12_sy)

            def w12_pe(t, C0=C0, D0=D0, Y0=Y0):
                for ob in range(16):
                    t.wait_ge(sy_s.h, Y0 + 32 * (ob + 1))
                    for t_i in range(8):
                        s = ob * 8 + t_i
                        if s >= 1:
                            t.wait_ge(sc_s.h, C0 + s)
                            t.wait_ge(dv_s.h, D0 + s)
                        for fc in range(16):
                            t.matmul(psA[:, 0:512],
                                     xq1_sb[:, fc * 1024 + t_i * 128:
                                            fc * 1024 + (t_i + 1) * 128],
                                     t1_buf[ob][:, fc * 512:(fc + 1) * 512],
                                     start=(fc == 0), stop=(fc == 15))
                        for fc in range(16):
                            i = t.matmul(psB[:, 0:512],
                                         xq2_sb[:, fc * 1024 + t_i * 128:
                                                fc * 1024 + (t_i + 1) * 128],
                                         t2_buf[ob][:, fc * 512:(fc + 1) * 512],
                                         start=(fc == 0), stop=(fc == 15))
                        i.then_inc(pe_s.h, 1)
            S.raw("pe", w12_pe)

            def w12_sc(s, P0=P0, D0=D0):
                for ob in range(16):
                    for t_i in range(8):
                        s_i = ob * 8 + t_i
                        s.wait_ge(pe_s.h, P0 + s_i + 1)
                        if s_i >= 1:
                            s.wait_ge(dv_s.h, D0 + s_i)
                        s.activation(s1_t[:], psA[:, 0:512], AF.Silu,
                                     scale=a_bc[:, 3:4]).then_inc(sc_s.h, 1)
            S.raw("sc", w12_sc)

            def w12_dv(v, C0=C0, G0=G0):
                for ob in range(16):
                    for t_i in range(8):
                        s_i = ob * 8 + t_i
                        v.wait_ge(sc_s.h, C0 + s_i + 1)
                        if s_i >= 2:
                            v.wait_ge(gp_s.h, G0 + 16 * (s_i - 1))
                        v.scalar_tensor_tensor(
                            h2st[:, 0:512] if s_i % 2 == 0 else xqst_sb[:, 0:512],
                            s1_t[:], a_bc[:, 4:5], psB[:, 0:512],
                            ALU.mult, ALU.mult).then_inc(dv_s.h, 1)
            S.raw("dv", w12_dv)

            def w12_gp(g, dma, coll, D0=D0):
                for ob in range(16):
                    for t_i in range(8):
                        s_i = ob * 8 + t_i
                        g.wait_ge(dv_s.h, D0 + s_i + 1)
                        srcb = h2st[:, 0:512] if s_i % 2 == 0 else xqst_sb[:, 0:512]
                        dma(out=h2_dram[t_i * 128:(t_i + 1) * 128,
                                        ob * 512:(ob + 1) * 512], in_=srcb)
            S.raw("pl", w12_gp)
            pe_s.n = P0 + 128
            sc_s.n = C0 + 128
            dv_s.n = D0 + 128
            sy_s.n = Y0 + 32 * 16
            gp_s.n = G0 + 16 * 128
            for j in range(4):
                S.sy(lambda s, dma, j=j: dma(
                    out=stg_sb[0:1, :], in_=ins["g3"][0:1, j * D:(j + 1) * D]))
                bcast_to(g3b_sb[:, j * D:(j + 1) * D], stg_sb, D)
            P0, C0, D0, Y0 = pe_s.n, sc_s.n, dv_s.n, sy_s.n
            DVT = 11 + 16

            def _lpos(t):
                return t + 16 * max(t - 1, 0) + 1

            def _spos(t, fb):
                return min(t + 2, 8) + 16 * t + fb + 1

            def q3_sy(s2, dma):
                def store(t_i, fb):
                    s2.wait_ge(dv_s.h, D0 + DVT * t_i + 11 + fb + 1)
                    src_ = q3st_sb[:] if fb % 2 == 0 else q3st2_sb[:]
                    dma(out=h2qf_dram[fb * 512:(fb + 1) * 512,
                                      t_i * 128:(t_i + 1) * 128]
                        .rearrange("(a p) c -> p a c", p=128),
                        in_=src_.rearrange("p (a c) -> p a c", a=4))
                for t_i in range(8):
                    if t_i >= 1:
                        s2.wait_ge(sc_s.h, C0 + 4 * (t_i - 1) + 4)
                    dma(out=h2_t[:], in_=h2_dram[t_i * 128:(t_i + 1) * 128, :])
                    if t_i >= 1:
                        for fb in range(16):
                            store(t_i - 1, fb)
                for fb in range(16):
                    store(7, fb)
            S.raw("sy", q3_sy)

            def q3_sc(s):
                n = C0
                for t_i in range(8):
                    def g(i):
                        nonlocal n
                        n += 1
                        i.then_inc(sc_s.h, 1)
                        s.wait_ge(sc_s.h, n)
                    s.wait_ge(sy_s.h, Y0 + 16 * _lpos(t_i))
                    if t_i >= 1:
                        s.wait_ge(dv_s.h, D0 + DVT * t_i - 16)
                        s.wait_ge(pe_s.h, P0 + 16 * t_i)
                    g(s.activation(n3_t[:], h2_t[:], AF.Square,
                                   accum_out=stat_sb[:, 1:2]))
                    s.wait_ge(dv_s.h, D0 + DVT * t_i + 1)
                    g(s.activation(stat_sb[:, 3:4], stat_sb[:, 2:3], AF.Ln,
                                   bias=epsbt[:, 0:1]))
                    g(s.activation(stat_sb[:, 4:5], stat_sb[:, 3:4], AF.Exp,
                                   scale=-0.5))
                    g(s.activation(n3_t[:], h2_t[:], AF.Identity,
                                   scale=stat_sb[:, 4:5]))
            S.raw("sc", q3_sc)

            def q3_dv(v):
                n = D0
                for t_i in range(8):
                    def g(i):
                        nonlocal n
                        n += 1
                        i.then_inc(dv_s.h, 1)
                        v.wait_ge(dv_s.h, n)
                    v.wait_ge(sc_s.h, C0 + 4 * t_i + 1)
                    g(v.tensor_scalar(stat_sb[:, 2:3], stat_sb[:, 1:2],
                                      1.0 / DFF, None, ALU.mult))
                    v.wait_ge(sc_s.h, C0 + 4 * t_i + 4)
                    g(v.tensor_tensor(n3_t[:], n3_t[:], g3b_sb[:], ALU.mult))
                    g(v.tensor_reduce(stat_sb[:, 5:6], n3_t[:], AX, ALU.max,
                                      apply_absolute_value=True))
                    g(v.tensor_single_scalar(stat_sb[:, 5:6], stat_sb[:, 5:6],
                                             1e-4, ALU.max))
                    g(v.reciprocal(stat_sb[:, 6:7], stat_sb[:, 5:6]))
                    g(v.tensor_scalar(stat_sb[:, 6:7], stat_sb[:, 6:7], 127.0,
                                      None, ALU.mult))
                    g(v.tensor_scalar(stat_sb[:, 7:8], stat_sb[:, 5:6],
                                      1.0 / 127.0, None, ALU.mult))
                    g(v.tensor_scalar(n3_t[:], n3_t[:], stat_sb[:, 6:7], None,
                                      ALU.mult))
                    g(v.tensor_scalar(n3_t[:], n3_t[:], MAGIC, -MAGIC, ALU.add,
                                      ALU.add))
                    g(v.tensor_scalar(n3_t[:], n3_t[:], stat_sb[:, 7:8], None,
                                      ALU.mult))
                    g(v.engine_nop())
                    for fb in range(16):
                        v.wait_ge(pe_s.h, P0 + 16 * t_i + fb + 1)
                        if fb >= 2:
                            v.wait_ge(sy_s.h, Y0 + 16 * _spos(t_i, fb - 2))
                        elif t_i >= 1:
                            v.wait_ge(sy_s.h, Y0 + 16 * _spos(t_i - 1, 14 + fb))
                        dst_ = q3st_sb[:] if fb % 2 == 0 else q3st2_sb[:]
                        i = v.tensor_copy(dst_, pbA[:, 0:512])
                        i.then_inc(dv_s.h, 1)
                        n += 1
            S.raw("dv", q3_dv)

            def q3_pe(t):
                for t_i in range(8):
                    t.wait_ge(dv_s.h, D0 + DVT * t_i + 11)
                    for fb in range(16):
                        if fb >= 1:
                            t.wait_ge(dv_s.h, D0 + DVT * t_i + 11 + fb)
                        elif t_i >= 1:
                            t.wait_ge(dv_s.h, D0 + DVT * (t_i - 1) + 11 + 16)
                        for j in range(4):
                            i = t.transpose(pbA[:, j * 128:(j + 1) * 128],
                                            n3_t[:, (fb * 4 + j) * 128:
                                                 (fb * 4 + j + 1) * 128],
                                            ident_bf[:])
                        i.then_inc(pe_s.h, 1)
            S.raw("pe", q3_pe)
            pe_s.n = P0 + 128
            sc_s.n = C0 + 32
            dv_s.n = D0 + DVT * 8
            sy_s.n = Y0 + 16 * (8 + 128)

            load_mod_row(1, 5, transform_g=True)
            bcast_to(gb_t, stg_sb, D)
            for db in range(4):
                for tp0, tcnt in ((0, 6), (6, 2)):
                    P0, Y0 = pe_s.n, sy_s.n
                    nst = 4 * tcnt
                    t5 = tern_ag.rearrange(
                        "(rk blk j p two) c -> rk blk j p two c",
                        rk=8, blk=3, j=8, two=2)

                    def w3_sy(s2, dma, db=db, tp0=tp0, tcnt=tcnt, P0=P0,
                              nst=nst, t5=t5):
                        for ocg in range(4):
                            if ocg >= 2:
                                s2.wait_ge(pe_s.h, P0 + tcnt * (ocg - 1))
                            for rkr in range(2):
                                rk = ocg * 2 + rkr
                                sl = t5[rk, 2, :, :, db // 2,
                                        (db % 2) * 512:(db % 2) * 512 + 512]
                                dma(out=t3_buf[ocg]
                                    [:, rkr * 8 * 512:(rkr + 1) * 8 * 512]
                                    .rearrange("p (j c) -> p j c", j=8),
                                    in_=sl.rearrange("j p c -> p j c"))
                            for ti in range(tp0, tp0 + tcnt):
                                s_i = ocg * tcnt + (ti - tp0)
                                if s_i >= 2:
                                    s2.wait_ge(pe_s.h, P0 + s_i - 1)
                                dma(out=h2qf_buf[ti][:, 0:16 * 128]
                                    .rearrange("p (f c) -> p f c", f=16),
                                    in_=h2qf_dram[ocg * 2048:(ocg + 1) * 2048,
                                                  ti * 128:(ti + 1) * 128]
                                    .rearrange("(f p) c -> p f c", p=128))
                    S.raw("sy", w3_sy)

                    def w3_pe(t, db=db, tp0=tp0, tcnt=tcnt, P0=P0, Y0=Y0):
                        for ocg in range(4):
                            for ti in range(tp0, tp0 + tcnt):
                                s_i = ocg * tcnt + (ti - tp0)
                                t.wait_ge(sy_s.h,
                                          Y0 + 32 * (ocg + 1) + 16 * (s_i + 1))
                                for i2 in range(16):
                                    i = t.matmul(
                                        pstile[ti - tp0][:, 0:512],
                                        h2qf_buf[ti][:, i2 * 128:(i2 + 1) * 128],
                                        t3_buf[ocg][:, i2 * 512:(i2 + 1) * 512],
                                        start=(ocg == 0 and i2 == 0),
                                        stop=(ocg == 3 and i2 == 15))
                                i.then_inc(pe_s.h, 1)
                    S.raw("pe", w3_pe)
                    pe_s.n = P0 + nst
                    sy_s.n = Y0 + 32 * 4 + 16 * nst
                    E0, EY0 = dv_s.n, sy_s.n

                    def w3e_sy(s2, dma, db=db, tp0=tp0, tcnt=tcnt, E0=E0,
                               EY0=EY0):
                        for k2, ti in enumerate(range(tp0, tp0 + tcnt)):
                            if k2 >= 1:
                                s2.wait_ge(sy_s.h, EY0 + 32 * k2)
                                s2.wait_ge(dv_s.h, E0 + 2 * k2)
                            dma(out=xst_sb[:],
                                in_=x_cur[ti * 128:(ti + 1) * 128,
                                          db * 512:(db + 1) * 512])
                            s2.wait_ge(dv_s.h, E0 + 2 * (k2 + 1))
                            dma(out=x_cur[ti * 128:(ti + 1) * 128,
                                          db * 512:(db + 1) * 512],
                                in_=xst_sb[:])
                    S.raw("sy", w3e_sy)

                    def w3e_dv(v, db=db, tp0=tp0, tcnt=tcnt, E0=E0, EY0=EY0):
                        n = E0
                        for k2, ti in enumerate(range(tp0, tp0 + tcnt)):
                            def g(i):
                                nonlocal n
                                n += 1
                                i.then_inc(dv_s.h, 1)
                                v.wait_ge(dv_s.h, n)
                            v.wait_ge(sy_s.h, EY0 + 32 * k2 + 16)
                            g(v.scalar_tensor_tensor(
                                s1_t[:], pstile[ti - tp0][:, 0:512],
                                a_bc[:, 5:6], gb_t[:, db * 512:(db + 1) * 512],
                                ALU.mult, ALU.mult))
                            g(v.tensor_tensor(xst_sb[:], xst_sb[:], s1_t[:],
                                              ALU.add))
                    S.raw("dv", w3e_dv)
                    dv_s.n = E0 + 2 * tcnt
                    sy_s.n = EY0 + 32 * tcnt

        layer(0)
        layer(1)

        S.sy(lambda s, dma: dma(out=y_out, in_=x_cur))
        S.final_wait()

    return nc, tap_list


_NC_CACHE = {}


def kernel(**inputs):
    taps = tuple(sorted(globals().get("DEBUG_TAPS", ())))
    if taps not in _NC_CACHE:
        _NC_CACHE[taps] = build_nc(taps)
    nc, tap_list = _NC_CACHE[taps]
    cst = _consts()
    in_maps = []
    for c in range(8):
        m = dict(cst)
        m.update(_prep_core(inputs, c))
        in_maps.append(m)
    res = run_bass_kernel_spmd(nc, in_maps, core_ids=list(range(8)))
    out = np.zeros((B, T, D), np.float32)
    for c in range(8):
        b, r = c // 4, c % 4
        y = np.asarray(res.results[c]["y_out"])
        out[b, r * CH:(r + 1) * CH] = y[0:512]
        out[b, (7 - r) * CH:(8 - r) * CH] = y[512:1024]
    if taps:
        kernel._tap_results = [
            {k: np.asarray(v) for k, v in r.items()} for r in res.results]
    return out


DEBUG_TAPS = ()


# version 12
# speedup vs baseline: 1.0952x; 1.0952x over previous
import numpy as np
import ml_dtypes
import concourse.bass as bass
import concourse.mybir as mybir
from concourse.bass_utils import run_bass_kernel_spmd

F32 = mybir.dt.float32
BF16 = mybir.dt.bfloat16
AF = mybir.ActivationFunctionType
ALU = mybir.AluOpType
AX = mybir.AxisListType.X

BFD = ml_dtypes.bfloat16
NEG = -1e30
MAGIC = 12582912.0
EPS_LN = 1e-5
EPS_BIT = 1e-8

B, T, D = 2, 4096, 2048
NCORE, GRP = 8, 4
CH = 512
TOK = 1024
HEADS, HD = 16, 128
LH = 4
DFF = 8192
SCL = float(HD) ** -0.5
RG_BATCH = [[0, 1, 2, 3], [4, 5, 6, 7]]
RG_ALL = [list(range(8))]

SHUF_PAIR = [1, 0, 3, 2, 5, 4, 7, 6, 9, 8, 11, 10, 13, 12, 15, 14,
             17, 16, 19, 18, 21, 20, 23, 22, 25, 24, 27, 26, 29, 28, 31, 30]


def owner(g):
    return g if g < 4 else 7 - g


def lidx(g):
    return 0 if g < 4 else 1


def _f32(a):
    return np.ascontiguousarray(np.asarray(a), dtype=np.float32)


def _bf(a):
    return np.ascontiguousarray(np.asarray(a, dtype=np.float32).astype(BFD))


def _consts():
    c = {}
    c["ident_bf"] = _bf(np.eye(128))
    c["ident_f32"] = _f32(np.eye(128))
    m = np.where(np.arange(CH)[:, None] <= np.arange(CH)[None, :], 0.0, NEG)
    c["maskT_bf"] = _bf(np.concatenate(
        [m[i * 128:(i + 1) * 128] for i in range(4)], axis=1))
    c["ones_f"] = _f32(np.ones((1, 128)))
    c["onesc_f"] = _f32(np.ones((128, 1)))
    c["onesc_b"] = _bf(np.ones((128, 1)))
    c["iota8"] = _f32(np.broadcast_to(np.arange(8, dtype=np.float32), (128, 8)))
    c["m8"] = _f32(np.broadcast_to(np.arange(8, dtype=np.float32) - 8.0, (128, 8)))
    c["eye64x8"] = _f32((np.arange(64)[:, None] // 8 == np.arange(8)[None, :]))
    c["eye8"] = _f32(np.eye(8))
    inv = 1.0 / (10000.0 ** (np.arange(0, HD, 2, dtype=np.float64) / HD))
    fr = np.arange(T, dtype=np.float64)[None, :] * inv[:, None]
    cfm = np.empty((HD, T))
    sfm = np.empty((HD, T))
    cfm[0::2] = np.cos(fr)
    cfm[1::2] = np.cos(fr)
    sfm[0::2] = -np.sin(fr)
    sfm[1::2] = np.sin(fr)
    c["cos_fm"] = _bf(cfm)
    c["sin_fm"] = _bf(sfm)
    return c


def _prep_core(inp, c):
    b, r = c // 4, c % 4
    d = {}
    x = _f32(inp["L0_attn_x"])
    d["x_in"] = _f32(np.concatenate(
        [x[b, r * CH:(r + 1) * CH], x[b, (7 - r) * CH:(8 - r) * CH]], axis=0))

    wq0 = np.transpose(_f32(inp["L0_attn_Wq"]), (1, 0, 2)).reshape(D, D)
    wk0 = np.transpose(_f32(inp["L0_attn_Wk"]), (1, 0, 2)).reshape(D, D)
    wv0 = np.transpose(_f32(inp["L0_attn_Wv"]), (1, 0, 2)).reshape(D, D)
    hsl = slice(4 * r * HD, (4 * r + 4) * HD)
    d["wq_0"] = _f32(wq0[:, hsl])
    d["wk_0"] = _f32(wk0[:, hsl])
    d["wv_0"] = _f32(wv0[:, hsl])
    d["bq_0"] = _f32(_f32(inp["L0_attn_bq"]).reshape(D)[hsl].reshape(4, 128).T)
    d["bk_0"] = _f32(_f32(inp["L0_attn_bk"]).reshape(D)[hsl].reshape(4, 128).T)
    d["bv_0"] = _f32(_f32(inp["L0_attn_bv"]).reshape(D)[hsl].reshape(1, 512))
    d["wo_0"] = _f32(_f32(inp["L0_attn_Wo"])[hsl, :])
    d["bo_0"] = _f32(inp["L0_attn_bo"]).reshape(1, D)

    wq1 = _f32(inp["L1_attn_Wq"]).T
    wk1 = _f32(inp["L1_attn_Wk"]).T
    wv1 = _f32(inp["L1_attn_Wv"]).T
    kvsl = slice(r * HD, (r + 1) * HD)
    d["wq_1"] = _f32(wq1[:, hsl])
    d["wk_1"] = _f32(wk1[:, kvsl])
    d["wv_1"] = _f32(wv1[:, kvsl])
    d["bq_1"] = _f32(_f32(inp["L1_attn_bq"])[hsl].reshape(4, 128).T)
    d["bk_1"] = _f32(_f32(inp["L1_attn_bk"])[kvsl].reshape(1, 128).T)
    d["bv_1"] = _f32(_f32(inp["L1_attn_bv"])[kvsl].reshape(1, 128))
    d["wo_1"] = _f32(_f32(inp["L1_attn_Wo"]).T[hsl, :])
    d["bo_1"] = _f32(inp["L1_attn_bo"]).reshape(1, D)

    for l in range(2):
        wm = _f32(inp[f"L{l}_wmod"])
        sl = np.concatenate([wm[:, m * D + r * 512:m * D + (r + 1) * 512]
                             for m in range(6)], axis=1)
        d[f"wmod_{l}"] = _f32(sl)
        d[f"lc_{l}"] = _f32(inp[f"L{l}_c"])[b:b + 1]

    sw = _f32(inp["L0_ffn_select_weight"])
    md = _f32(inp["L0_ffn_select_mean_diff"])
    lu = _f32(inp["L0_ffn_lora_update"])
    lg = _f32(inp["L0_ffn_lora_degen"])
    sw2 = np.transpose(sw, (1, 0, 2)).reshape(D, 64)
    lu2 = np.transpose(lu, (1, 0, 2)).reshape(D, 64)
    d["rsmall"] = _f32(np.concatenate([sw2, md.T, lu2], axis=1))
    d["dgnt"] = _bf(np.transpose(lg, (0, 2, 1)).reshape(64, D))
    d["biasw"] = _bf(inp["L0_ffn_bias_w"])
    d["debiaswT"] = _f32(_f32(inp["L0_ffn_debias_w"]).T)
    d["impc"] = _f32(inp["L0_ffn_imp_center"]).reshape(1, 8)
    d["imps"] = _f32(inp["L0_ffn_imp_slope"]).reshape(1, 8)

    osl = slice(c * 1024, (c + 1) * 1024)
    d["w1t_s"] = _f32(_f32(inp["L1_ffn_w1"])[osl, :].T)
    d["w2t_s"] = _f32(_f32(inp["L1_ffn_w2"])[osl, :].T)
    d["w3t_s"] = _f32(_f32(inp["L1_ffn_w3"])[:, osl].T)
    d["g1"] = _f32(inp["L1_ffn_g1"]).reshape(1, D)
    d["g2"] = _f32(inp["L1_ffn_g2"]).reshape(1, D)
    d["g3"] = _f32(inp["L1_ffn_g3"]).reshape(1, DFF)
    return d


class Sem:
    def __init__(self, nc, name):
        self.h = nc.alloc_semaphore(name)
        self.n = 0


_GUARD_OPS = ("tensor_reduce", "reduce_max", "reduce_sum", "reciprocal",
              "stream_shuffle")


class _Synced:
    def __init__(self, i):
        self.i = i


class VG:

    def __init__(self, eng, sem):
        self._e = eng
        self._s = sem

    def __getattr__(self, name):
        a = getattr(self._e, name)
        if not callable(a) or name.startswith("wait") or name.startswith("sem"):
            return a

        def w(*args, **kw):
            i = a(*args, **kw)
            if hasattr(i, "then_inc"):
                self._s.n += 1
                i.then_inc(self._s.h, 1)
                self._e.wait_ge(self._s.h, self._s.n)
                return _Synced(i)
            return i
        return w


class Ser:

    def __init__(self, nc, block):
        self.nc = nc
        self.block = block
        self.sems = {n: Sem(nc, n) for n in
                     ("sy", "gp", "cc", "pe", "dv", "sc", "pl")}
        self.seen = {n: {} for n in ("sy", "gp", "pe", "dv", "sc", "pl")}
        self.qsem = {"sy": "sy", "gp": "gp"}

    def _waits(self, ekey, eng):
        for n, s in self.sems.items():
            if n == ekey and ekey not in ("sy", "gp"):
                continue
            if s.n > self.seen[ekey].get(n, 0):
                eng.wait_ge(s.h, s.n)
                self.seen[ekey][n] = s.n

    def _body(self, ekey, blockfn, fn):
        def run(eng):
            self._waits(ekey, eng)
            s = self.sems[ekey]
            if ekey == "dv":
                last = fn(VG(eng, s))
            else:
                last = fn(eng)
            if isinstance(last, _Synced):
                eng.sem_inc(s.h, 1)
                s.n += 1
            elif last is not None:
                last.then_inc(s.h, 1)
                s.n += 1
        blockfn(run)

    def pe(self, fn):
        self._body("pe", self.block.tensor, fn)

    def dv(self, fn):
        self._body("dv", self.block.vector, fn)

    def sc(self, fn):
        self._body("sc", self.block.scalar, fn)

    def pl(self, fn):
        self._body("pl", self.block.gpsimd, fn)

    def sy(self, fn):
        def run(eng):
            self._waits("sy", eng)
            def dma(**kw):
                i = eng.dma_start(**kw)
                i.then_inc(self.sems["sy"].h, 16)
                self.sems["sy"].n += 16
                return i
            fn(eng, dma)
        self.block.sync(run)

    def gp(self, fn):
        def run(eng):
            self._waits("gp", eng)
            def dma(**kw):
                i = eng.dma_start(**kw)
                i.then_inc(self.sems["gp"].h, 16)
                self.sems["gp"].n += 16
                return i
            def coll(*a, **kw):
                i = eng.collective_compute(*a, **kw)
                i.then_inc(self.sems["cc"].h, 1)
                self.sems["cc"].n += 1
                return i
            fn(eng, dma, coll)
        self.block.gpsimd(run)

    def raw(self, ekey, fn):
        blockfn = {"pe": self.block.tensor, "dv": self.block.vector,
                   "sc": self.block.scalar, "pl": self.block.gpsimd,
                   "sy": self.block.sync}[ekey]

        def run(eng):
            self._waits(ekey, eng)
            if ekey in ("sy", "pl"):
                qs = "sy" if ekey == "sy" else "gp"

                def dma(**kw):
                    i = eng.dma_start(**kw)
                    i.then_inc(self.sems[qs].h, 16)
                    return i
                if ekey == "pl":
                    fn(eng, dma, None)
                else:
                    fn(eng, dma)
            else:
                fn(eng)
        blockfn(run)

    def final_wait(self):
        def run(eng):
            self._waits("sy", eng)
            eng.wait_ge(self.sems["sy"].h, self.sems["sy"].n)
        self.block.sync(run)



class DBuf:

    def __init__(self, tsr, n, width):
        self.t = tsr
        self.n = n
        self.w = width

    def __getitem__(self, i):
        return self.t[:, (i % self.n) * self.w:((i % self.n) + 1) * self.w]


def build_nc(taps=()):
    taps = set(taps)
    nc = bass.Bass("TRN2", target_bir_lowering=False,
                   detect_race_conditions=False)

    def din(name, shape, dt=F32):
        return nc.dram_tensor(name, list(shape), dt, kind="ExternalInput").ap()

    ins = {}
    ins["x_in"] = din("x_in", [TOK, D])
    y_out = nc.dram_tensor("y_out", [TOK, D], F32, kind="ExternalOutput").ap()
    for l in (0, 1):
        kw = 512 if l == 0 else 128
        ins[f"wq_{l}"] = din(f"wq_{l}", [D, 512])
        ins[f"wk_{l}"] = din(f"wk_{l}", [D, kw])
        ins[f"wv_{l}"] = din(f"wv_{l}", [D, kw])
        ins[f"bq_{l}"] = din(f"bq_{l}", [128, 4])
        ins[f"bk_{l}"] = din(f"bk_{l}", [128, kw // 128])
        ins[f"bv_{l}"] = din(f"bv_{l}", [1, kw])
        ins[f"wo_{l}"] = din(f"wo_{l}", [512, D])
        ins[f"bo_{l}"] = din(f"bo_{l}", [1, D])
        ins[f"wmod_{l}"] = din(f"wmod_{l}", [D, 3072])
        ins[f"lc_{l}"] = din(f"lc_{l}", [1, D])
    ins["rsmall"] = din("rsmall", [D, 136])
    ins["dgnt"] = din("dgnt", [64, D], BF16)
    ins["biasw"] = din("biasw", [8, D], BF16)
    ins["debiaswT"] = din("debiaswT", [D, 8])
    ins["impc"] = din("impc", [1, 8])
    ins["imps"] = din("imps", [1, 8])
    ins["w1t_s"] = din("w1t_s", [D, 1024])
    ins["w2t_s"] = din("w2t_s", [D, 1024])
    ins["w3t_s"] = din("w3t_s", [1024, D])
    ins["g1"] = din("g1", [1, D])
    ins["g2"] = din("g2", [1, D])
    ins["g3"] = din("g3", [1, DFF])
    for n, shape, dt in [("ident_bf", [128, 128], BF16),
                         ("ident_f32", [128, 128], F32),
                         ("maskT_bf", [128, 4 * CH], BF16),
                         ("ones_f", [1, 128], F32),
                         ("onesc_f", [128, 1], F32),
                         ("onesc_b", [128, 1], BF16),
                         ("iota8", [128, 8], F32), ("m8", [128, 8], F32),
                         ("eye64x8", [64, 8], F32), ("eye8", [8, 8], F32),
                         ("cos_fm", [HD, T], BF16), ("sin_fm", [HD, T], BF16)]:
        ins[n] = din(n, shape, dt)

    x_cur = nc.dram_tensor("x_cur", [TOK, D], F32).ap()
    hfm_in = nc.dram_tensor("hfm_in", [D, TOK], BF16).ap()
    hfm_ag = nc.dram_tensor("hfm_ag", [GRP * D, TOK], BF16).ap()
    rs_in = nc.dram_tensor("rs_in", [T, D], BF16).ap()
    rs_out = nc.dram_tensor("rs_out", [TOK, D], BF16).ap()
    mod_in = nc.dram_tensor("mod_in", [1, 6144], F32).ap()
    mod_ag = nc.dram_tensor("mod_ag", [4, 6144], F32).ap()
    st_in = nc.dram_tensor("st_in", [1, 4], F32).ap()
    st_ag = nc.dram_tensor("st_ag", [1, 4], F32).ap()
    tern_in = nc.dram_tensor("tern_in", [6144, 1024], BF16).ap()
    tern_ag = nc.dram_tensor("tern_ag", [NCORE * 6144, 1024], BF16).ap()
    h2_dram = nc.dram_tensor("h2_dram", [TOK, DFF], BF16).ap()
    h2qf_dram = nc.dram_tensor("h2qf_dram", [DFF, TOK], BF16).ap()

    ARENA = 211264
    _arena = nc.bump_sbuf(ARENA)
    BASE = _arena[0] if isinstance(_arena, tuple) else _arena

    def at(name, off, shape, dt):
        return nc.alloc_sbuf_tensor_at(name, list(shape), dt, offset=BASE + off)

    Ao, Bo, Co, Do, Eo, Fo = 0, 32768, 65536, 98304, 131072, 163840
    hfm_sb = at("hfm_sb", Ao, [128, 16 * TOK], BF16)
    qfm_sb = at("qfm_sb", Ao, [128, LH * T], BF16)
    attn_tm = at("attn_tm", Ao, [128, 8 * D], BF16)
    gb_t = at("gb_t", Ao, [128, D], F32)
    hfm32_t = at("hfm32_t", Ao + 8192, [128, 16 * 128], F32)
    lcs_sb = at("lcs_sb", Ao + 16384, [1, 2 * D], F32)
    xq1_sb = at("xq1_sb", Ao, [128, 16 * TOK], BF16)
    kfm_sb = at("kfm_sb", Bo, [128, LH * T], BF16)
    h_sb1 = at("h_sb1", Bo, [128, 4 * D], F32)
    wo_buf = DBuf(at("wo_buf", Bo, [128, 2 * 16 * 512], BF16), 2, 16 * 512)
    t1_buf = DBuf(at("t1_buf", Bo, [128, 2 * 16 * 512], BF16), 2, 16 * 512)
    g3b_sb = at("g3b_sb", Bo, [128, DFF], BF16)
    t3_buf = DBuf(at("t3_buf", Bo, [128, 2 * 16 * 512], BF16), 2, 16 * 512)
    modc_sb = at("modc_sb", Bo, [1, 6144], F32)
    q3st2_sb = at("q3st2", Bo + 16384, [128, 512], BF16)
    vtm_sb = at("vtm_sb", Co, [128, 32 * 512], BF16)
    h_sb2 = at("h_sb2", Co, [128, 4 * D], F32)
    ga_t = at("ga_t", Co, [128, D], F32)
    bo_t = at("bo_t", Co + 8192, [128, D], F32)
    t2_buf = DBuf(at("t2_buf", Co, [128, 2 * 16 * 512], BF16), 2, 16 * 512)
    h2qf_buf = DBuf(at("h2qf_buf", Co, [128, 2 * 16 * 128], BF16), 2, 16 * 128)
    wst_buf = DBuf(at("wst_buf", Do, [128, 2 * 16 * 512], BF16), 2, 16 * 512)
    attn_sb = at("attn_sb", Do, [128, 32 * 512], BF16)
    xq2_sb = at("xq2_sb", Do, [128, 16 * TOK], BF16)
    ca_t = at("ca_t", Do, [128, D], F32)
    sa_t = at("sa_t", Do + 8192, [128, D], F32)
    tw_buf = DBuf(at("tw_buf", Do, [128, 2 * 3072], F32), 2, 3072)
    xt_sb = at("xt_sb", Eo, [128, D], F32)
    zt_sb = at("zt_sb", Eo + 8192, [128, D], F32)
    hbt_sb = at("hbt_sb", Eo + 16384, [128, D], BF16)
    hag_buf = DBuf(at("hag_buf", Eo, [128, 2 * 16 * CH], BF16), 2, 16 * CH)
    g1b_sb = at("g1b_sb", Eo + 16384, [128, D], F32)
    xqst_sb = at("xqst_sb", Eo + 24576, [128, D], BF16)
    pbt_sb = DBuf(at("pbt_sb", Eo + 20480, [128, 4 * CH], BF16), 4, CH)
    av_sb = at("av_sb", Eo + 24576, [128, CH], BF16)
    dn_sb = at("dn_sb", Eo + 25600, [1, CH], F32)
    rl_sb = at("rl_sb", Eo + 27648, [128, 4], F32)
    rsm_sb = at("rsm_sb", Eo + 8192, [128, 16 * 136], F32)
    rsm2_sb = at("rsm2_sb", Ao + 16384, [128, 16 * 136], F32)
    afm_t = at("afm_t", Eo + 8192, [128, 16 * 128], BF16)
    h2_t = at("h2_t", Eo, [128, DFF], BF16)
    n3_t = at("n3_t", Eo + 16384, [128, DFF], BF16)
    s1_t = at("s1_t", Eo + 12288, [128, CH], F32)
    h2st = at("h2st", Eo + 14336, [128, CH], BF16)
    _fc = [BASE + Fo]

    def fal(name, shape, dt):
        t = nc.alloc_sbuf_tensor_at(name, list(shape), dt, offset=_fc[0])
        it = 2 if dt == BF16 else 4
        n = shape[-1] * (shape[1] if len(shape) == 3 else 1) * it
        _fc[0] += (n + 31) // 32 * 32
        return t

    ident_bf = fal("identb", [128, 128], BF16)
    ident_f32 = fal("identf", [128, 128], F32)
    maskT_sb = fal("maskT", [128, 4 * CH], BF16)
    ones_f = fal("ones_fs", [1, 128], F32)
    onesc_f = fal("onesc_fs", [128, 1], F32)
    onesc_b = fal("onesc_bs", [128, 1], BF16)
    iota8_sb = fal("iota8s", [128, 8], F32)
    m8_sb = fal("m8s", [128, 8], F32)
    eye64_sb = fal("eye64s", [64, 8], F32)
    eye8_sb = fal("eye8s", [8, 8], F32)
    cosg_sb = fal("cosg", [128, CH], BF16)
    sing_sb = fal("sing", [128, CH], BF16)
    rp0_sb = fal("rp0", [128, CH], F32)
    rp1_sb = fal("rp1", [128, CH], F32)
    dgnt_sb = fal("dgnts", [64, D], BF16)
    biasw_sb = fal("biasws", [8, D], BF16)
    dbwT_sb = fal("dbwTs", [128, 16 * 8], F32)
    stg_sb = fal("stgs", [1, D], F32)
    stg2_sb = fal("stg2s", [1, D], F32)
    bq_sb = fal("bqs", [128, 4], F32)
    bk_sb = fal("bks", [128, 4], F32)
    bvb_sb = fal("bvbs", [128, 512], F32)
    sml = fal("smls", [128, 144], F32)
    gsc_sb = fal("gscs", [128, 64], F32)
    g8a_sb = fal("g8as", [128, 8], F32)
    g8b_sb = fal("g8bs", [128, 8], F32)
    g8c_sb = fal("g8cs", [128, 8], F32)
    g8d_sb = fal("g8ds", [128, 8], F32)
    smlb = fal("smlbs", [128, 64], BF16)
    mkb_sb = fal("mkbs", [128, 8], BF16)
    fmm_sb = fal("fmms", [64, 128], BF16)
    fmk_sb = fal("fmks", [8, 128], BF16)
    qn_sb = fal("qns", [1, 64], F32)
    kn_sb = fal("kns", [1, 8], F32)
    nB_sb = fal("nBs", [128, 1], F32)
    epsln = fal("epslns", [128, 1], F32)
    epsbt = fal("epsbts", [128, 1], F32)
    sq_sb = fal("sqs", [128, 512], F32)
    stat_sb = fal("stats", [128, 16], F32)
    asum_sb = fal("asums", [128, 4], F32)
    arr_sb = fal("arrs", [1, 64], F32)
    a_bc = fal("a_bcs", [128, 8], F32)
    impb = fal("impbs", [128, 16], F32)
    c1b_sb = fal("c1bs", [128, 64], F32)
    c3b_sb = fal("c3bs", [128, 64], F32)
    mtmb_sb = fal("mtmbs", [128, 8], F32)
    scfm_sb = fal("scfms", [128, 64], F32)
    xst_sb = fal("xsts", [128, 512], F32)
    q3st_sb = fal("q3st", [128, 512], BF16)
    assert _fc[0] - BASE - Fo <= 47424, f"F overflow {_fc[0] - BASE - Fo}"

    psA = nc.alloc_psum_tensor("psA", [128, 512], F32)
    psB = nc.alloc_psum_tensor("psB", [128, 512], F32)
    psC = nc.alloc_psum_tensor("psC", [128, 512], F32)
    psD = nc.alloc_psum_tensor("psD", [128, 512], F32)
    psEF = nc.alloc_psum_tensor("psEF", [128, 1024], F32)
    psE = psEF[:, 0:512]
    psF = psEF[:, 512:1024]
    pbA = nc.alloc_psum_tensor("pbA", [128, 512], BF16)
    pdn = nc.alloc_psum_tensor("pdn", [2, 512], F32)
    pstile = [psA, psB, psC, psD, psE, psF, pbA, pdn]

    tap_list = []

    def want(name):
        return name in taps

    def mk_tap(name, shape, dt=F32):
        t = nc.dram_tensor("tap_" + name, list(shape), dt,
                           kind="ExternalOutput").ap()
        tap_list.append(name)
        return t

    def V3(tsr, a):
        return tsr[:].rearrange("p (x y) -> p x y", x=a)

    hfm_v = V3(hfm_sb, 16)
    qfm_v = V3(qfm_sb, LH)
    kfm_v = V3(kfm_sb, LH)
    vtm_v = V3(vtm_sb, 32)
    attn_v = V3(attn_sb, 32)
    attn_tm_v = V3(attn_tm, 8)
    xq1_v = V3(xq1_sb, 16)
    xq2_v = V3(xq2_sb, 16)
    hfm32_v = V3(hfm32_t, 16)
    afm_v = V3(afm_t, 16)
    rsm_v = V3(rsm_sb, 16)
    dbwT_v = V3(dbwT_sb, 16)

    with nc.Block() as block:
        S = Ser(nc, block)

        def p0_loads(s, dma):
            dma(out=x_cur, in_=ins["x_in"])
            dma(out=ident_bf[:], in_=ins["ident_bf"])
            dma(out=ident_f32[:], in_=ins["ident_f32"])
            dma(out=maskT_sb[:], in_=ins["maskT_bf"])
            dma(out=ones_f[:], in_=ins["ones_f"])
            dma(out=onesc_f[:], in_=ins["onesc_f"])
            dma(out=onesc_b[:], in_=ins["onesc_b"])
            dma(out=iota8_sb[:], in_=ins["iota8"])
            dma(out=m8_sb[:], in_=ins["m8"])
            dma(out=eye64_sb[:], in_=ins["eye64x8"])
            dma(out=eye8_sb[:], in_=ins["eye8"])
            dma(out=dgnt_sb[:], in_=ins["dgnt"])
            dma(out=biasw_sb[:], in_=ins["biasw"])
            dma(out=dbwT_v, in_=ins["debiaswT"].rearrange("(f p) c -> p f c", p=128))
            dma(out=lcs_sb[0:1, 0:D], in_=ins["lc_0"])
            dma(out=lcs_sb[0:1, D:2 * D], in_=ins["lc_1"])
        S.sy(p0_loads)

        def p0_eps(v):
            v.memset(epsln[:], EPS_LN)
            v.memset(asum_sb[:], 0.0)
            v.memset(arr_sb[0:1, :], 0.0)
            v.memset(qn_sb[0:1, :], 0.0)
            v.memset(kn_sb[0:1, :], 0.0)
            v.memset(stat_sb[:], 0.0)
            return v.memset(epsbt[:], EPS_BIT)
        S.dv(p0_eps)

        S.sc(lambda s: s.activation(lcs_sb[0:1, 0:2 * D], lcs_sb[0:1, 0:2 * D],
                                    AF.Silu))

        def p1_sctr(t):
            for l in range(2):
                for fc in range(16):
                    i = t.transpose(
                        psD[:, (l * 16 + fc):(l * 16 + fc) + 1],
                        lcs_sb[0:1, l * D + fc * 128:l * D + (fc + 1) * 128],
                        ident_f32[0:1, 0:1])
            return i
        S.pe(p1_sctr)
        S.dv(lambda v: v.tensor_copy(scfm_sb[:, 0:32], psD[:, 0:32]))

        for l in range(2):
            for fc in range(16):
                S.sy(lambda s, dma, l=l, fc=fc:
                     dma(out=tw_buf[fc % 2][:, 0:3072],
                         in_=ins[f"wmod_{l}"][fc * 128:(fc + 1) * 128, :]))

                def p1_modmm(t, l=l, fc=fc):
                    for j in range(6):
                        i = t.matmul(pstile[j][0:1, 0:512],
                                     scfm_sb[:, (l * 16 + fc):(l * 16 + fc) + 1],
                                     tw_buf[fc % 2][:, j * 512:(j + 1) * 512],
                                     start=(fc == 0), stop=(fc == 15))
                    return i
                S.pe(p1_modmm)

            def p1_modev(v, l=l):
                for j in range(6):
                    i = v.tensor_copy(
                        modc_sb[0:1, l * 3072 + j * 512:l * 3072 + (j + 1) * 512],
                        pstile[j][0:1, 0:512])
                return i
            S.dv(p1_modev)

        def p1_modag(g, dma, coll):
            dma(out=mod_in, in_=modc_sb[0:1, :])
            g.wait_ge(S.sems["gp"].h, S.sems["gp"].n)
            coll("AllGather", ALU.bypass, replica_groups=RG_BATCH,
                 ins=[mod_in], outs=[mod_ag])
            g.wait_ge(S.sems["cc"].h, S.sems["cc"].n)
        S.gp(p1_modag)


        wseq = []
        for wi, (name, nch, wdt) in enumerate(
                [("w1t_s", 16, 1024), ("w2t_s", 16, 1024), ("w3t_s", 8, 2048)]):
            for chk in range(nch):
                wseq.append((wi, name, chk, wdt))

        pe_s, sc_s, dv_s, sy_s, gp_s, cc_s = (S.sems[k] for k in
                                              ("pe", "sc", "dv", "sy", "gp", "cc"))
        P0, D0, Y0 = pe_s.n, dv_s.n, sy_s.n

        def abs_sy(s2, dma):
            for i, (wi, name, chk, wdt) in enumerate(wseq):
                if i >= 2:
                    s2.wait_ge(dv_s.h, D0 + 2 * (i - 1))
                dma(out=tw_buf[i][:, 0:wdt],
                    in_=ins[name][chk * 128:(chk + 1) * 128, :])
        S.raw("sy", abs_sy)

        def abs_dv(v):
            n = D0
            for i, (wi, name, chk, wdt) in enumerate(wseq):
                def g(ii):
                    nonlocal n
                    n += 1
                    ii.then_inc(dv_s.h, 1)
                    v.wait_ge(dv_s.h, n)
                v.wait_ge(sy_s.h, Y0 + 16 * (i + 1))
                if chk == 0:
                    g(v.tensor_reduce(asum_sb[:, wi:wi + 1],
                                      tw_buf[i][:, 0:wdt], AX, ALU.add,
                                      apply_absolute_value=True))
                    g(v.engine_nop())
                else:
                    g(v.tensor_reduce(stat_sb[:, 0:1], tw_buf[i][:, 0:wdt],
                                      AX, ALU.add, apply_absolute_value=True))
                    g(v.tensor_tensor(asum_sb[:, wi:wi + 1],
                                      asum_sb[:, wi:wi + 1], stat_sb[:, 0:1],
                                      ALU.add))
        S.raw("dv", abs_dv)
        dv_s.n = D0 + 2 * len(wseq)
        sy_s.n = Y0 + 16 * len(wseq)

        S.pe(lambda t: t.matmul(pdn[0:1, 0:4], onesc_f[:, 0:1], asum_sb[:, 0:4],
                                start=True, stop=True))
        S.dv(lambda v: v.tensor_copy(arr_sb[0:1, 0:4], pdn[0:1, 0:4]))
        if want("arr0"):
            tp = mk_tap("arr0", [1, 64])
            S.sy(lambda s, dma: dma(out=tp, in_=arr_sb[0:1, :]))

        def p1_ar(g, dma, coll):
            dma(out=st_in, in_=arr_sb[0:1, 0:4])
            g.wait_ge(S.sems["gp"].h, S.sems["gp"].n)
            coll("AllReduce", ALU.add, replica_groups=RG_ALL,
                 ins=[st_in], outs=[st_ag])
            g.wait_ge(S.sems["cc"].h, S.sems["cc"].n)
            dma(out=arr_sb[0:1, 4:8], in_=st_ag)
        S.gp(p1_ar)

        def p1_scal(v):
            v.tensor_scalar(arr_sb[0:1, 8:11], arr_sb[0:1, 4:7],
                            1.0 / (DFF * D), None, ALU.mult)
            v.tensor_scalar(arr_sb[0:1, 12:15], arr_sb[0:1, 8:11], EPS_BIT,
                            None, ALU.add)
            v.reciprocal(arr_sb[0:1, 12:15], arr_sb[0:1, 12:15])
            v.tensor_scalar(arr_sb[0:1, 16:19], arr_sb[0:1, 8:11], 1.0, None,
                            ALU.add)
            v.tensor_scalar(arr_sb[0:1, 20:23], arr_sb[0:1, 8:11], -1.0, 1.0,
                            ALU.mult, ALU.add)
            v.reciprocal(arr_sb[0:1, 20:23], arr_sb[0:1, 20:23])
            return v.tensor_tensor(arr_sb[0:1, 16:19], arr_sb[0:1, 16:19],
                                   arr_sb[0:1, 20:23], ALU.mult)
        S.dv(p1_scal)
        S.sc(lambda s: s.activation(arr_sb[0:1, 24:27], arr_sb[0:1, 16:19], AF.Ln))
        S.dv(lambda v: v.tensor_scalar(arr_sb[0:1, 24:27], arr_sb[0:1, 24:27],
                                       0.5, None, ALU.mult))
        if want("arr1"):
            tp = mk_tap("arr1", [1, 64])
            S.sy(lambda s, dma: dma(out=tp, in_=arr_sb[0:1, :]))

        def p1_bc(t):
            for j in range(3):
                t.matmul(psD[:, j:j + 1], ones_f[0:1, :],
                         arr_sb[0:1, 12 + j:13 + j], start=True, stop=True)
            for j in range(3):
                i = t.matmul(psD[:, 3 + j:4 + j], ones_f[0:1, :],
                             arr_sb[0:1, 24 + j:25 + j], start=True, stop=True)
            return i
        S.pe(p1_bc)
        S.dv(lambda v: v.tensor_copy(a_bc[:, 0:6], psD[:, 0:6]))

        P0, C0, D0, Y0, G0 = pe_s.n, sc_s.n, dv_s.n, sy_s.n, gp_s.n

        def tern_sy(s2, dma):
            for i, (wi, name, chk, wdt) in enumerate(wseq):
                if i >= 2:
                    s2.wait_ge(sc_s.h, C0 + i - 1)
                dma(out=tw_buf[i][:, 0:wdt],
                    in_=ins[name][chk * 128:(chk + 1) * 128, :])
        S.raw("sy", tern_sy)

        def tern_sc(s):
            for i, (wi, name, chk, wdt) in enumerate(wseq):
                s.wait_ge(sy_s.h, Y0 + 16 * (i + 1))
                if i >= 1:
                    s.wait_ge(dv_s.h, D0 + i)
                s.activation(zt_sb[:, 0:wdt], tw_buf[i][:, 0:wdt], AF.Tanh,
                             scale=a_bc[:, wi:wi + 1]).then_inc(sc_s.h, 1)
        S.raw("sc", tern_sc)

        def tern_dv(v):
            for i, (wi, name, chk, wdt) in enumerate(wseq):
                v.wait_ge(sc_s.h, C0 + i + 1)
                if i >= 2:
                    v.wait_ge(gp_s.h, G0 + 16 * (i - 1))
                dst = hbt_sb[:, 0:wdt] if i % 2 == 0 else xqst_sb[:, 0:wdt]
                v.tensor_scalar(dst, zt_sb[:, 0:wdt], MAGIC, -MAGIC,
                                ALU.add, ALU.add).then_inc(dv_s.h, 1)
        S.raw("dv", tern_dv)

        def tern_gp(g, dma, coll):
            for i, (wi, name, chk, wdt) in enumerate(wseq):
                g.wait_ge(dv_s.h, D0 + i + 1)
                row0 = [0, 2048, 4096][wi] + chk * (128 if wi < 2 else 256)
                srcb = hbt_sb[:, 0:wdt] if i % 2 == 0 else xqst_sb[:, 0:wdt]
                if wi < 2:
                    dst = tern_in[row0:row0 + 128, :]
                else:
                    dst = tern_in.rearrange("r t -> (r t)")[
                        row0 * 1024: row0 * 1024 + 128 * 2048].rearrange(
                        "(p w) -> p w", p=128)
                dma(out=dst, in_=srcb)
        S.raw("pl", tern_gp)
        sc_s.n = C0 + len(wseq)
        dv_s.n = D0 + len(wseq)
        sy_s.n = Y0 + 16 * len(wseq)
        gp_s.n = G0 + 16 * len(wseq)

        tern_sem = Sem(nc, "tc")

        def p1_ternag(g, dma, coll):
            g.collective_compute(
                "AllGather", ALU.bypass, replica_groups=RG_ALL,
                ins=[tern_in], outs=[tern_ag]).then_inc(tern_sem.h, 1)
        S.gp(p1_ternag)

        if want("mod"):
            tp = mk_tap("mod", [4, 6144])
            S.sy(lambda s, dma: dma(out=tp, in_=mod_ag))
        if want("abc"):
            tp = mk_tap("abc", [128, 8])
            S.sy(lambda s, dma: dma(out=tp, in_=a_bc[:]))


        S.sy(lambda s, dma: dma(
            out=rsm_v, in_=ins["rsmall"].rearrange("(f p) c -> p f c", p=128)))

        def p2_cacc(v):
            for fc in range(16):
                base = fc * 136
                sw3 = rsm_sb[:, base:base + 64].rearrange("p (a b) -> p a b", b=8)
                md2 = rsm_sb[:, base + 64:base + 72]
                mdb = md2.to_broadcast([128, 8, 8])
                uu3 = rsm_sb[:, base + 72:base + 136].rearrange("p (a b) -> p a b", b=8)
                dbb = dbwT_sb[:, fc * 8:(fc + 1) * 8].to_broadcast([128, 8, 8])
                g3 = gsc_sb[:, 0:64].rearrange("p (a b) -> p a b", b=8)
                v.tensor_tensor(g3, sw3, mdb, ALU.mult)
                if fc == 0:
                    v.tensor_copy(sq_sb[:, 0:64], gsc_sb[:, 0:64])
                else:
                    v.tensor_tensor(sq_sb[:, 0:64], sq_sb[:, 0:64],
                                    gsc_sb[:, 0:64], ALU.add)
                v.tensor_tensor(gsc_sb[:, 0:8], md2, md2, ALU.mult)
                if fc == 0:
                    v.tensor_copy(sq_sb[:, 64:72], gsc_sb[:, 0:8])
                else:
                    v.tensor_tensor(sq_sb[:, 64:72], sq_sb[:, 64:72],
                                    gsc_sb[:, 0:8], ALU.add)
                v.tensor_tensor(g3, uu3, dbb, ALU.mult)
                if fc == 0:
                    i = v.tensor_copy(sq_sb[:, 72:136], gsc_sb[:, 0:64])
                else:
                    i = v.tensor_tensor(sq_sb[:, 72:136], sq_sb[:, 72:136],
                                        gsc_sb[:, 0:64], ALU.add)
            return i
        S.dv(p2_cacc)
        S.pe(lambda t: t.matmul(pdn[0:1, 0:136], onesc_f[:, 0:1],
                                sq_sb[:, 0:136], start=True, stop=True))
        S.dv(lambda v: v.tensor_copy(stg_sb[0:1, 0:136], pdn[0:1, 0:136]))

        def bcast_to(dst, src_row, W, j0=0):
            nj = (W + 511) // 512
            pe_s, dv_s = S.sems["pe"], S.sems["dv"]
            P0, D0 = pe_s.n, dv_s.n

            def bc_pe(t):
                for j in range(nj):
                    w_ = min(512, W - j * 512)
                    if j >= 1:
                        t.wait_ge(dv_s.h, D0 + j)
                    t.matmul(psD[:, 0:w_], ones_f[0:1, :],
                             src_row[0:1, j0 + j * 512:j0 + j * 512 + w_],
                             start=True, stop=True).then_inc(pe_s.h, 1)
            S.raw("pe", bc_pe)

            def bc_dv(v):
                for j in range(nj):
                    w_ = min(512, W - j * 512)
                    v.wait_ge(pe_s.h, P0 + j + 1)
                    v.tensor_copy(dst[:, j * 512:j * 512 + w_],
                                  psD[:, 0:w_]).then_inc(dv_s.h, 1)
            S.raw("dv", bc_dv)
            pe_s.n = P0 + nj
            dv_s.n = D0 + nj

        bcast_to(c1b_sb, stg_sb, 64)
        bcast_to(mtmb_sb, stg_sb, 8, j0=64)
        bcast_to(c3b_sb, stg_sb, 64, j0=72)
        S.sy(lambda s, dma: (dma(out=stg_sb[0:1, 0:8], in_=ins["impc"]),
                             dma(out=stg_sb[0:1, 8:16], in_=ins["imps"])))
        bcast_to(impb[:, 0:8], stg_sb, 8)
        bcast_to(impb[:, 8:16], stg_sb, 8, j0=8)
        if want("c1b"):
            tp = mk_tap("c1b", [128, 64])
            S.sy(lambda s, dma: dma(out=tp, in_=c1b_sb[:]))

        def load_mod_row(l, m, plus1=False, transform_g=False):
            def f(s, dma):
                dma(out=stg_sb[0:1, :].rearrange("x (rk c) -> x rk c", rk=4),
                    in_=mod_ag[:, l * 3072 + m * 512:l * 3072 + (m + 1) * 512]
                        .rearrange("(x rk) c -> x rk c", x=1))
            S.sy(f)
            if plus1:
                S.dv(lambda v: v.tensor_scalar(stg_sb[0:1, :], stg_sb[0:1, :],
                                               1.0, None, ALU.add))
            if transform_g:
                def tg(v):
                    v.tensor_single_scalar(stg2_sb[0:1, :], stg_sb[0:1, :],
                                           0.0, ALU.is_ge)
                    v.tensor_scalar(stg2_sb[0:1, :], stg2_sb[0:1, :], 2.0,
                                    -1.0, ALU.mult, ALU.add)
                    v.tensor_tensor(stg_sb[0:1, :], stg_sb[0:1, :],
                                    stg2_sb[0:1, :], ALU.mult)
                    v.tensor_single_scalar(stg_sb[0:1, :], stg_sb[0:1, :],
                                           0.25, ALU.max)
                    return v.tensor_tensor(stg_sb[0:1, :], stg_sb[0:1, :],
                                           stg2_sb[0:1, :], ALU.mult)
                S.dv(tg)

        def ln_phase(dst, l):
            pe_s, sc_s, dv_s, sy_s = (S.sems[k] for k in ("pe", "sc", "dv", "sy"))
            P0, C0, D0, Y0 = pe_s.n, sc_s.n, dv_s.n, sy_s.n
            att = dst == "attn"
            dvn = 13 if att else 8
            scn = 4
            pen = 4 if att else 0

            def ln_sy(s2, dma):
                for t_i in range(8):
                    if t_i >= 1:
                        s2.wait_ge(sc_s.h, C0 + scn * t_i)
                        s2.wait_ge(dv_s.h, D0 + dvn * (t_i - 1) + 1)
                    dma(out=xt_sb[:], in_=x_cur[t_i * 128:(t_i + 1) * 128, :])
            S.raw("sy", ln_sy)

            def ln_sc(s):
                n = C0
                for t_i in range(8):
                    s.wait_ge(sy_s.h, Y0 + 16 * (t_i + 1))
                    if t_i >= 1:
                        s.wait_ge(dv_s.h, D0 + dvn * t_i)
                    i = s.activation(zt_sb[:], xt_sb[:], AF.Square,
                                     accum_out=stat_sb[:, 1:2])
                    i.then_inc(s.h_ if False else sc_s.h, 1)
                    n += 1
                    s.wait_ge(sc_s.h, n)
                    s.wait_ge(dv_s.h, D0 + dvn * t_i + 5)
                    i = s.activation(stat_sb[:, 6:7], stat_sb[:, 5:6], AF.Ln,
                                     bias=epsln[:, 0:1])
                    i.then_inc(sc_s.h, 1)
                    n += 1
                    s.wait_ge(sc_s.h, n)
                    i = s.activation(stat_sb[:, 7:8], stat_sb[:, 6:7], AF.Exp,
                                     scale=-0.5)
                    i.then_inc(sc_s.h, 1)
                    n += 1
                    s.wait_ge(sc_s.h, n)
                    s.wait_ge(dv_s.h, D0 + dvn * t_i + 6)
                    i = s.activation(zt_sb[:], xt_sb[:], AF.Identity,
                                     bias=stat_sb[:, 8:9],
                                     scale=stat_sb[:, 7:8])
                    i.then_inc(sc_s.h, 1)
                    n += 1
            S.raw("sc", ln_sc)

            def ln_dv(v):
                n = D0
                dvt = 13 if att else 8
                for t_i in range(8):
                    def g(i):
                        nonlocal n
                        n += 1
                        i.then_inc(dv_s.h, 1)
                        v.wait_ge(dv_s.h, n)
                    v.wait_ge(sy_s.h, Y0 + 16 * (t_i + 1))
                    g(v.tensor_reduce(stat_sb[:, 0:1], xt_sb[:], AX, ALU.add))
                    g(v.tensor_scalar(stat_sb[:, 2:3], stat_sb[:, 0:1],
                                      1.0 / D, None, ALU.mult))
                    v.wait_ge(sc_s.h, C0 + scn * t_i + 1)
                    g(v.tensor_scalar(stat_sb[:, 3:4], stat_sb[:, 1:2],
                                      1.0 / D, None, ALU.mult))
                    g(v.tensor_tensor(stat_sb[:, 4:5], stat_sb[:, 2:3],
                                      stat_sb[:, 2:3], ALU.mult))
                    g(v.tensor_tensor(stat_sb[:, 5:6], stat_sb[:, 3:4],
                                      stat_sb[:, 4:5], ALU.subtract))
                    v.wait_ge(sc_s.h, C0 + scn * t_i + 3)
                    g(v.tensor_scalar(stat_sb[:, 8:9], stat_sb[:, 2:3],
                                      stat_sb[:, 7:8], -1.0, ALU.mult,
                                      ALU.mult))
                    v.wait_ge(sc_s.h, C0 + scn * t_i + 4)
                    if att:
                        g(v.tensor_tensor(zt_sb[:], zt_sb[:], ca_t[:],
                                          ALU.mult))
                        g(v.tensor_tensor(zt_sb[:], zt_sb[:], sa_t[:],
                                          ALU.add))
                        if t_i >= 1:
                            v.wait_ge(pe_s.h, P0 + 4 * t_i)
                        g(v.tensor_copy(hbt_sb[:], zt_sb[:]))
                        for fb in range(4):
                            v.wait_ge(pe_s.h, P0 + 4 * t_i + fb + 1)
                            i = v.tensor_copy(
                                hfm_sb[:].rearrange("p (f k) -> p f k", f=16)
                                [:, fb * 4:(fb + 1) * 4,
                                 t_i * 128:(t_i + 1) * 128],
                                pbA[:].rearrange("p (a b) -> p a b", a=4))
                            i.then_inc(dv_s.h, 1)
                            n += 1
                    else:
                        hdst = h_sb1 if t_i < 4 else h_sb2
                        off = (t_i % 4) * D
                        g(v.tensor_tensor(zt_sb[:], zt_sb[:], ca_t[:],
                                          ALU.mult))
                        g(v.tensor_tensor(hdst[:, off:off + D], zt_sb[:],
                                          sa_t[:], ALU.add))
            S.raw("dv", ln_dv)

            if att:
                def ln_pe(t):
                    dvt = 13
                    for t_i in range(8):
                        t.wait_ge(dv_s.h, D0 + dvt * t_i + 9)
                        for fb in range(4):
                            if t_i > 0 and fb == 0:
                                t.wait_ge(dv_s.h, D0 + dvt * t_i)
                            elif fb >= 1:
                                t.wait_ge(dv_s.h, D0 + dvt * t_i + 9 + fb)
                            for j in range(4):
                                i = t.transpose(
                                    pbA[:, j * 128:(j + 1) * 128],
                                    hbt_sb[:, (fb * 4 + j) * 128:
                                           (fb * 4 + j + 1) * 128],
                                    ident_bf[:])
                            i.then_inc(pe_s.h, 1)
                S.raw("pe", ln_pe)
            pe_s.n = P0 + pen * 8
            pe_s.n = P0 + pen * 8
            sc_s.n = C0 + scn * 8
            dv_s.n = D0 + dvn * 8
            sy_s.n = Y0 + 16 * 8

        def layer(l):
            kw = 512 if l == 0 else 128
            nof_kv = kw // 128
            load_mod_row(l, 1, plus1=True)
            bcast_to(ca_t, stg_sb, D)
            load_mod_row(l, 0)
            bcast_to(sa_t, stg_sb, D)
            ln_phase("attn", l)
            if want(f"h_l{l}"):
                tp = mk_tap(f"h_l{l}", [128, 16 * TOK], BF16)
                S.sy(lambda s, dma: dma(out=tp, in_=hfm_sb[:]))

            def agh(g, dma, coll):
                dma(out=hfm_in.rearrange("(f p) c -> p f c", p=128),
                    in_=hfm_sb[:].rearrange("p (f k) -> p f k", f=16))
                g.wait_ge(S.sems["gp"].h, S.sems["gp"].n)
                coll("AllGather", ALU.bypass, replica_groups=RG_BATCH,
                     ins=[hfm_in], outs=[hfm_ag])
                g.wait_ge(S.sems["cc"].h, S.sems["cc"].n)
            S.gp(agh)

            def ldb(s, dma):
                dma(out=bq_sb[:, 0:4], in_=ins[f"bq_{l}"])
                dma(out=bk_sb[:, 0:nof_kv], in_=ins[f"bk_{l}"])
                dma(out=stg_sb[0:1, 0:kw], in_=ins[f"bv_{l}"])
            S.sy(ldb)
            bcast_to(bvb_sb[:, 0:kw], stg_sb, kw)

            pe_s, sc_s, dv_s, sy_s, gp_s = (S.sems[k] for k in
                                            ("pe", "sc", "dv", "sy", "gp"))
            for wg in ("q", "k", "v"):
                nof = 4 if wg == "q" else nof_kv
                wname = f"w{wg}_{l}"
                rope = (l == 1 and wg in ("q", "k"))
                S.gp(lambda g, dma, coll, wname=wname, nof=nof:
                     dma(out=wst_buf.t[:, 0:16 * nof * 128]
                         .rearrange("p (f c) -> p f c", f=16),
                         in_=ins[wname].rearrange("(f p) c -> p f c", p=128)))
                steps = [(g_, j) for g_ in range(8)
                         for j in range(nof if wg != "v" else 4)]
                ns = len(steps)
                P0, D0, Y0, G0 = pe_s.n, dv_s.n, sy_s.n, gp_s.n
                dvinc = 2 if rope else 1
                syinc = 16 * (3 if rope else 1)
                nperg = nof if wg != "v" else 4

                def qkv_sy(s2, dma, wg=wg, steps=steps, nperg=nperg,
                           dvinc=dvinc, rope=rope, P0=P0, D0=D0):
                    for g_ in range(8):
                        if rope and g_ >= 1:
                            s2.wait_ge(dv_s.h, D0 + dvinc * nperg * g_)
                        elif g_ >= 2:
                            s2.wait_ge(dv_s.h, D0 + dvinc * nperg * (g_ - 1))
                        dma(out=hag_buf[g_][:].rearrange("p (f c) -> p f c", f=16),
                            in_=hfm_ag[owner(g_) * D:(owner(g_) + 1) * D,
                                       lidx(g_) * CH:(lidx(g_) + 1) * CH]
                            .rearrange("(f p) c -> p f c", p=128))
                        if rope:
                            dma(out=cosg_sb[:],
                                in_=ins["cos_fm"][:, g_ * CH:(g_ + 1) * CH])
                            dma(out=sing_sb[:],
                                in_=ins["sin_fm"][:, g_ * CH:(g_ + 1) * CH])
                S.raw("sy", qkv_sy)

                def qkv_pe(t, wg=wg, steps=steps, nof=nof, syinc=syinc,
                           dvinc=dvinc, P0=P0, D0=D0, Y0=Y0, G0=G0):
                    t.wait_ge(gp_s.h, G0)
                    seen = [0]

                    def wdv(val):
                        if val > seen[0]:
                            t.wait_ge(dv_s.h, val)
                            seen[0] = val
                    for si, (g_, j) in enumerate(steps):
                        if j == 0:
                            t.wait_ge(sy_s.h, Y0 + syinc * (g_ + 1))
                        if si >= 2:
                            wdv(D0 + dvinc * (si - 1))
                        ps = psA if si % 2 == 0 else psB
                        kw2 = 512 if (wg != "v" or l == 0) else 128
                        for fc in range(16):
                            if wg == "v":
                                i = t.matmul(ps[:, 0:kw2],
                                             hag_buf[g_][:, fc * 512 + j * 128:
                                                         fc * 512 + (j + 1) * 128],
                                             wst_buf.t[:, fc * kw2:(fc + 1) * kw2],
                                             start=(fc == 0), stop=(fc == 15))
                            else:
                                i = t.matmul(ps[:, 0:512],
                                             wst_buf.t[:, (fc * nof + j) * 128:
                                                       (fc * nof + j + 1) * 128],
                                             hag_buf[g_][:, fc * 512:(fc + 1) * 512],
                                             start=(fc == 0), stop=(fc == 15))
                        i.then_inc(pe_s.h, 1)
                S.raw("pe", qkv_pe)

                def qkv_dv(v, wg=wg, steps=steps, nof=nof, rope=rope,
                           P0=P0, D0=D0):
                    for si, (g_, j) in enumerate(steps):
                        v.wait_ge(pe_s.h, P0 + si + 1)
                        ps = psA if si % 2 == 0 else psB
                        if wg == "v":
                            kw2 = 512 if l == 0 else 128
                            v.scalar_tensor_tensor(
                                vtm_sb[:, (g_ * 4 + j) * 512:
                                       (g_ * 4 + j) * 512 + kw2],
                                ps[:, 0:kw2], 1.0, bvb_sb[:, 0:kw2],
                                ALU.mult, ALU.add).then_inc(dv_s.h, 1)
                            continue
                        dst = qfm_sb if wg == "q" else kfm_sb
                        dstoff = (j * T + g_ * CH) if wg == "q" else                                  ((j * T + g_ * CH) if l == 0 else g_ * CH)
                        bias = bq_sb if wg == "q" else bk_sb
                        if rope:
                            v.tensor_scalar(rp0_sb[:], ps[:, 0:512],
                                            bias[:, j:j + 1],
                                            SCL if wg == "q" else None,
                                            ALU.add,
                                            ALU.mult if wg == "q" else ALU.bypass)
                            v.tensor_tensor(rp1_sb[:], rp0_sb[:], cosg_sb[:],
                                            ALU.mult)
                            i = v.stream_shuffle(xst_sb[:], rp0_sb[:], SHUF_PAIR)
                            i.then_inc(dv_s.h, 1)
                            v.wait_ge(dv_s.h, D0 + 2 * si + 1)
                            v.tensor_tensor(xst_sb[:], xst_sb[:], sing_sb[:],
                                            ALU.mult)
                            v.tensor_tensor(dst[:, dstoff:dstoff + 512],
                                            rp1_sb[:], xst_sb[:],
                                            ALU.add).then_inc(dv_s.h, 1)
                        else:
                            v.tensor_scalar(dst[:, dstoff:dstoff + 512],
                                            ps[:, 0:512], bias[:, j:j + 1],
                                            SCL if wg == "q" else None,
                                            ALU.add,
                                            ALU.mult if wg == "q" else
                                            ALU.bypass).then_inc(dv_s.h, 1)
                S.raw("dv", qkv_dv)
                pe_s.n = P0 + ns
                dv_s.n = D0 + dvinc * ns
                sy_s.n = Y0 + syinc * 8

            nus = [( "q", h, g_) for h in range(LH) for g_ in range(8)] +                   [("k", kvh, g_) for kvh in range(nof_kv) for g_ in range(8)]
            P0, C0, D0 = pe_s.n, sc_s.n, dv_s.n

            def nrm_sc(s, nus=nus, P0=P0):
                for ui, (wg, h, g_) in enumerate(nus):
                    if ui >= 2:
                        s.wait_ge(pe_s.h, P0 + ui - 1)
                    src_ = qfm_sb if wg == "q" else kfm_sb
                    off = (h * T + g_ * CH) if (wg == "q" or l == 0) else g_ * CH
                    dstsq = sq_sb if ui % 2 == 0 else xst_sb
                    s.activation(dstsq[:], src_[:, off:off + 512],
                                 AF.Square).then_inc(sc_s.h, 1)
            S.raw("sc", nrm_sc)

            def nrm_pe(t, nus=nus, C0=C0, D0=D0):
                for ui, (wg, h, g_) in enumerate(nus):
                    t.wait_ge(sc_s.h, C0 + ui + 1)
                    if ui >= 2:
                        t.wait_ge(dv_s.h, D0 + 2 * (ui - 1))
                    dstsq = sq_sb if ui % 2 == 0 else xst_sb
                    ps = psE if ui % 2 == 0 else psF
                    t.matmul(ps[0:1, 0:512], onesc_f[:, 0:1], dstsq[:],
                             start=True, stop=True).then_inc(pe_s.h, 1)
            S.raw("pe", nrm_pe)

            def nrm_dv(v, nus=nus, P0=P0, D0=D0):
                n = D0
                for ui, (wg, h, g_) in enumerate(nus):
                    v.wait_ge(pe_s.h, P0 + ui + 1)
                    ps = psE if ui % 2 == 0 else psF
                    if wg == "q":
                        i = v.tensor_reduce(qn_sb[0:1, h * 8 + g_:h * 8 + g_ + 1],
                                            ps[0:1, 0:512], AX, ALU.max)
                        i.then_inc(dv_s.h, 1)
                        n += 1
                        v.wait_ge(dv_s.h, n)
                        v.engine_nop().then_inc(dv_s.h, 1)
                        n += 1
                    else:
                        i = v.tensor_reduce(arr_sb[0:1, 40:41], ps[0:1, 0:512],
                                            AX, ALU.max)
                        i.then_inc(dv_s.h, 1)
                        n += 1
                        v.wait_ge(dv_s.h, n)
                        if g_ == 0:
                            i = v.tensor_copy(kn_sb[0:1, h:h + 1],
                                              arr_sb[0:1, 40:41])
                        else:
                            i = v.tensor_tensor(kn_sb[0:1, h:h + 1],
                                                kn_sb[0:1, h:h + 1],
                                                arr_sb[0:1, 40:41], ALU.max)
                        i.then_inc(dv_s.h, 1)
                        n += 1
            S.raw("dv", nrm_dv)
            pe_s.n = P0 + len(nus)
            sc_s.n = C0 + len(nus)
            dv_s.n = D0 + 2 * len(nus)

            if l == 1:
                S.dv(lambda v: v.tensor_copy(
                    kn_sb[0:1, 1:4].rearrange("x (a b) -> x a b", a=1),
                    kn_sb[0:1, 0:1].to_broadcast([1, 1, 3])))
            def negsq(s):
                i = s.activation(qn_sb[0:1, 32:64], qn_sb[0:1, 0:32], AF.Ln)
                i.then_inc(S.sems["sc"].h, 1)
                S.sems["sc"].n += 1
                s.wait_ge(S.sems["sc"].h, S.sems["sc"].n)
                i = s.activation(qn_sb[0:1, 32:64], qn_sb[0:1, 32:64], AF.Exp,
                                 scale=0.5)
                i.then_inc(S.sems["sc"].h, 1)
                S.sems["sc"].n += 1
                s.wait_ge(S.sems["sc"].h, S.sems["sc"].n)
                i = s.activation(kn_sb[0:1, 4:8], kn_sb[0:1, 0:4], AF.Ln)
                i.then_inc(S.sems["sc"].h, 1)
                S.sems["sc"].n += 1
                s.wait_ge(S.sems["sc"].h, S.sems["sc"].n)
                return s.activation(kn_sb[0:1, 4:8], kn_sb[0:1, 4:8], AF.Exp,
                                    scale=0.5)
            S.sc(negsq)

            def negb(v):
                v.tensor_tensor(arr_sb[0:1, 0:32].rearrange("x (a b) -> x a b", b=8),
                                qn_sb[0:1, 32:64].rearrange("x (a b) -> x a b", b=8),
                                kn_sb[0:1, 4:8].to_broadcast([1, 4, 8]),
                                ALU.mult)
                return v.tensor_scalar(arr_sb[0:1, 0:32], arr_sb[0:1, 0:32],
                                       -1.0, None, ALU.mult)
            S.dv(negb)
            if want(f"qkv_l{l}"):
                tq = mk_tap(f"q_l{l}", [128, LH * T], BF16)
                tk_ = mk_tap(f"k_l{l}", [128, LH * T], BF16)
                tv = mk_tap(f"v_l{l}", [128, 32 * 512], BF16)
                S.sy(lambda s, dma: (dma(out=tq, in_=qfm_sb[:]),
                                     dma(out=tk_, in_=kfm_sb[:]),
                                     dma(out=tv, in_=vtm_sb[:])))

            pe_s, sc_s, dv_s = S.sems["pe"], S.sems["sc"], S.sems["dv"]
            P0, C0, D0 = pe_s.n, sc_s.n, dv_s.n
            us = [(h, gq) for h in range(LH) for gq in range(8)]
            nkcs = [4 * gq + 4 for (h, gq) in us]
            Pu = [P0]
            Cu = [C0]
            Du = [D0]
            for n_ in nkcs:
                Pu.append(Pu[-1] + 2 * n_ + 2)
                Cu.append(Cu[-1] + n_)
                Du.append(Du[-1] + 5)

            def attn_pe(t):
                seen = {}
                pn = [P0]

                def w(sem, val):
                    if val > seen.get(sem, -1):
                        t.wait_ge(S.sems[sem].h, val)
                        seen[sem] = val

                def inc(i):
                    i.then_inc(pe_s.h, 1)
                    pn[0] += 1
                    return pn[0]
                for ui, (h, gq) in enumerate(us):
                    n_ = nkcs[ui]
                    qoff = h * T + gq * CH
                    idx = h * 8 + gq
                    if ui > 0:
                        w("dv", Du[ui] - 1)
                    inc(t.matmul(psD[:, 0:1], ones_f[0:1, :],
                                 arr_sb[0:1, idx:idx + 1], start=True,
                                 stop=True))

                    def scores(kc):
                        if kc >= 2:
                            w("sc", Cu[ui] + kc - 1)
                        ps = psE if kc % 2 == 0 else psF
                        koff = (h * T if l == 0 else 0) + kc * 128
                        diag = kc >= 4 * gq
                        i = t.matmul(ps[:, 0:512], kfm_sb[:, koff:koff + 128],
                                     qfm_sb[:, qoff:qoff + 512],
                                     start=True, stop=not diag)
                        if diag:
                            kcr = kc % 4
                            i = t.matmul(ps[:, 0:512], ident_bf[:],
                                         maskT_sb[:, kcr * 512:(kcr + 1) * 512],
                                         start=False, stop=True)
                        inc(i)

                    scores(0)
                    for kc in range(n_):
                        if kc + 1 < n_:
                            scores(kc + 1)
                        w("sc", Cu[ui] + kc + 1)
                        if kc == 0 and ui > 0:
                            w("dv", Du[ui] - 2)
                        voff = kc * 512 + (h * 128 if l == 0 else 0)
                        t.matmul(psC[:, 0:512], vtm_sb[:, voff:voff + 128],
                                 pbt_sb[kc], start=(kc == 0),
                                 stop=(kc == n_ - 1))
                        inc(t.matmul(pdn[0:1, 0:512], onesc_b[:, 0:1],
                                     pbt_sb[kc], start=(kc == 0),
                                     stop=(kc == n_ - 1)))
                    w("dv", Du[ui] + 3)
                    for qt in range(4):
                        t.transpose(pbA[:, qt * 128:(qt + 1) * 128],
                                    av_sb[:, qt * 128:(qt + 1) * 128],
                                    ident_bf[:])
                    for qt in range(4):
                        i = t.transpose(psD[:, qt:qt + 1],
                                        dn_sb[0:1, qt * 128:(qt + 1) * 128],
                                        ident_f32[0:1, 0:1])
                    inc(i)
                    assert pn[0] == Pu[ui + 1], (pn[0], ui)
            S.raw("pe", attn_pe)

            def attn_sc(s):
                seen = {}

                def w(sem, val):
                    if val > seen.get(sem, -1):
                        s.wait_ge(S.sems[sem].h, val)
                        seen[sem] = val
                for ui, (h, gq) in enumerate(us):
                    n_ = nkcs[ui]
                    for kc in range(n_):
                        sk = (Pu[ui] + 2) if kc == 0 else (Pu[ui] + 1 + 2 * kc)
                        w("pe", sk)
                        if kc == 0:
                            w("dv", Du[ui] + 1)
                        ps = psE if kc % 2 == 0 else psF
                        s.activation(pbt_sb[kc], ps[:, 0:512], AF.Exp,
                                     bias=nB_sb[:, 0:1]).then_inc(sc_s.h, 1)
            S.raw("sc", attn_sc)

            def attn_dv(v):
                seen = {}

                def w(sem, val):
                    if val > seen.get(sem, -1):
                        v.wait_ge(S.sems[sem].h, val)
                        seen[sem] = val
                for ui, (h, gq) in enumerate(us):
                    n_ = nkcs[ui]
                    w("pe", Pu[ui] + 1)
                    if ui > 0:
                        w("sc", Cu[ui])
                    v.tensor_copy(nB_sb[:], psD[:, 0:1]).then_inc(dv_s.h, 1)
                    w("pe", Pu[ui + 1] - 1)
                    v.tensor_copy(av_sb[:], psC[:, 0:512]).then_inc(dv_s.h, 1)
                    v.reciprocal(dn_sb[0:1, :],
                                 pdn[0:1, 0:512]).then_inc(dv_s.h, 1)
                    v.wait_ge(dv_s.h, Du[ui] + 3)
                    w("pe", Pu[ui + 1])
                    v.tensor_copy(rl_sb[:, 0:4], psD[:, 0:4]).then_inc(dv_s.h, 1)
                    v.wait_ge(dv_s.h, Du[ui] + 4)
                    for qt in range(4):
                        doff = (gq * 4 + qt) * 512 + h * 128
                        i = v.tensor_scalar(attn_sb[:, doff:doff + 128],
                                            pbA[:, qt * 128:(qt + 1) * 128],
                                            rl_sb[:, qt:qt + 1], None,
                                            ALU.mult)
                    i.then_inc(dv_s.h, 1)
            S.raw("dv", attn_dv)
            pe_s.n, sc_s.n, dv_s.n = Pu[-1], Cu[-1], Du[-1]

            if want(f"attn_l{l}"):
                tp = mk_tap(f"attn_l{l}", [128, 32 * 512], BF16)
                S.sy(lambda s, dma: dma(out=tp, in_=attn_sb[:]))

            P0, D0 = pe_s.n, dv_s.n
            yts = [(h, fb) for h in range(LH) for fb in range(8)]

            def ytr_pe(t, P0=P0, D0=D0):
                for ui, (h, fb) in enumerate(yts):
                    if ui >= 1:
                        t.wait_ge(dv_s.h, D0 + ui)
                    for j in range(4):
                        tc = fb * 4 + j
                        i = t.transpose(pbA[:, j * 128:(j + 1) * 128],
                                        attn_sb[:, tc * 512 + h * 128:
                                                tc * 512 + (h + 1) * 128],
                                        ident_bf[:])
                    i.then_inc(pe_s.h, 1)
            S.raw("pe", ytr_pe)

            def ytr_dv(v, P0=P0, D0=D0):
                for ui, (h, fb) in enumerate(yts):
                    v.wait_ge(pe_s.h, P0 + ui + 1)
                    v.tensor_copy(kfm_sb[:, h * T + fb * 512:h * T + (fb + 1) * 512],
                                  pbA[:, 0:512]).then_inc(dv_s.h, 1)
            S.raw("dv", ytr_dv)
            pe_s.n = P0 + 32
            dv_s.n = D0 + 32

            S.gp(lambda g, dma, coll: dma(
                out=wst_buf.t[:, 0:4 * D].rearrange("p (f c) -> p f c", f=4),
                in_=ins[f"wo_{l}"].rearrange("(f p) c -> p f c", p=128)))
            P0, D0, Y0, G0 = pe_s.n, dv_s.n, sy_s.n, gp_s.n

            def op_pe(t, P0=P0, D0=D0, G0=G0):
                t.wait_ge(gp_s.h, G0)
                for tt in range(32):
                    for ob in range(4):
                        s = tt * 4 + ob
                        if s >= 2:
                            t.wait_ge(dv_s.h, D0 + s - 1)
                        ps = psA if s % 2 == 0 else psB
                        for hh in range(4):
                            i = t.matmul(ps[:, 0:512],
                                         kfm_sb[:, hh * T + tt * 128:
                                                hh * T + (tt + 1) * 128],
                                         wst_buf.t[:, hh * D + ob * 512:
                                                   hh * D + (ob + 1) * 512],
                                         start=(hh == 0), stop=(hh == 3))
                        i.then_inc(pe_s.h, 1)
            S.raw("pe", op_pe)

            def op_dv(v, P0=P0, Y0=Y0):
                for tt in range(32):
                    for ob in range(4):
                        s = tt * 4 + ob
                        v.wait_ge(pe_s.h, P0 + s + 1)
                        if tt >= 1 and ob == 0:
                            v.wait_ge(sy_s.h, Y0 + 16 * tt)
                        ps = psA if s % 2 == 0 else psB
                        v.tensor_copy(hbt_sb[:, ob * 512:(ob + 1) * 512],
                                      ps[:, 0:512]).then_inc(dv_s.h, 1)
            S.raw("dv", op_dv)

            def op_sy(s2, dma, D0=D0):
                for tt in range(32):
                    s2.wait_ge(dv_s.h, D0 + 4 * (tt + 1))
                    g_ = tt // 4
                    ro = owner(g_) * 1024 + lidx(g_) * 512 + (tt % 4) * 128
                    dma(out=rs_in[ro:ro + 128, :], in_=hbt_sb[:])
            S.raw("sy", op_sy)
            pe_s.n = P0 + 128
            dv_s.n = D0 + 128
            sy_s.n = Y0 + 16 * 32

            def rsc(g, dma, coll):
                coll("ReduceScatter", ALU.add, replica_groups=RG_BATCH,
                     ins=[rs_in], outs=[rs_out])
                g.wait_ge(S.sems["cc"].h, S.sems["cc"].n)
            S.gp(rsc)

            S.sy(lambda s, dma: dma(out=stg_sb[0:1, :], in_=ins[f"bo_{l}"]))
            bcast_to(bo_t, stg_sb, D)
            load_mod_row(l, 2, transform_g=True)
            bcast_to(ga_t, stg_sb, D)
            P0, D0, Y0 = pe_s.n, dv_s.n, sy_s.n

            def ae_sy(s2, dma):
                for t_i in range(8):
                    if t_i >= 1:
                        s2.wait_ge(dv_s.h, D0 + 3 * t_i)
                        s2.wait_ge(sy_s.h, Y0 + 48 * t_i)
                    dma(out=xt_sb[:], in_=x_cur[t_i * 128:(t_i + 1) * 128, :])
                    dma(out=hbt_sb[:], in_=rs_out[t_i * 128:(t_i + 1) * 128, :])
                    s2.wait_ge(dv_s.h, D0 + 3 * (t_i + 1))
                    dma(out=x_cur[t_i * 128:(t_i + 1) * 128, :], in_=xt_sb[:])
            S.raw("sy", ae_sy)

            def ae_dv(v):
                n = D0
                for t_i in range(8):
                    def g(i):
                        nonlocal n
                        n += 1
                        i.then_inc(dv_s.h, 1)
                        v.wait_ge(dv_s.h, n)
                    v.wait_ge(sy_s.h, Y0 + 48 * t_i + 32)
                    g(v.tensor_tensor(zt_sb[:], hbt_sb[:], bo_t[:], ALU.add))
                    g(v.tensor_tensor(zt_sb[:], zt_sb[:], ga_t[:], ALU.mult))
                    g(v.tensor_tensor(xt_sb[:], xt_sb[:], zt_sb[:], ALU.add))
            S.raw("dv", ae_dv)
            dv_s.n = D0 + 24
            sy_s.n = Y0 + 48 * 8
            if want(f"xattn_l{l}"):
                tp = mk_tap(f"xattn_l{l}", [TOK, D], F32)
                S.sy(lambda s, dma: dma(out=tp, in_=x_cur))

            load_mod_row(l, 4, plus1=True)
            bcast_to(ca_t, stg_sb, D)
            load_mod_row(l, 3)
            bcast_to(sa_t, stg_sb, D)
            ln_phase("ffn", l)
            if l == 0:
                load_mod_row(l, 5, transform_g=True)
                bcast_to(gb_t, stg_sb, D)
            if want(f"hffn_l{l}"):
                tp = mk_tap(f"hffn_l{l}", [128, 4 * D], F32)
                S.sy(lambda s, dma: dma(out=tp, in_=h_sb1[:]))

            if l == 0:
                ffn0()
            else:
                ffn1()
            if want(f"xout_l{l}"):
                tp = mk_tap(f"xout_l{l}", [TOK, D], F32)
                S.sy(lambda s, dma: dma(out=tp, in_=x_cur))

        def ffn0():
            pe_s, sc_s, dv_s, sy_s = (S.sems[k] for k in ("pe", "sc", "dv", "sy"))
            S.sy(lambda s, dma: dma(
                out=rsm2_sb[:].rearrange("p (f c) -> p f c", f=16),
                in_=ins["rsmall"].rearrange("(f p) c -> p f c", p=128)))
            P0, C0, D0, Y0 = pe_s.n, sc_s.n, dv_s.n, sy_s.n
            PET = 4 + 1 + 1 + 4
            SCT = 2
            DVT0 = 43
            SYT = 32

            def f0_sy(s2, dma):
                for t_i in range(8):
                    if t_i >= 1:
                        s2.wait_ge(dv_s.h, D0 + DVT0 * t_i)
                        s2.wait_ge(sy_s.h, Y0 + SYT * t_i)
                    dma(out=xt_sb[:], in_=x_cur[t_i * 128:(t_i + 1) * 128, :])
                    s2.wait_ge(dv_s.h, D0 + DVT0 * (t_i + 1))
                    dma(out=x_cur[t_i * 128:(t_i + 1) * 128, :], in_=xt_sb[:])
            S.raw("sy", f0_sy)

            def f0_sc(s):
                for t_i in range(8):
                    hsl_ = h_sb1 if t_i < 4 else h_sb2
                    h_ap = hsl_[:, (t_i % 4) * D:(t_i % 4) * D + D]
                    if t_i >= 1:
                        s.wait_ge(dv_s.h, D0 + DVT0 * (t_i - 1) + 10)
                    s.activation(zt_sb[:], h_ap, AF.Square,
                                 accum_out=stat_sb[:, 9:10]).then_inc(sc_s.h, 1)
                    if t_i >= 1:
                        s.wait_ge(dv_s.h, D0 + DVT0 * (t_i - 1) + 17)
                    s.wait_ge(dv_s.h, D0 + DVT0 * t_i + 15)
                    s.activation(g8d_sb[:], g8a_sb[:],
                                 AF.Sigmoid).then_inc(sc_s.h, 1)
            S.raw("sc", f0_sc)

            def f0_pe(t):
                for t_i in range(8):
                    hsl_ = h_sb1 if t_i < 4 else h_sb2
                    h_ap = hsl_[:, (t_i % 4) * D:(t_i % 4) * D + D]
                    for fb in range(4):
                        if t_i >= 1 and fb == 0:
                            t.wait_ge(dv_s.h, D0 + DVT0 * (t_i - 1) + 43)
                        if fb >= 1:
                            t.wait_ge(dv_s.h, D0 + DVT0 * t_i + fb + 1)
                        for j in range(4):
                            i = t.transpose(psD[:, j * 128:(j + 1) * 128],
                                            h_ap[:, (fb * 4 + j) * 128:
                                                 (fb * 4 + j + 1) * 128],
                                            ident_f32[:])
                        i.then_inc(pe_s.h, 1)
                    t.wait_ge(dv_s.h, D0 + DVT0 * t_i + 5)
                    if t_i >= 1:
                        t.wait_ge(dv_s.h, D0 + DVT0 * (t_i - 1) + 6)
                    for fc in range(16):
                        i = t.matmul(psE[:, 0:136],
                                     hfm32_t[:, fc * 128:(fc + 1) * 128],
                                     rsm2_sb[:, fc * 136:(fc + 1) * 136],
                                     start=(fc == 0), stop=(fc == 15))
                    i.then_inc(pe_s.h, 1)
                    t.wait_ge(dv_s.h, D0 + DVT0 * t_i + 25)
                    t.transpose(pbA[0:64, 0:128], smlb[:, 0:64], ident_bf[:])
                    t.transpose(pbA[0:8, 128:256], mkb_sb[:, 0:8],
                                ident_bf[:]).then_inc(pe_s.h, 1)
                    t.wait_ge(dv_s.h, D0 + DVT0 * t_i + 27)
                    for ob in range(4):
                        ps = psA if ob % 2 == 0 else psB
                        if t_i + ob > 0:
                            if ob >= 2:
                                t.wait_ge(dv_s.h, D0 + DVT0 * t_i + 27 + 4 * (ob - 1))
                            elif t_i >= 1:
                                t.wait_ge(dv_s.h, D0 + DVT0 * (t_i - 1) + 27 + 4 * (ob + 3))
                        t.matmul(ps[:, 0:512], fmm_sb[0:64, :],
                                 dgnt_sb[0:64, ob * 512:(ob + 1) * 512],
                                 start=True, stop=False)
                        t.matmul(ps[:, 0:512], fmk_sb[0:8, :],
                                 biasw_sb[0:8, ob * 512:(ob + 1) * 512],
                                 start=False, stop=True).then_inc(pe_s.h, 1)
            S.raw("pe", f0_pe)

            def f0_dv(v):
                n = D0
                for t_i in range(8):
                    hsl_ = h_sb1 if t_i < 4 else h_sb2
                    h_ap = hsl_[:, (t_i % 4) * D:(t_i % 4) * D + D]

                    def g(i):
                        nonlocal n
                        n += 1
                        i.then_inc(dv_s.h, 1)
                        v.wait_ge(dv_s.h, n)
                    v.wait_ge(sy_s.h, Y0 + SYT * t_i + 16)
                    g(v.engine_nop())
                    for fb in range(4):
                        v.wait_ge(pe_s.h, P0 + PET * t_i + fb + 1)
                        g(v.tensor_copy(hfm32_t[:, fb * 512:(fb + 1) * 512],
                                        psD[:, 0:512]))
                    v.wait_ge(pe_s.h, P0 + PET * t_i + 5)
                    g(v.tensor_copy(sml[:, 0:136], psE[:, 0:136]))
                    v.wait_ge(sc_s.h, C0 + SCT * t_i + 1)
                    g(v.tensor_tensor(gsc_sb[:], sml[:, 0:64], c1b_sb[:],
                                      ALU.subtract))
                    g(v.tensor_tensor(gsc_sb[:], gsc_sb[:], gsc_sb[:],
                                      ALU.mult))
                    g(v.tensor_reduce(g8a_sb[:],
                                      gsc_sb[:].rearrange("p (a b) -> p a b", b=8),
                                      AX, ALU.add))
                    g(v.tensor_scalar(g8b_sb[:], sml[:, 64:72], -2.0,
                                      stat_sb[:, 9:10], ALU.mult, ALU.add))
                    g(v.tensor_tensor(g8b_sb[:], g8b_sb[:], mtmb_sb[:],
                                      ALU.add))
                    g(v.reciprocal(g8c_sb[:], g8b_sb[:]))
                    g(v.tensor_tensor(g8a_sb[:], g8a_sb[:], g8c_sb[:],
                                      ALU.mult))
                    g(v.tensor_tensor(g8a_sb[:], g8a_sb[:], impb[:, 0:8],
                                      ALU.subtract))
                    g(v.tensor_tensor(g8a_sb[:], g8a_sb[:], impb[:, 8:16],
                                      ALU.mult))
                    v.wait_ge(sc_s.h, C0 + SCT * t_i + 2)
                    g(v.reduce_max(stat_sb[:, 10:11], g8d_sb[:], AX))
                    g(v.tensor_single_scalar(g8a_sb[:], g8d_sb[:],
                                             stat_sb[:, 10:11], ALU.is_equal))
                    g(v.tensor_tensor(g8a_sb[:], g8a_sb[:], m8_sb[:],
                                      ALU.mult))
                    g(v.tensor_scalar(g8a_sb[:], g8a_sb[:], 8.0, None,
                                      ALU.add))
                    g(v.tensor_reduce(stat_sb[:, 11:12], g8a_sb[:], AX,
                                      ALU.min))
                    g(v.tensor_single_scalar(g8a_sb[:], iota8_sb[:],
                                             stat_sb[:, 11:12], ALU.is_equal))
                    g(v.tensor_tensor(gsc_sb[:], sml[:, 72:136], c3b_sb[:],
                                      ALU.subtract))
                    g(v.tensor_tensor(gsc_sb[:].rearrange("p (a b) -> p a b", b=8),
                                      gsc_sb[:].rearrange("p (a b) -> p a b", b=8),
                                      g8a_sb[:].to_broadcast([128, 8, 8]),
                                      ALU.mult))
                    if t_i >= 1:
                        v.wait_ge(pe_s.h, P0 + PET * (t_i - 1) + 6)
                    g(v.tensor_copy(smlb[:, 0:64], gsc_sb[:]))
                    g(v.tensor_copy(mkb_sb[:], g8a_sb[:]))
                    v.wait_ge(pe_s.h, P0 + PET * t_i + 6)
                    g(v.tensor_copy(fmm_sb[:], pbA[0:64, 0:128]))
                    g(v.tensor_copy(fmk_sb[:], pbA[0:8, 128:256]))
                    for ob in range(4):
                        ps = psA if ob % 2 == 0 else psB
                        v.wait_ge(pe_s.h, P0 + PET * t_i + 6 + ob + 1)
                        hsl2 = h_ap[:, ob * 512:(ob + 1) * 512]
                        g(v.tensor_tensor(s1_t[:], ps[:, 0:512], hsl2,
                                          ALU.subtract))
                        g(v.scalar_tensor_tensor(s1_t[:], s1_t[:],
                                                 stat_sb[:, 10:11], hsl2,
                                                 ALU.mult, ALU.add))
                        g(v.tensor_tensor(s1_t[:], s1_t[:],
                                          gb_t[:, ob * 512:(ob + 1) * 512],
                                          ALU.mult))
                        g(v.tensor_tensor(xt_sb[:, ob * 512:(ob + 1) * 512],
                                          xt_sb[:, ob * 512:(ob + 1) * 512],
                                          s1_t[:], ALU.add))
                    assert n == D0 + DVT0 * (t_i + 1), (n, t_i)
            S.raw("dv", f0_dv)
            pe_s.n = P0 + PET * 8
            sc_s.n = C0 + SCT * 8
            dv_s.n = D0 + DVT0 * 8
            sy_s.n = Y0 + SYT * 8

        def ffn1():
            for t_i in range(8):
                hsl_ = h_sb1 if t_i < 4 else h_sb2
                off = (t_i % 4) * D
                h_ap = hsl_[:, off:off + D]
                S.sc(lambda s, h_ap=h_ap: s.activation(
                    zt_sb[:], h_ap, AF.Square, accum_out=stat_sb[:, 1:2]))
                S.dv(lambda v: v.tensor_scalar(stat_sb[:, 2:3],
                                               stat_sb[:, 1:2], 1.0 / D, None,
                                               ALU.mult))
                S.sc(lambda s: s.activation(stat_sb[:, 3:4], stat_sb[:, 2:3],
                                            AF.Sqrt, bias=epsbt[:, 0:1]))
                S.dv(lambda v, t_i=t_i: v.reciprocal(
                    stat_sb[:, 8 + t_i:9 + t_i], stat_sb[:, 3:4]))
            for br, (gname, dstv) in enumerate((("g1", xq1_sb), ("g2", xq2_sb))):
                S.sy(lambda s, dma, gname=gname: dma(out=stg_sb[0:1, :],
                                                     in_=ins[gname]))
                bcast_to(g1b_sb, stg_sb, D)
                for t_i in range(8):
                    hsl_ = h_sb1 if t_i < 4 else h_sb2
                    off = (t_i % 4) * D
                    h_ap = hsl_[:, off:off + D]
                    S.sc(lambda s, h_ap=h_ap, t_i=t_i: s.activation(
                        zt_sb[:], h_ap, AF.Identity,
                        scale=stat_sb[:, 8 + t_i:9 + t_i]))

                    def qnt(v):
                        v.tensor_tensor(xt_sb[:], zt_sb[:], g1b_sb[:], ALU.mult)
                        v.tensor_reduce(stat_sb[:, 5:6], xt_sb[:], AX, ALU.max,
                                        apply_absolute_value=True)
                        v.tensor_single_scalar(stat_sb[:, 5:6],
                                               stat_sb[:, 5:6], 1e-4, ALU.max)
                        v.reciprocal(stat_sb[:, 6:7], stat_sb[:, 5:6])
                        v.tensor_scalar(stat_sb[:, 6:7], stat_sb[:, 6:7],
                                        127.0, None, ALU.mult)
                        v.tensor_scalar(stat_sb[:, 7:8], stat_sb[:, 5:6],
                                        1.0 / 127.0, None, ALU.mult)
                        v.tensor_scalar(xt_sb[:], xt_sb[:], stat_sb[:, 6:7],
                                        None, ALU.mult)
                        v.tensor_scalar(xt_sb[:], xt_sb[:], MAGIC, -MAGIC,
                                        ALU.add, ALU.add)
                        return v.tensor_scalar(xqst_sb[:], xt_sb[:],
                                               stat_sb[:, 7:8], None, ALU.mult)
                    S.dv(qnt)
                    for fb in range(4):
                        def trq(t, fb=fb):
                            for j in range(4):
                                i = t.transpose(
                                    pbA[:, j * 128:(j + 1) * 128],
                                    xqst_sb[:, (fb * 4 + j) * 128:
                                            (fb * 4 + j + 1) * 128],
                                    ident_bf[:])
                            return i
                        S.pe(trq)
                        S.dv(lambda v, fb=fb, dstv=dstv, t_i=t_i: v.tensor_copy(
                            dstv[:].rearrange("p (f k) -> p f k", f=16)
                            [:, fb * 4:(fb + 1) * 4, t_i * 128:(t_i + 1) * 128],
                            pbA[:].rearrange("p (a b) -> p a b", a=4)))
            pe_s, sc_s, dv_s, sy_s, gp_s = (S.sems[k] for k in
                                            ("pe", "sc", "dv", "sy", "gp"))
            P0, C0, D0, Y0, G0 = (pe_s.n, sc_s.n, dv_s.n, sy_s.n, gp_s.n)

            def w12_sy(s2, dma, P0=P0):
                s2.wait_ge(tern_sem.h, 1)
                for ob in range(16):
                    rk, oc = ob // 2, ob % 2
                    if ob >= 2:
                        s2.wait_ge(pe_s.h, P0 + 8 * (ob - 1))
                    dma(out=t1_buf[ob][:].rearrange("p (f c) -> p f c", f=16),
                        in_=tern_ag[rk * 6144:rk * 6144 + 2048,
                                    oc * 512:(oc + 1) * 512]
                        .rearrange("(f p) c -> p f c", p=128))
                    dma(out=t2_buf[ob][:].rearrange("p (f c) -> p f c", f=16),
                        in_=tern_ag[rk * 6144 + 2048:rk * 6144 + 4096,
                                    oc * 512:(oc + 1) * 512]
                        .rearrange("(f p) c -> p f c", p=128))
            S.raw("sy", w12_sy)

            def w12_pe(t, C0=C0, D0=D0, Y0=Y0):
                for ob in range(16):
                    t.wait_ge(sy_s.h, Y0 + 32 * (ob + 1))
                    for t_i in range(8):
                        s = ob * 8 + t_i
                        if s >= 1:
                            t.wait_ge(sc_s.h, C0 + s)
                            t.wait_ge(dv_s.h, D0 + s)
                        for fc in range(16):
                            t.matmul(psA[:, 0:512],
                                     xq1_sb[:, fc * 1024 + t_i * 128:
                                            fc * 1024 + (t_i + 1) * 128],
                                     t1_buf[ob][:, fc * 512:(fc + 1) * 512],
                                     start=(fc == 0), stop=(fc == 15))
                        for fc in range(16):
                            i = t.matmul(psB[:, 0:512],
                                         xq2_sb[:, fc * 1024 + t_i * 128:
                                                fc * 1024 + (t_i + 1) * 128],
                                         t2_buf[ob][:, fc * 512:(fc + 1) * 512],
                                         start=(fc == 0), stop=(fc == 15))
                        i.then_inc(pe_s.h, 1)
            S.raw("pe", w12_pe)

            def w12_sc(s, P0=P0, D0=D0):
                for ob in range(16):
                    for t_i in range(8):
                        s_i = ob * 8 + t_i
                        s.wait_ge(pe_s.h, P0 + s_i + 1)
                        if s_i >= 1:
                            s.wait_ge(dv_s.h, D0 + s_i)
                        s.activation(s1_t[:], psA[:, 0:512], AF.Silu,
                                     scale=a_bc[:, 3:4]).then_inc(sc_s.h, 1)
            S.raw("sc", w12_sc)

            def w12_dv(v, C0=C0, G0=G0):
                for ob in range(16):
                    for t_i in range(8):
                        s_i = ob * 8 + t_i
                        v.wait_ge(sc_s.h, C0 + s_i + 1)
                        if s_i >= 2:
                            v.wait_ge(gp_s.h, G0 + 16 * (s_i - 1))
                        v.scalar_tensor_tensor(
                            h2st[:, 0:512] if s_i % 2 == 0 else xqst_sb[:, 0:512],
                            s1_t[:], a_bc[:, 4:5], psB[:, 0:512],
                            ALU.mult, ALU.mult).then_inc(dv_s.h, 1)
            S.raw("dv", w12_dv)

            def w12_gp(g, dma, coll, D0=D0):
                for ob in range(16):
                    for t_i in range(8):
                        s_i = ob * 8 + t_i
                        g.wait_ge(dv_s.h, D0 + s_i + 1)
                        srcb = h2st[:, 0:512] if s_i % 2 == 0 else xqst_sb[:, 0:512]
                        dma(out=h2_dram[t_i * 128:(t_i + 1) * 128,
                                        ob * 512:(ob + 1) * 512], in_=srcb)
            S.raw("pl", w12_gp)
            pe_s.n = P0 + 128
            sc_s.n = C0 + 128
            dv_s.n = D0 + 128
            sy_s.n = Y0 + 32 * 16
            gp_s.n = G0 + 16 * 128
            for j in range(4):
                S.sy(lambda s, dma, j=j: dma(
                    out=stg_sb[0:1, :], in_=ins["g3"][0:1, j * D:(j + 1) * D]))
                bcast_to(g3b_sb[:, j * D:(j + 1) * D], stg_sb, D)
            P0, C0, D0, Y0 = pe_s.n, sc_s.n, dv_s.n, sy_s.n
            DVT = 11 + 16

            def _lpos(t):
                return t + 16 * max(t - 1, 0) + 1

            def _spos(t, fb):
                return min(t + 2, 8) + 16 * t + fb + 1

            def q3_sy(s2, dma):
                def store(t_i, fb):
                    s2.wait_ge(dv_s.h, D0 + DVT * t_i + 11 + fb + 1)
                    src_ = q3st_sb[:] if fb % 2 == 0 else q3st2_sb[:]
                    dma(out=h2qf_dram[fb * 512:(fb + 1) * 512,
                                      t_i * 128:(t_i + 1) * 128]
                        .rearrange("(a p) c -> p a c", p=128),
                        in_=src_.rearrange("p (a c) -> p a c", a=4))
                for t_i in range(8):
                    if t_i >= 1:
                        s2.wait_ge(sc_s.h, C0 + 4 * (t_i - 1) + 4)
                    dma(out=h2_t[:], in_=h2_dram[t_i * 128:(t_i + 1) * 128, :])
                    if t_i >= 1:
                        for fb in range(16):
                            store(t_i - 1, fb)
                for fb in range(16):
                    store(7, fb)
            S.raw("sy", q3_sy)

            def q3_sc(s):
                n = C0
                for t_i in range(8):
                    def g(i):
                        nonlocal n
                        n += 1
                        i.then_inc(sc_s.h, 1)
                        s.wait_ge(sc_s.h, n)
                    s.wait_ge(sy_s.h, Y0 + 16 * _lpos(t_i))
                    if t_i >= 1:
                        s.wait_ge(dv_s.h, D0 + DVT * t_i - 16)
                        s.wait_ge(pe_s.h, P0 + 16 * t_i)
                    g(s.activation(n3_t[:], h2_t[:], AF.Square,
                                   accum_out=stat_sb[:, 1:2]))
                    s.wait_ge(dv_s.h, D0 + DVT * t_i + 1)
                    g(s.activation(stat_sb[:, 3:4], stat_sb[:, 2:3], AF.Ln,
                                   bias=epsbt[:, 0:1]))
                    g(s.activation(stat_sb[:, 4:5], stat_sb[:, 3:4], AF.Exp,
                                   scale=-0.5))
                    g(s.activation(n3_t[:], h2_t[:], AF.Identity,
                                   scale=stat_sb[:, 4:5]))
            S.raw("sc", q3_sc)

            def q3_dv(v):
                n = D0
                for t_i in range(8):
                    def g(i):
                        nonlocal n
                        n += 1
                        i.then_inc(dv_s.h, 1)
                        v.wait_ge(dv_s.h, n)
                    v.wait_ge(sc_s.h, C0 + 4 * t_i + 1)
                    g(v.tensor_scalar(stat_sb[:, 2:3], stat_sb[:, 1:2],
                                      1.0 / DFF, None, ALU.mult))
                    v.wait_ge(sc_s.h, C0 + 4 * t_i + 4)
                    g(v.tensor_tensor(n3_t[:], n3_t[:], g3b_sb[:], ALU.mult))
                    g(v.tensor_reduce(stat_sb[:, 5:6], n3_t[:], AX, ALU.max,
                                      apply_absolute_value=True))
                    g(v.tensor_single_scalar(stat_sb[:, 5:6], stat_sb[:, 5:6],
                                             1e-4, ALU.max))
                    g(v.reciprocal(stat_sb[:, 6:7], stat_sb[:, 5:6]))
                    g(v.tensor_scalar(stat_sb[:, 6:7], stat_sb[:, 6:7], 127.0,
                                      None, ALU.mult))
                    g(v.tensor_scalar(stat_sb[:, 7:8], stat_sb[:, 5:6],
                                      1.0 / 127.0, None, ALU.mult))
                    g(v.tensor_scalar(n3_t[:], n3_t[:], stat_sb[:, 6:7], None,
                                      ALU.mult))
                    g(v.tensor_scalar(n3_t[:], n3_t[:], MAGIC, -MAGIC, ALU.add,
                                      ALU.add))
                    g(v.tensor_scalar(n3_t[:], n3_t[:], stat_sb[:, 7:8], None,
                                      ALU.mult))
                    g(v.engine_nop())
                    for fb in range(16):
                        v.wait_ge(pe_s.h, P0 + 16 * t_i + fb + 1)
                        if fb >= 2:
                            v.wait_ge(sy_s.h, Y0 + 16 * _spos(t_i, fb - 2))
                        elif t_i >= 1:
                            v.wait_ge(sy_s.h, Y0 + 16 * _spos(t_i - 1, 14 + fb))
                        dst_ = q3st_sb[:] if fb % 2 == 0 else q3st2_sb[:]
                        i = v.tensor_copy(dst_, pbA[:, 0:512])
                        i.then_inc(dv_s.h, 1)
                        n += 1
            S.raw("dv", q3_dv)

            def q3_pe(t):
                for t_i in range(8):
                    t.wait_ge(dv_s.h, D0 + DVT * t_i + 11)
                    for fb in range(16):
                        if fb >= 1:
                            t.wait_ge(dv_s.h, D0 + DVT * t_i + 11 + fb)
                        elif t_i >= 1:
                            t.wait_ge(dv_s.h, D0 + DVT * (t_i - 1) + 11 + 16)
                        for j in range(4):
                            i = t.transpose(pbA[:, j * 128:(j + 1) * 128],
                                            n3_t[:, (fb * 4 + j) * 128:
                                                 (fb * 4 + j + 1) * 128],
                                            ident_bf[:])
                        i.then_inc(pe_s.h, 1)
            S.raw("pe", q3_pe)
            pe_s.n = P0 + 128
            sc_s.n = C0 + 32
            dv_s.n = D0 + DVT * 8
            sy_s.n = Y0 + 16 * (8 + 128)

            load_mod_row(1, 5, transform_g=True)
            bcast_to(gb_t, stg_sb, D)
            for db in range(4):
                for tp0, tcnt in ((0, 6), (6, 2)):
                    P0, Y0 = pe_s.n, sy_s.n
                    nst = 4 * tcnt
                    t5 = tern_ag.rearrange(
                        "(rk blk j p two) c -> rk blk j p two c",
                        rk=8, blk=3, j=8, two=2)

                    def w3_sy(s2, dma, db=db, tp0=tp0, tcnt=tcnt, P0=P0,
                              nst=nst, t5=t5):
                        for ocg in range(4):
                            if ocg >= 2:
                                s2.wait_ge(pe_s.h, P0 + tcnt * (ocg - 1))
                            for rkr in range(2):
                                rk = ocg * 2 + rkr
                                sl = t5[rk, 2, :, :, db // 2,
                                        (db % 2) * 512:(db % 2) * 512 + 512]
                                dma(out=t3_buf[ocg]
                                    [:, rkr * 8 * 512:(rkr + 1) * 8 * 512]
                                    .rearrange("p (j c) -> p j c", j=8),
                                    in_=sl.rearrange("j p c -> p j c"))
                            for ti in range(tp0, tp0 + tcnt):
                                s_i = ocg * tcnt + (ti - tp0)
                                if s_i >= 2:
                                    s2.wait_ge(pe_s.h, P0 + s_i - 1)
                                dma(out=h2qf_buf[ti][:, 0:16 * 128]
                                    .rearrange("p (f c) -> p f c", f=16),
                                    in_=h2qf_dram[ocg * 2048:(ocg + 1) * 2048,
                                                  ti * 128:(ti + 1) * 128]
                                    .rearrange("(f p) c -> p f c", p=128))
                    S.raw("sy", w3_sy)

                    def w3_pe(t, db=db, tp0=tp0, tcnt=tcnt, P0=P0, Y0=Y0):
                        for ocg in range(4):
                            for ti in range(tp0, tp0 + tcnt):
                                s_i = ocg * tcnt + (ti - tp0)
                                t.wait_ge(sy_s.h,
                                          Y0 + 32 * (ocg + 1) + 16 * (s_i + 1))
                                for i2 in range(16):
                                    i = t.matmul(
                                        pstile[ti - tp0][:, 0:512],
                                        h2qf_buf[ti][:, i2 * 128:(i2 + 1) * 128],
                                        t3_buf[ocg][:, i2 * 512:(i2 + 1) * 512],
                                        start=(ocg == 0 and i2 == 0),
                                        stop=(ocg == 3 and i2 == 15))
                                i.then_inc(pe_s.h, 1)
                    S.raw("pe", w3_pe)
                    pe_s.n = P0 + nst
                    sy_s.n = Y0 + 32 * 4 + 16 * nst
                    E0, EY0 = dv_s.n, sy_s.n

                    def w3e_sy(s2, dma, db=db, tp0=tp0, tcnt=tcnt, E0=E0,
                               EY0=EY0):
                        for k2, ti in enumerate(range(tp0, tp0 + tcnt)):
                            if k2 >= 1:
                                s2.wait_ge(sy_s.h, EY0 + 32 * k2)
                                s2.wait_ge(dv_s.h, E0 + 2 * k2)
                            dma(out=xst_sb[:],
                                in_=x_cur[ti * 128:(ti + 1) * 128,
                                          db * 512:(db + 1) * 512])
                            s2.wait_ge(dv_s.h, E0 + 2 * (k2 + 1))
                            dma(out=x_cur[ti * 128:(ti + 1) * 128,
                                          db * 512:(db + 1) * 512],
                                in_=xst_sb[:])
                    S.raw("sy", w3e_sy)

                    def w3e_dv(v, db=db, tp0=tp0, tcnt=tcnt, E0=E0, EY0=EY0):
                        n = E0
                        for k2, ti in enumerate(range(tp0, tp0 + tcnt)):
                            def g(i):
                                nonlocal n
                                n += 1
                                i.then_inc(dv_s.h, 1)
                                v.wait_ge(dv_s.h, n)
                            v.wait_ge(sy_s.h, EY0 + 32 * k2 + 16)
                            g(v.scalar_tensor_tensor(
                                s1_t[:], pstile[ti - tp0][:, 0:512],
                                a_bc[:, 5:6], gb_t[:, db * 512:(db + 1) * 512],
                                ALU.mult, ALU.mult))
                            g(v.tensor_tensor(xst_sb[:], xst_sb[:], s1_t[:],
                                              ALU.add))
                    S.raw("dv", w3e_dv)
                    dv_s.n = E0 + 2 * tcnt
                    sy_s.n = EY0 + 32 * tcnt

        layer(0)
        layer(1)

        S.sy(lambda s, dma: dma(out=y_out, in_=x_cur))
        S.final_wait()

    return nc, tap_list


_NC_CACHE = {}


def kernel(**inputs):
    taps = tuple(sorted(globals().get("DEBUG_TAPS", ())))
    if taps not in _NC_CACHE:
        _NC_CACHE[taps] = build_nc(taps)
    nc, tap_list = _NC_CACHE[taps]
    cst = _consts()
    in_maps = []
    for c in range(8):
        m = dict(cst)
        m.update(_prep_core(inputs, c))
        in_maps.append(m)
    res = run_bass_kernel_spmd(nc, in_maps, core_ids=list(range(8)))
    out = np.zeros((B, T, D), np.float32)
    for c in range(8):
        b, r = c // 4, c % 4
        y = np.asarray(res.results[c]["y_out"])
        out[b, r * CH:(r + 1) * CH] = y[0:512]
        out[b, (7 - r) * CH:(8 - r) * CH] = y[512:1024]
    if taps:
        kernel._tap_results = [
            {k: np.asarray(v) for k, v in r.items()} for r in res.results]
    return out


DEBUG_TAPS = ()
